# version 12
# baseline (speedup 1.0000x reference)
from contextlib import ExitStack

import numpy as np
import concourse.bass as bass
import concourse.mybir as mybir
from concourse.bass_utils import run_bass_kernel_spmd

F32 = mybir.dt.float32
BF16 = mybir.dt.bfloat16
AF = mybir.ActivationFunctionType
ALU = mybir.AluOpType
AX = mybir.AxisListType

N_CORES = 8
B_CORE = 2
SEQ = 2048
D = 1024
NH = 16
DH = 64
BS = 64
NB = SEQ // BS
E = 4
HID = 4096
TOK = B_CORE * SEQ

ENGINES = ("sync", "gpsimd", "tensor", "scalar", "vector")

_DEBUG_TAPS = None
_PHASE = 99
_PSB_ZERO = True
_ING_TC = 16
_ING_SKIP = ()
_QT_PARTS = 9
_HL_LVL = 9
_NH_LIM = NH
_PV_LVL = 9


class Prog:
    def __init__(self, nc, es):
        self.nc = nc
        self.es = es
        self.blocks = []
        self.cnt = {}
        self.sems = {}
        self.cur = None

    def sem(self, name):
        self.sems[name] = self.es.enter_context(self.nc.semaphore(name))
        self.cnt[name] = 0

    def new_block(self):
        self.cur = {e: [] for e in ENGINES}
        self.blocks.append(self.cur)

    DMA_SEMS = ("s_ld", "s_gld", "s_st", "s_gp", "s_lda", "s_cast")

    def emit(self, eng, fn, waits=(), inc=None, dma=False):
        w = []
        for x in waits:
            if x is None:
                continue
            if isinstance(x, list):
                w.extend(t for t in x if t is not None)
            else:
                w.append(x)
        w = [(t[0], t[1]) if len(t) == 3 else
             (t[0], self.cnt[t[0]] if t[0] in self.DMA_SEMS else t[1])
             for t in w]
        tok = None
        incs = ()
        if inc is not None:
            k = 16 if dma else 1
            self.cnt[inc] += k
            tok = (inc, self.cnt[inc])
            incs = ((inc, k),)
        self.cur[eng].append((fn, tuple(w), incs))
        return tok

    def replay(self):
        for blk_ops in self.blocks:
            with self.nc.Block() as blk:
                for eng in ENGINES:
                    ops = blk_ops[eng]
                    if not ops:
                        continue
                    def make(ops=ops):
                        def body(engobj):
                            seen = {}
                            for fn, waits, incs in ops:
                                for s, v in waits:
                                    if seen.get(s, -1) >= v:
                                        continue
                                    engobj.wait_ge(self.sems[s], v)
                                    seen[s] = v
                                instr = fn(engobj)
                                for s, k in incs:
                                    instr.then_inc(self.sems[s], k)
                        return body
                    getattr(blk, eng)(make())


class Res:
    __slots__ = ("ready", "free")

    def __init__(self):
        self.ready = None
        self.free = None


def wl(*items):
    out = []
    for it in items:
        if it is None:
            continue
        if isinstance(it, list):
            out.extend(x for x in it if x is not None)
        else:
            out.append(it)
    return out


def build_program(g_gather, g_keep):
    NKEEP = max(1, len(g_keep))
    nc = bass.Bass("TRN2", target_bir_lowering=False)

    x = nc.dram_tensor("x", [TOK, D], F32, kind="ExternalInput").ap()
    wq_h = nc.dram_tensor("wq", [D, D], F32, kind="ExternalInput").ap()
    wk_h = nc.dram_tensor("wk", [D, D], F32, kind="ExternalInput").ap()
    wv_h = nc.dram_tensor("wv", [D, D], F32, kind="ExternalInput").ap()
    wo_h = nc.dram_tensor("wo", [D, D], F32, kind="ExternalInput").ap()
    bq_h = nc.dram_tensor("bq", [D], F32, kind="ExternalInput").ap()
    bk_h = nc.dram_tensor("bk", [D], F32, kind="ExternalInput").ap()
    bv_h = nc.dram_tensor("bv", [D], F32, kind="ExternalInput").ap()
    bo_h = nc.dram_tensor("bo", [D], F32, kind="ExternalInput").ap()
    w1_h = nc.dram_tensor("w1", [E, D, HID], F32, kind="ExternalInput").ap()
    b1_h = nc.dram_tensor("b1", [E, HID], F32, kind="ExternalInput").ap()
    w2_h = nc.dram_tensor("w2", [E, HID, D], F32, kind="ExternalInput").ap()
    b2_h = nc.dram_tensor("b2", [E, D], F32, kind="ExternalInput").ap()
    wr_h = nc.dram_tensor("wr", [D, E], F32, kind="ExternalInput").ap()
    br_h = nc.dram_tensor("br", [E], F32, kind="ExternalInput").ap()
    y = nc.dram_tensor("y", [TOK, D], F32, kind="ExternalOutput").ap()

    sx32 = nc.dram_tensor("sx32", [B_CORE, 8, 128, SEQ], F32, kind="Internal").ap()
    sx1bf = nc.dram_tensor("sx1bf", [B_CORE, 8, 128, SEQ], BF16, kind="Internal").ap()
    sacc = nc.dram_tensor("sacc", [8, 128, TOK], F32, kind="Internal").ap()
    swk = nc.dram_tensor("swk", [128, 8 * D], BF16, kind="Internal").ap()
    swv = nc.dram_tensor("swv", [128, 8 * D], BF16, kind="Internal").ap()
    swq = nc.dram_tensor("swq", [128, 8 * D], BF16, kind="Internal").ap()
    swo = nc.dram_tensor("swo", [128, 8 * D], BF16, kind="Internal").ap()
    m_scr = nc.dram_tensor("m_scr", [4, TOK], F32, kind="Internal").ap()
    lg_scr = nc.dram_tensor("lg_scr", [4, TOK], F32, kind="Internal").ap()
    sw1bf = nc.dram_tensor("sw1bf", [E, D, HID], BF16, kind="Internal").ap()
    sxbf = nc.dram_tensor("sxbf", [B_CORE, 8, 128, SEQ], BF16, kind="Internal").ap()
    sw2bf = nc.dram_tensor("sw2bf", [E, HID, D], BF16, kind="Internal").ap()

    taps = {}
    if _DEBUG_TAPS is not None:
        for name, shape, dt in _DEBUG_TAPS:
            taps[name] = nc.dram_tensor("tap_" + name, shape, dt,
                                        kind="ExternalOutput").ap()

    es = ExitStack()
    P = Prog(nc, es)
    for s in ("s_ld", "s_gld", "s_st", "s_pe", "s_act", "s_dve", "s_gp",
              "s_gpc", "s_lda", "s_cast"):
        P.sem(s)

    def sb(es_, name, shape, dtype):
        return es_.enter_context(nc.sbuf_tensor(name, shape, dtype))

    def psum(name, shape, dtype):
        return es.enter_context(nc.psum_tensor(name, shape, dtype))

    ident_bf = sb(es, "ident_bf", [128, 128], BF16)
    ident32 = sb(es, "ident32", [128, 128], F32)
    ones_row_bf = sb(es, "ones_row_bf", [1, 128], BF16)
    ones_row_32 = sb(es, "ones_row_32", [1, 128], F32)
    ones_col_bf = sb(es, "ones_col_bf", [128, 1], BF16)
    b1c = [sb(es, f"b1c{e}", [128, 32], F32) for e in range(E)]
    csum_c = sb(es, "csum_c", [128, 8], F32)
    ccols = sb(es, "ccols", [128, 8], F32)
    strow = [sb(es, f"strow{i}", [1, 512], F32) for i in range(3)]
    rbf = [sb(es, f"rbf{i}", [128, 512], BF16) for i in range(2)]
    sqbf = [sb(es, f"sqbf{i}", [128, 512], BF16) for i in range(2)]
    rrbf = [Res(), Res()]
    rsq = [Res(), Res()]
    big_scr = sb(es, "big_scr", [128, 4096], F32)
    big_bf = big_scr.bitcast(BF16)
    hTc = sb(es, "hTc", [128, 32], BF16)
    zeros64 = sb(es, "zeros64", [64, 64], BF16)

    es_attn = ExitStack()
    bqc = sb(es_attn, "bqc", [128, 8], F32)
    bkc = sb(es_attn, "bkc", [128, 8], F32)
    boc = sb(es_attn, "boc", [128, 8], F32)
    bv_row = sb(es_attn, "bv_row", [1, D], BF16)
    wr_sb = sb(es_attn, "wr_sb", [128, 32], F32)
    br_c = sb(es_attn, "br_c", [4, 1], F32)
    ztmp = sb(es_attn, "ztmp", [128, 24], F32)
    lg_sb = [sb(es_attn, f"lg_sb{i}", [4, 512], F32) for i in range(2)]
    rlg = [Res(), Res()]
    brow = sb(es_attn, "brow", [32, 128], F32)
    rbrow = Res()

    psA = [psum("psA0", [128, 512], F32), psum("psA1", [128, 512], F32)]
    psB = [psum("psB0", [128, 1024], BF16), psum("psB1", [128, 1024], BF16)]
    psB32 = [t.bitcast(F32) for t in psB]
    psC = [psum("psC0", [128, 512], F32), psum("psC1", [128, 512], F32)]
    psD = [psum("psD0", [128, 512], F32), psum("psD1", [128, 512], F32)]
    rA = [Res(), Res()]
    rB = [Res(), Res()]
    rC = [Res(), Res()]
    rD = [Res(), Res()]

    def dma(eng, out_ap, in_ap, waits=(), slow=False, sem=None):
        if sem is None:
            sem = {"gpsimd": "s_gld", "scalar": "s_lda"}.get(eng, "s_ld")
        kw = {"allow_slow_non_contiguous": True} if slow else {}
        w = wl(*waits) if sem == "s_cast" else wl(*waits, (sem, P.cnt[sem]))
        return P.emit(eng, lambda E_, o=out_ap, i=in_ap, kw=kw:
                      E_.dma_start(o, i, **kw),
                      waits=w, inc=sem, dma=True)

    def dma_store(out_ap, in_ap, waits=(), eng="sync", accum=False):
        kw = {"accum_op": ALU.add} if accum else {}
        sem = "s_gp" if eng == "gpsimd" else "s_st"
        return P.emit(eng, lambda E_, o=out_ap, i=in_ap, kw=kw:
                      E_.dma_start(o, i, **kw),
                      waits=wl(*waits, (sem, P.cnt[sem])), inc=sem, dma=True)

    def mm(out_ap, lhsT, rhs, start, stop, waits=(), inc=None, transpose=False):
        def fn(E_, o=out_ap, l=lhsT, r=rhs, st=start, sp=stop, tr=transpose):
            if tr:
                return E_.matmul(o, l, r, start=True, stop=True,
                                 is_transpose=True)
            return E_.matmul(o, l, r, start=st, stop=sp)
        return P.emit("tensor", fn, waits=waits, inc=inc)

    def vop(fn, waits=()):
        return P.emit("vector", fn, waits=waits, inc="s_dve")

    def sop(fn, waits=()):
        return P.emit("scalar", fn, waits=waits, inc="s_act")

    def gop(fn, waits=(), inc="s_gpc"):
        return P.emit("gpsimd", fn, waits=waits, inc=inc)

    def tap(name, src_ap, waits):
        if name in taps:
            dma_store(taps[name], src_ap, waits=waits)

    P.new_block()

    t0 = gop(lambda g: g.memset(ident_bf[:, :], 0.0))
    t0 = gop(lambda g: g.affine_select(out=ident_bf[:, :], in_=ident_bf[:, :],
                                       compare_op=ALU.not_equal, fill=1.0, base=0,
                                       pattern=[[-1, 128]], channel_multiplier=1),
             waits=[t0])
    t1_ = gop(lambda g: g.memset(ident32[:, :], 0.0), waits=[t0])
    t1_ = gop(lambda g: g.affine_select(out=ident32[:, :], in_=ident32[:, :],
                                        compare_op=ALU.not_equal, fill=1.0, base=0,
                                        pattern=[[-1, 128]], channel_multiplier=1),
              waits=[t1_])
    t2_ = gop(lambda g: g.memset(ones_row_bf[:, :], 1.0), waits=[t1_])
    t2_ = gop(lambda g: g.memset(ones_row_32[:, :], 1.0), waits=[t2_])
    t2_ = gop(lambda g: g.memset(zeros64[:, :], 0.0), waits=[t2_])
    t_const = gop(lambda g: g.memset(ones_col_bf[:, :], 1.0), waits=[t2_])

    def load_cols(dst, src_row_ap, nrows, dst_free=None):
        t_ld = dma("sync", brow[0:nrows, :src_row_ap.shape[1]], src_row_ap,
                   waits=wl(rbrow.free))
        t_tr = mm(psD[0][0:src_row_ap.shape[1], 0:nrows],
                  brow[0:nrows, :src_row_ap.shape[1]],
                  ident32[0:nrows, 0:nrows], True, True,
                  waits=wl(t_ld, rD[0].free, t_const), transpose=True, inc="s_pe")
        rbrow.free = t_tr
        t_ev = vop(lambda v, d=dst, n=nrows, m=src_row_ap.shape[1]:
                   v.tensor_copy(d[0:m, 0:n], psD[0][0:m, 0:n]),
                   waits=wl(t_tr, dst_free))
        rD[0].free = t_ev
        return t_ev

    t_bq = load_cols(bqc, bq_h.rearrange("(c p) -> c p", p=128), 8)
    t_bk = load_cols(bkc, bk_h.rearrange("(c p) -> c p", p=128), 8)
    t_bo = load_cols(boc, bo_h.rearrange("(c p) -> c p", p=128), 8)
    t_bv = dma("gpsimd", bv_row[:, :], bv_h.rearrange("(o d) -> o d", o=1))
    t_b1 = [load_cols(b1c[e], b1_h[e, :].rearrange("(c p) -> c p", p=128), 32)
            for e in range(E)]
    b2tmp = sb(es_attn, "b2tmp", [128, 8], F32)
    t_cs = load_cols(csum_c, b2_h[0, :].rearrange("(c p) -> c p", p=128), 8)
    for e in range(1, E):
        t_e = load_cols(b2tmp, b2_h[e, :].rearrange("(c p) -> c p", p=128), 8,
                        dst_free=t_cs)
        t_cs = vop(lambda v: v.tensor_add(csum_c[:, :], csum_c[:, :], b2tmp[:, :]),
                   waits=[t_e, t_cs])
    t_csum_b2 = t_cs
    t_wr = None
    for dc in range(8):
        t_wr = dma("sync", wr_sb[:, 4 * dc:4 * dc + 4],
                   wr_h[128 * dc:128 * dc + 128, :])
    t_br = load_cols(br_c, br_h.rearrange("(o e) -> o e", o=1), 1)

    wkv_sb = sb(es_attn, "wkv", [128, 8 * D], BF16)
    rwkv = Res()
    xn = sb(es_attn, "xn", [128, D], F32)
    rxn = Res()
    wnat = xn.bitcast(BF16)[:, 0:D]
    rwnat = rxn

    def stage_weight(w_hbm, dst_sb, scratch, dst_free=None):
        last = None
        for r in range(8):
            buf = wnat
            t_ld = dma("gpsimd", buf[:, :], w_hbm[128 * r:128 * r + 128, :],
                       waits=wl(rwnat.free, t_const))
            for g in range(2):
                toks = []
                for j in range(4):
                    dc = 4 * g + j
                    toks.append(mm(psB[g][:, 128 * j:128 * j + 128],
                                   buf[:, 128 * dc:128 * dc + 128], ident_bf[:, :],
                                   j == 0, j == 3,
                                   waits=wl(t_ld, rB[g].free) if j == 0 else (),
                                   transpose=True, inc="s_pe" if j == 3 else None))
                def ev(v, g=g, r=r, dst=dst_sb):
                    dst_ap = dst[:, :].rearrange("p (dc c) -> p dc c", c=D)[
                        :, 4 * g:4 * g + 4, 128 * r:128 * r + 128]
                    return v.tensor_copy(dst_ap, psB[g][:, 0:512].rearrange(
                        "p (j c) -> p j c", c=128))
                t_ev = vop(ev, waits=wl(toks[-1],
                                        dst_free if (r == 0 and g == 0) else None))
                rB[g].free = t_ev
                last = t_ev
            rwnat.free = last
        if scratch is not None:
            return dma_store(scratch, dst_sb[:, :], waits=[last])
        return last

    t_sq = stage_weight(wq_h, wkv_sb, swq)
    t_so = stage_weight(wo_h, wkv_sb, swo, dst_free=t_sq)
    rwkv.ready = stage_weight(wk_h, wkv_sb, swk, dst_free=t_so)

    cur = {}
    rqx = Res()
    kT = sb(es_attn, "kT", [128, 8 * SEQ], BF16)
    v_all = sb(es_attn, "v_all", [128, 16 * D], BF16)
    kTg = sb(es_attn, "kTg", [128, 8 * 128], BF16)
    vg = sb(es_attn, "vg", [128, D], BF16)
    qTg = sb(es_attn, "qTg", [128, 512 * NKEEP], BF16)
    qT = sb(es_attn, "qT", [128, 8 * 512], BF16)
    ctxT = sb(es_attn, "ctxT", [128, 8 * 512], BF16)
    NPB = 4
    P_l = [sb(es_attn, f"P_l{i}", [128, 192], BF16) for i in range(NPB)]
    P_c = [sb(es_attn, f"P_c{i}", [128, 128], BF16) for i in range(NPB)]
    PT_sb = [sb(es_attn, f"PT{i}", [128, 384], BF16) for i in range(NPB)]
    rPl = [Res() for _ in range(NPB)]
    rPT = [Res() for _ in range(NPB)]
    xr_sb = [sb(es_attn, f"xr{i}", [128, 512], F32) for i in range(2)]
    rxr = [Res(), Res()]
    x1bf_out = sb(es_attn, "x1o", [128, 4096], BF16)
    rx1o = Res()
    xe32 = [sb(es_attn, "xe32_0", [128, 512], F32)]
    rxe = [Res()]
    ctmp = sb(es_attn, "ctmp", [128, 512], F32)

    rqT = Res()
    r32 = [big_scr[:, 512 * dc:512 * dc + 512] for dc in range(8)]
    P_row = big_bf[:, 0:2048]
    PT_row = big_bf[:, 2048:4096]
    rbig = Res()

    scale = 1.0 / 8.0
    SEQB = SEQ

    def ingest(b):
        for tc in range(_ING_TC):
            t_ld = dma("sync", xn[:, :],
                       x[SEQ * b + 128 * tc: SEQ * b + 128 * tc + 128, :],
                       waits=wl(rxn.free))
            for g in range(2):
                toks = []
                for j in range(4):
                    dc = 4 * g + j
                    toks.append(mm(psB32[g][:, 128 * j:128 * j + 128],
                                   xn[:, 128 * dc:128 * dc + 128],
                                   ident32[:, :], j == 0, j == 3,
                                   waits=wl(t_ld, rB[g].free) if j == 0 else (),
                                   transpose=True, inc="s_pe" if j == 3 else None))
                if "vec" in _ING_SKIP:
                    t_v = toks[-1]
                else:
                    t_v = vop(lambda v, g=g: v.tensor_copy(xe32[0][:, :],
                                                           psB32[g][:, 0:512]),
                              waits=wl(toks[-1], rxe[0].free))
                def v_ev(v, g=g, tc=tc, xTbf=cur["xTbf"]):
                    dst = xTbf[:, :].rearrange("p (dc t) -> p dc t", t=SEQB)[
                        :, 4 * g:4 * g + 4, 128 * tc:128 * tc + 128]
                    return v.tensor_copy(dst, psB32[g][:, 0:512].rearrange(
                        "p (j t) -> p j t", t=128))
                t_s = vop(v_ev, waits=[toks[-1]])
                rB[g].free = [t_v, t_s]
                if "store" in _ING_SKIP:
                    rxe[0].free = t_v
                else:
                    t_st = dma_store(
                        sx32[b, 4 * g:4 * g + 4, :, 128 * tc:128 * tc + 128]
                        .transpose([1, 0, 2]),
                        xe32[0][:, 0:512].rearrange("p (j t) -> p j t", t=128),
                        waits=[t_v])
                    rxe[0].free = t_st
            rxn.free = ("s_pe", P.cnt["s_pe"])
        dma_store(sxbf[b].transpose([1, 0, 2]),
                  cur["xTbf"][:, :].rearrange("p (dc t) -> p dc t", t=SEQB),
                  waits=[("s_dve", P.cnt["s_dve"])])
        return ("s_st", P.cnt["s_st"])

    def kv_proj(b):
        if b > 0:
            rwkv.ready = dma("gpsimd", wkv_sb[:, :], swk, waits=wl(rwkv.free))
        t_xready = [("s_act", P.cnt["s_act"]), ("s_dve", P.cnt["s_dve"])]
        for doutc in range(8):
            for tt in range(4):
                i = (doutc * 4 + tt) % 2
                tok = None
                for dc in range(8):
                    tok = mm(psA[i][:, :],
                             wkv_sb[:, D * dc + 128 * doutc: D * dc + 128 * doutc + 128],
                             cur["xTbf"][:, SEQB * dc + 512 * tt: SEQB * dc + 512 * tt + 512],
                             dc == 0, dc == 7,
                             waits=wl(rA[i].free, rwkv.ready, t_bk, t_xready)
                             if dc == 0 else (),
                             inc="s_pe" if dc == 7 else None)
                t_ev = sop(lambda s, doutc=doutc, tt=tt, i=i: s.activation(
                    kT[:, SEQB * doutc + 512 * tt: SEQB * doutc + 512 * tt + 512],
                    psA[i][:, :], AF.Identity, bias=bkc[:, doutc:doutc + 1], scale=1.0),
                    waits=[tok])
                rA[i].free = t_ev
        rwkv.free = ("s_pe", P.cnt["s_pe"])
        if b == 0:
            rwkv.ready = stage_weight(wv_h, wkv_sb, swv)
        else:
            rwkv.ready = dma("gpsimd", wkv_sb[:, :], swv, waits=wl(rwkv.free))
        for tc in range(16):
            for half in range(2):
                i = (tc * 2 + half) % 2
                tok = None
                for dc in range(8):
                    tok = mm(psC[i][:, :],
                             cur["xTbf"][:, SEQB * dc + 128 * tc: SEQB * dc + 128 * tc + 128],
                             wkv_sb[:, D * dc + 512 * half: D * dc + 512 * half + 512],
                             dc == 0, False,
                             waits=wl(rC[i].free, rwkv.ready, t_bv, t_xready)
                             if dc == 0 else ())
                tok = mm(psC[i][:, :], ones_row_bf[:, :],
                         bv_row[:, 512 * half: 512 * half + 512],
                         False, True, inc="s_pe")
                t_ev = sop(lambda s, tc=tc, half=half, i=i: s.activation(
                    v_all[:, D * tc + 512 * half: D * tc + 512 * half + 512],
                    psC[i][:, :], AF.Copy, bias=0.0, scale=1.0), waits=[tok])
                rC[i].free = t_ev
        rwkv.free = ("s_pe", P.cnt["s_pe"])
        t_kv = ("s_act", P.cnt["s_act"])
        for j, g in enumerate(g_gather):
            for dd in range(8):
                vop(lambda v, j=j, g=g, dd=dd: v.tensor_copy(
                    kTg[:, 128 * dd + 64 * j: 128 * dd + 64 * j + 64],
                    kT[:, SEQB * dd + 64 * g: SEQB * dd + 64 * g + 64]),
                    waits=[t_kv])
            vop(lambda v, j=j, g=g: v.tensor_copy(
                vg[64 * j:64 * j + 64, :],
                v_all[64 * (g % 2):64 * (g % 2) + 64,
                      D * (g // 2):D * (g // 2) + D]), waits=[t_kv])
        return ("s_dve", P.cnt["s_dve"])

    def layer_norm(rt_ready, rtiles, TW=512):
        tok0 = tok1 = None
        for dc in range(8):
            i = dc % 2
            t_c = sop(lambda s, dc=dc, i=i: s.activation(
                rbf[i][:, 0:TW], rtiles[dc], AF.Copy, bias=0.0, scale=1.0),
                waits=wl(rt_ready[dc], rrbf[i].free))
            t_q = sop(lambda s, dc=dc, i=i: s.activation(
                sqbf[i][:, 0:TW], rtiles[dc], AF.Square, bias=0.0, scale=1.0),
                waits=wl(rsq[i].free))
            tok0 = mm(psD[0][0:1, 0:TW], ones_col_bf[:, :], rbf[i][:, 0:TW],
                      dc == 0, dc == 7,
                      waits=wl(t_c, rD[0].free) if dc == 0 else [t_c],
                      inc="s_pe")
            rrbf[i].free = wl(tok0, t_c)
            tok1 = mm(psD[1][0:1, 0:TW], ones_col_bf[:, :], sqbf[i][:, 0:TW],
                      dc == 0, dc == 7,
                      waits=wl(t_q, rD[1].free) if dc == 0 else [t_q],
                      inc="s_pe")
            rsq[i].free = wl(tok1, t_q)
        t1 = vop(lambda v: v.tensor_scalar(strow[0][:, 0:TW], psD[0][0:1, 0:TW],
                                           1.0 / D, None, ALU.mult), waits=[tok0])
        t2 = vop(lambda v: v.tensor_scalar(strow[1][:, 0:TW], psD[1][0:1, 0:TW],
                                           1.0 / D, 1e-5, ALU.mult, ALU.add),
                 waits=[tok1])
        t3 = vop(lambda v: v.tensor_mul(strow[2][:, 0:TW], strow[0][:, 0:TW],
                                        strow[0][:, 0:TW]), waits=[t1])
        t4 = vop(lambda v: v.tensor_sub(strow[1][:, 0:TW], strow[1][:, 0:TW],
                                        strow[2][:, 0:TW]), waits=[t2, t3])
        t5 = sop(lambda s: s.activation(strow[2][:, 0:TW], strow[1][:, 0:TW], AF.Ln,
                                        bias=0.0, scale=1.0), waits=[t4])
        t6 = sop(lambda s: s.activation(strow[2][:, 0:TW], strow[2][:, 0:TW], AF.Exp,
                                        bias=0.0, scale=-0.5), waits=[t5])
        t7 = vop(lambda v: v.tensor_mul(strow[0][:, 0:TW], strow[0][:, 0:TW],
                                        strow[2][:, 0:TW]), waits=[t6])
        tb0 = mm(psD[0][:, 0:TW], ones_row_32[:, :], strow[2][:, 0:TW], True, True,
                 waits=[t6, t1], inc="s_pe")
        tb1 = mm(psD[1][:, 0:TW], ones_row_32[:, :], strow[0][:, 0:TW], True, True,
                 waits=[t7, t2], inc="s_pe")
        done = []
        for dc in range(8):
            ta = vop(lambda v, dc=dc: v.tensor_mul(rtiles[dc], rtiles[dc],
                                                   psD[0][:, 0:TW]), waits=[tb0])
            tb = vop(lambda v, dc=dc: v.tensor_sub(rtiles[dc], rtiles[dc],
                                                   psD[1][:, 0:TW]), waits=[tb1, ta])
            done.append(tb)
        rD[0].free = done[-1]
        rD[1].free = done[-1]
        return done

    def attention_qtile(b, qt, t_ing_st):
        t_prev_st = ("s_st", P.cnt["s_st"])
        t_prev_gp = ("s_gp", P.cnt["s_gp"])
        t_prev_v = rqT.free
        qx = cur["qx"]
        t_qx = cur.pop("t_qx_next", None)
        if t_qx is None:
            t_qx = dma("scalar",
                       qx[:, :].rearrange("p (dc t) -> p dc t", t=512),
                       sxbf[b, :, :, 512 * qt:512 * qt + 512]
                       .transpose([1, 0, 2]),
                       waits=wl(rqx.free, t_ing_st))
        rqx.free = None
        for doutc in range(8):
            i = doutc % 2
            tok = None
            for dc in range(8):
                tok = mm(psA[i][:, :],
                         cur["wqt"][:, D * dc + 128 * doutc: D * dc + 128 * doutc + 128],
                         qx[:, 512 * dc:512 * dc + 512],
                         dc == 0, dc == 7,
                         waits=wl(rA[i].free, t_bq, cur["t_wq"], t_prev_v, t_qx)
                         if dc == 0 else (),
                         inc="s_pe" if dc == 7 else None)
            t_ev = sop(lambda s, doutc=doutc, i=i: s.activation(
                qT[:, 512 * doutc:512 * doutc + 512], psA[i][:, :], AF.Identity,
                bias=bqc[:, doutc:doutc + 1], scale=1.0), waits=[tok])
            rA[i].free = t_ev
        rqx.free = ("s_pe", P.cnt["s_pe"])
        cast_piece(2)
        if qt < 3:
            cur["t_qx_next"] = dma(
                "scalar", qx[:, :].rearrange("p (dc t) -> p dc t", t=512),
                sxbf[b, :, :, 512 * (qt + 1):512 * (qt + 1) + 512]
                .transpose([1, 0, 2]), waits=wl(rqx.free, t_ing_st))
            rqx.free = None
        t_q_done = ("s_act", P.cnt["s_act"])
        if _QT_PARTS < 2:
            return ("s_act", P.cnt["s_act"])

        keep_here = [g for g in g_keep if g // 8 == qt]
        for g in keep_here:
            j = g_keep.index(g)
            for dd in range(8):
                vop(lambda v, j=j, g=g, dd=dd: v.tensor_copy(
                    qTg[:, 512 * j + 64 * dd: 512 * j + 64 * dd + 64],
                    qT[:, 512 * dd + 64 * (g % 8): 512 * dd + 64 * (g % 8) + 64]),
                    waits=[t_q_done])
        t_qg = ("s_dve", P.cnt["s_dve"])
        rqT.free = t_qg
        if qt == 0 and _PSB_ZERO:
            for g2 in range(2):
                mm(psB[g2][64:128, 256:320], zeros64[:, :],
                   ident_bf[0:64, 0:64], True, True,
                   waits=wl(rB[g2].free, t_const), transpose=True)
                t_z0 = mm(psB[g2][0:64, 320:384], zeros64[:, :],
                          ident_bf[0:64, 0:64], True, True, transpose=True,
                          inc="s_pe")
                rB[g2].free = t_z0

        SBk = [psA[0], psA[1]]
        rS = [rA[0], rA[1]]
        t_ev_prev_guard = None
        pend = None

        def emit_tail(st):
            ia, ib, h, qc, qc4, pb, t_nl, t_nc = st
            pt2 = it_pt = ia % 2
            w0 = wl(rB[pt2].free, t_nl)
            mm(psB[pt2][:, 0:128], P_l[ib][:, 64:192], ident_bf[:, :],
               True, False, waits=w0, transpose=True)
            mm(psB[pt2][0:64, 256:320], P_l[ib][64:128, 0:64],
               ident_bf[64:128, 64:128], False, False, transpose=True)
            mm(psB[pt2][64:128, 320:384], P_l[ib][0:64, 0:64],
               ident_bf[0:64, 0:64], False, False, transpose=True)
            t_tr = mm(psB[pt2][:, 128:256], P_c[ib][:, :], ident_bf[:, :],
                      False, True, waits=[t_nc], transpose=True, inc="s_pe")
            rPl[ib].free = t_tr
            t_pte = vop(lambda v, pt2=pt2, ib=ib: v.tensor_copy(
                PT_sb[ib][:, :], psB[pt2][:, 0:384]),
                waits=wl(t_tr, rPT[ib].free))
            rB[pt2].free = t_pte
            ocol = 128 * qc4
            blo = (2 * qc - 1) % NB
            bhi = (2 * qc + 2) % NB

            def vsl(bk, hx=h):
                return v_all[64 * (bk % 2):64 * (bk % 2) + 64,
                             D * (bk // 2) + 64 * hx: D * (bk // 2) + 64 * hx + 64]
            firstX = (qc4 == 0)
            wX = wl(t_pte, rC[0].free if (firstX and h % 2 == 0) else None)
            wY = wl(t_pte, rC[1].free if (firstX and h % 2 == 0) else None)
            mm(psC[0][pb:pb + 64, ocol:ocol + 128], vsl(2 * qc),
               PT_sb[ib][0:64, 0:128], firstX, False, waits=wX)
            mm(psC[0][pb:pb + 64, ocol + 64:ocol + 128], vsl(bhi),
               PT_sb[ib][0:64, 256:320], False, False)
            mm(psC[0][pb:pb + 64, ocol:ocol + 128],
               vg[:, 64 * h:64 * h + 64], PT_sb[ib][:, 128:256],
               False, (qc4 == 3), inc="s_pe")
            t_pv = mm(psC[1][pb:pb + 64, ocol:ocol + 128], vsl(2 * qc + 1),
                      PT_sb[ib][64:128, 0:128], firstX, False, waits=wY)
            t_pv = mm(psC[1][pb:pb + 64, ocol:ocol + 64], vsl(blo),
                      PT_sb[ib][64:128, 320:384], False, (qc4 == 3),
                      inc="s_pe")
            rPT[ib].free = t_pv

        for h in range(_NH_LIM):
            hh = h // 2
            pb = 64 * (h % 2)
            for qc4 in range(4):
                qc = 4 * qt + qc4
                it = h * 4 + qc4
                ia = it % 2
                ib = it % 4
                Sb = SBk[ia]
                qap = qT[pb:pb + 64,
                         512 * hh + 128 * qc4: 512 * hh + 128 * qc4 + 128]
                w0 = wl(rS[ia].free, t_q_done, t_qg)
                mm(Sb[:, 64:192], qap,
                   kT[pb:pb + 64, SEQB * hh + 128 * qc: SEQB * hh + 128 * qc + 128],
                   True, False, waits=w0)
                lo = (64 * (2 * qc - 1)) % SEQ
                mm(Sb[0:64, 0:64], qap[:, 0:64],
                   kT[pb:pb + 64, SEQB * hh + lo: SEQB * hh + lo + 64],
                   False, False)
                hi = (64 * (2 * qc + 2)) % SEQ
                mm(Sb[64:128, 0:64], qap[:, 64:128],
                   kT[pb:pb + 64, SEQB * hh + hi: SEQB * hh + hi + 64],
                   False, False)
                t_s = mm(Sb[:, 192:320], qap,
                         kTg[pb:pb + 64, 128 * hh:128 * hh + 128],
                         False, True, inc="s_pe")
                if pend is not None and _HL_LVL >= 4:
                    emit_tail(pend)
                    pend = None
                zc = 4 * ib
                if _HL_LVL < 2:
                    rS[ia].free = vop(lambda v, ia=ia: v.tensor_copy(
                        ztmp[:, 0:2], SBk[ia][:, 0:2]), waits=[t_s])
                    continue
                t_el = sop(lambda s, ia=ia, ib=ib, zc=zc: s.activation(
                    P_l[ib][:, :], SBk[ia][:, 0:192], AF.Exp, bias=0.0,
                    scale=scale, accum_out=ztmp[:, zc:zc + 1]),
                    waits=wl(t_s, rPl[ib].free))
                t_ec = sop(lambda s, ia=ia, ib=ib, zc=zc: s.activation(
                    P_c[ib][:, :], SBk[ia][:, 192:320], AF.Exp, bias=0.0,
                    scale=scale, accum_out=ztmp[:, zc + 1:zc + 2]))
                rS[ia].free = t_ec
                t_z2 = vop(lambda v, zc=zc: v.reciprocal(
                    ztmp[:, zc + 2:zc + 4], ztmp[:, zc:zc + 2]), waits=[t_ec])
                t_nl = vop(lambda v, ib=ib, zc=zc: v.tensor_scalar(
                    P_l[ib][:, :], P_l[ib][:, :], ztmp[:, zc + 2:zc + 3], None,
                    ALU.mult), waits=[t_z2])
                t_nc = vop(lambda v, ib=ib, zc=zc: v.tensor_scalar(
                    P_c[ib][:, :], P_c[ib][:, :], ztmp[:, zc + 3:zc + 4], None,
                    ALU.mult), waits=[t_z2])
                if _HL_LVL < 3:
                    rPl[ib].free = t_nl
                    rS[ia].free = t_nc
                    continue
                if _HL_LVL < 4:
                    rPl[ib].free = t_nl
                    rPT[ib].free = t_nc
                    continue
                pend = (ia, ib, h, qc, qc4, pb, t_nl, t_nc)
            if h % 2 == 1 and _HL_LVL >= 4:
                if pend is not None:
                    emit_tail(pend)
                    pend = None
                t_cx = vop(lambda v: v.tensor_copy(ctmp[:, :], psC[0][:, :]),
                           waits=[("s_pe", P.cnt["s_pe"])])
                t_ev = vop(lambda v, hh=hh: v.tensor_add(
                    ctxT[:, 512 * hh:512 * hh + 512], ctmp[:, :],
                    psC[1][:, :]), waits=[t_ev_prev_guard, t_cx])
                rC[0].free = wl(t_cx, t_ev)
                rC[1].free = t_ev
                t_ev_prev_guard = t_ev
        t_ctx = ("s_dve", P.cnt["s_dve"])
        if _QT_PARTS < 3:
            return t_ctx

        for g in keep_here:
            j = g_keep.index(g)
            for hp in range(8):
                for kc in range(4):
                    ia = kc % 2
                    tok = None
                    for hx, pb2 in ((2 * hp, 0), (2 * hp + 1, 64)):
                        tok = mm(
                            psA[ia][pb2:pb2 + 64, :],
                            qTg[pb2:pb2 + 64,
                                512 * j + 64 * hp:512 * j + 64 * hp + 64],
                            kT[pb2:pb2 + 64,
                               SEQB * hp + 512 * kc: SEQB * hp + 512 * kc + 512],
                            True, True,
                            waits=wl(rA[ia].free, t_ctx) if pb2 == 0 else (),
                            inc="s_pe" if pb2 == 64 else None)
                    t_e = sop(lambda s, ia=ia, kc=kc: s.activation(
                        P_row[:, 512 * kc:512 * kc + 512], psA[ia][:, :], AF.Exp,
                        bias=0.0, scale=scale,
                        accum_out=ztmp[:, 16 + kc:17 + kc]),
                        waits=wl(tok, rbig.free if kc == 0 else None))
                    rA[ia].free = t_e
                t_zs = vop(lambda v: v.tensor_reduce(
                    ztmp[:, 20:21], ztmp[:, 16:20], AX.X, ALU.add),
                    waits=[("s_act", P.cnt["s_act"])])
                t_zr = vop(lambda v: v.reciprocal(ztmp[:, 20:21], ztmp[:, 20:21]),
                           waits=[t_zs])
                t_n = vop(lambda v: v.tensor_scalar(
                    P_row[:, :], P_row[:, :], ztmp[:, 20:21], None, ALU.mult),
                    waits=[t_zr])
                ev_toks = []
                for g2 in range(2):
                    w0 = wl(rB[g2].free, t_n)
                    tok = None
                    for i8 in range(8):
                        ci = 8 * g2 + i8
                        tok = mm(psB[g2][:, 128 * i8:128 * i8 + 128],
                                 P_row[:, 128 * ci:128 * ci + 128], ident_bf[:, :],
                                 i8 == 0, i8 == 7,
                                 waits=w0 if i8 == 0 else (), transpose=True,
                                 inc="s_pe" if i8 == 7 else None)
                    t_ev2 = vop(lambda v, g2=g2: v.tensor_copy(
                        PT_row[:, 1024 * g2:1024 * g2 + 1024], psB[g2][:, :]),
                        waits=[tok])
                    rB[g2].free = t_ev2
                    ev_toks.append(t_ev2)
                rbig.free = ev_toks[-1]
                w0 = wl(rD[0].free, ev_toks[0], ev_toks[1])
                tok = None
                for hx, ob in ((2 * hp, 0), (2 * hp + 1, 64)):
                    for tc in range(16):
                        firstpv = (tc == 0)
                        lastpv = (tc == 15)
                        tok = mm(psD[0][ob:ob + 64, 0:64],
                                 v_all[:, D * tc + 64 * hx: D * tc + 64 * hx + 64],
                                 PT_row[:, 128 * tc + ob:128 * tc + ob + 64],
                                 firstpv, lastpv,
                                 waits=w0 if (firstpv and ob == 0) else (),
                                 inc="s_pe" if (lastpv and ob == 64) else None)
                t_ov = vop(lambda v, hp=hp, g=g: v.tensor_copy(
                    ctxT[:, 512 * hp + 64 * (g % 8): 512 * hp + 64 * (g % 8) + 64],
                    psD[0][:, 0:64]), waits=[tok])
                rD[0].free = t_ov
        t_ctx = ("s_dve", P.cnt["s_dve"])
        if _QT_PARTS < 4:
            return t_ctx

        rt_ready = []
        for doutc in range(8):
            i = doutc % 2
            t_x = dma("sync", xr_sb[i][:, :],
                      sx32[b, doutc, :, 512 * qt:512 * qt + 512],
                      waits=wl(rxr[i].free, t_ing_st))
            tok = None
            for dc in range(8):
                tok = mm(psC[i][:, :],
                         cur["wot"][:, D * dc + 128 * doutc: D * dc + 128 * doutc + 128],
                         ctxT[:, 512 * dc:512 * dc + 512],
                         dc == 0, dc == 7,
                         waits=wl(rC[i].free, t_ctx, cur["t_wo"], t_bo)
                         if dc == 0 else (),
                         inc="s_pe" if dc == 7 else None)
            t_r = vop(lambda v, doutc=doutc, i=i: v.scalar_tensor_tensor(
                r32[doutc], psC[i][:, :], boc[:, doutc:doutc + 1], xr_sb[i][:, :],
                ALU.add, ALU.add),
                waits=[tok, t_x, t_prev_st, t_prev_gp, rbig.free])
            rC[i].free = t_r
            rxr[i].free = t_r
            rt_ready.append(t_r)
        rbig.free = None
        if _QT_PARTS < 5:
            return ("s_dve", P.cnt["s_dve"])

        ln_done = layer_norm(rt_ready, r32)

        if _QT_PARTS >= 7:
            il = (4 * b + qt) % 2
            tok = None
            for dc in range(8):
                tok = mm(psD[0][0:4, 0:512], wr_sb[:, 4 * dc:4 * dc + 4],
                         r32[dc], dc == 0, dc == 7,
                         waits=wl(rD[0].free, ln_done[dc], t_wr, t_br)
                         if dc == 0 else [ln_done[dc]],
                         inc="s_pe" if dc == 7 else None)
            t_lg = vop(lambda v, il=il: v.tensor_scalar(
                lg_sb[il][:, :], psD[0][0:4, 0:512], br_c[:, :], None,
                ALU.add), waits=wl(tok, rlg[il].free))
            rD[0].free = t_lg
            rlg[il].free = dma_store(
                lg_scr[0:4, SEQ * b + 512 * qt: SEQ * b + 512 * qt + 512],
                lg_sb[il][:, :], waits=[t_lg])

        t_c = None
        for dc in range(8):
            t_c = sop(lambda s, dc=dc: s.activation(
                x1bf_out[:, 512 * dc:512 * dc + 512], r32[dc], AF.Copy,
                bias=0.0, scale=1.0),
                waits=wl(ln_done[dc], rx1o.free if dc == 0 else None))
        rx1o.free = dma_store(
            sx1bf[b, :, :, 512 * qt:512 * qt + 512].transpose([1, 0, 2]),
            x1bf_out[:, :].rearrange("p (dc t) -> p dc t", t=512),
            waits=[t_c])
        dma_store(sx32[b, :, :, 512 * qt:512 * qt + 512].transpose([1, 0, 2]),
                  big_scr[:, :].rearrange("p (dc t) -> p dc t", t=512),
                  waits=[("s_dve", P.cnt["s_dve"])])
        rbig.free = ("s_st", P.cnt["s_st"])
        return ("s_dve", P.cnt["s_dve"])

    cast_jobs = []
    for e in range(E):
        for h in range(2):
            cast_jobs.append((sw1bf[e, 512 * h:512 * h + 512, :],
                              w1_h[e, 512 * h:512 * h + 512, :]))
            cast_jobs.append((sw2bf[e, 2048 * h:2048 * h + 2048, :],
                              w2_h[e, 2048 * h:2048 * h + 2048, :]))

    def cast_piece(n=1):
        for _ in range(n):
            if cast_jobs:
                dst, srcp = cast_jobs.pop(0)
                dma("gpsimd", dst, srcp, sem="s_cast")

    for b in range(B_CORE):
        if b * 5 >= _PHASE:
            break
        es_x = ExitStack()
        cur["xTbf"] = sb(es_x, f"xTbf{b}", [128, 8 * SEQ], BF16)
        t_ing_st = ingest(b)
        if _PHASE < b * 5 + 2:
            es_x.close()
            break
        kv_proj(b)
        cast_piece(2)
        if b == 0:
            tap("xTbf", cur["xTbf"][:, :], [("s_act", P.cnt["s_act"])])
            tap("kT", kT[:, :], [("s_act", P.cnt["s_act"])])
            tap("v_all", v_all[:, :], [("s_act", P.cnt["s_act"])])
        es_x.close()
        if _PHASE < b * 5 + 3:
            break
        es_qo = ExitStack()
        cur["wqt"] = sb(es_qo, f"wqt{b}", [128, 8 * D], BF16)
        cur["wot"] = sb(es_qo, f"wot{b}", [128, 8 * D], BF16)
        cur["qx"] = sb(es_qo, f"qx{b}", [128, 8 * 512], BF16)
        t_pe_now = ("s_pe", P.cnt["s_pe"])
        t_st_now = ("s_st", P.cnt["s_st"])
        cur["t_wq"] = dma("gpsimd", cur["wqt"][:, :], swq,
                          waits=[t_pe_now, t_st_now, t_sq])
        cur["t_wo"] = dma("gpsimd", cur["wot"][:, :], swo,
                          waits=[t_so, t_st_now])
        rqx.free = None
        for qt in range(4):
            attention_qtile(b, qt, t_ing_st)
            if b == 0 and qt == 0:
                tap("qT", qT[:, :], [("s_act", P.cnt["s_act"])])
                tap("ctxT", ctxT[:, :], [("s_dve", P.cnt["s_dve"])])
            if _PHASE < b * 5 + 4 and qt == 0:
                break
        es_qo.close()

    cast_piece(len(cast_jobs))
    t_attn = {s: ("" + s, P.cnt[s]) for s in
              ("s_dve", "s_act", "s_pe", "s_st", "s_gld", "s_ld", "s_gp",
               "s_lda", "s_cast")}
    t_attn_all = list(t_attn.values())

    es_attn.close()

    P.new_block()
    for r in (rA + rC + rD):
        r.free = list(t_attn_all)
    rbig.free = list(t_attn_all)
    for r in (rrbf + rsq):
        r.free = None

    es_fw = ExitStack()
    w1sb = sb(es_fw, "w1sb", [128, 8 * HID], BF16)
    w2sb = sb(es_fw, "w2sb", [128, 32 * D], BF16)
    rw1 = Res()
    rw2 = Res()

    def load_w1(e, waits):
        tk = None
        w = wl(waits, ("s_cast", P.cnt["s_cast"]))
        for dc in range(8):
            tk = dma("sync", w1sb[:, HID * dc:HID * dc + HID],
                     sw1bf[e, 128 * dc:128 * dc + 128, :],
                     waits=w if dc == 0 else ())
        return tk

    def load_w2(e, waits):
        tk = None
        w = wl(waits, ("s_cast", P.cnt["s_cast"]))
        for r in range(8):
            dst = w2sb[:, :].rearrange("p (g d) -> p g d", d=D)[
                :, 4 * r:4 * r + 4, :]
            srcv = sw2bf[e, 512 * r:512 * r + 512, :].rearrange(
                "(g p) d -> p g d", p=128)
            tk = dma("scalar", dst, srcv, waits=w if r == 0 else ())
        return tk

    def gate_pass():
        lg_all = big_scr[0:4, 0:4096]
        m4q = sb(es_fw, "m4qf", [4, 512], F32)
        gt = [sb(es_fw, f"gtf{i}", [128, 4], F32) for i in range(4)]
        gcol = [sb(es_fw, f"gcf{i}", [128, 1], F32) for i in range(3)]
        rm4 = Res()
        t_lg = dma("scalar", lg_all, lg_scr[0:4, :],
                   waits=wl(t_attn["s_st"], t_attn["s_gp"], t_attn["s_pe"]))
        for g in range(8):
            for i4 in range(4):
                pb = psB32[(4 * g + i4) % 2]
                t_tr = mm(pb[0:128, 0:4],
                          lg_all[0:4, 512 * g + 128 * i4:
                                 512 * g + 128 * i4 + 128],
                          ident32[0:4, 0:4], True, True,
                          waits=wl(t_lg, rB[(4 * g + i4) % 2].free),
                          transpose=True, inc="s_pe")
                t_l = vop(lambda v, pb=pb: v.tensor_copy(
                    gt[0][:, :], pb[0:128, 0:4]), waits=[t_tr])
                t_m1 = vop(lambda v: v.reduce_max(gcol[0][:, :], gt[0][:, :],
                                                  AX.X, negate=True),
                           waits=[t_l])
                t_msk = vop(lambda v: v.tensor_scalar(gt[1][:, :], gt[0][:, :],
                                                      gcol[0][:, :], 0.0,
                                                      ALU.add, ALU.is_ge),
                            waits=[t_m1])
                t_t = vop(lambda v: v.scalar_tensor_tensor(
                    gt[2][:, :], gt[1][:, :], -1e30, gt[0][:, :],
                    ALU.mult, ALU.add), waits=[t_msk])
                t_m2 = vop(lambda v: v.reduce_max(gcol[1][:, :], gt[2][:, :],
                                                  AX.X, negate=True),
                           waits=[t_t])
                t_ind = vop(lambda v: v.tensor_scalar(gt[3][:, :], gt[0][:, :],
                                                      gcol[1][:, :], 0.0,
                                                      ALU.add, ALU.is_ge),
                            waits=[t_m2])
                t_e = sop(lambda s: s.activation(gt[2][:, :], gt[0][:, :],
                                                 AF.Exp, bias=gcol[0][:, :],
                                                 scale=1.0), waits=[t_m2])
                t_en = vop(lambda v: v.tensor_mul(gt[2][:, :], gt[2][:, :],
                                                  gt[3][:, :]),
                           waits=[t_e, t_ind])
                t_zz = vop(lambda v: v.reduce_sum(gcol[2][:, :], gt[2][:, :],
                                                  AX.X), waits=[t_en])
                t_w = vop(lambda v: v.reciprocal(gcol[2][:, :], gcol[2][:, :]),
                          waits=[t_zz])
                t_m = vop(lambda v: v.tensor_scalar(gt[2][:, :], gt[2][:, :],
                                                    gcol[2][:, :], None,
                                                    ALU.mult), waits=[t_w])
                t_tr2 = mm(pb[0:4, 0:128], gt[2][:, :], ident32[:, :], True,
                           True, waits=[t_m], transpose=True, inc="s_pe")
                t_cp = vop(lambda v, pb=pb, i4=i4: v.tensor_copy(
                    m4q[0:4, 128 * i4:128 * i4 + 128], pb[0:4, 0:128]),
                    waits=wl(t_tr2, rm4.free if i4 == 0 else None))
                rB[(4 * g + i4) % 2].free = t_cp
            rm4.free = dma_store(m_scr[0:4, 512 * g:512 * g + 512],
                                 m4q[0:4, :], waits=[t_cp])
        return [("s_st", P.cnt["s_st"]), ("s_dve", P.cnt["s_dve"]),
                ("s_pe", P.cnt["s_pe"])]

    if _PHASE >= 11:
        rw1.ready = load_w1(0, t_attn_all)
        rw2.ready = load_w2(0, t_attn_all)
        t_gates = gate_pass()
        rbig.free = wl(rbig.free, t_gates)

    def c_pass(e):
        t_g = sop(lambda s, e=e: s.activation(hTc[:, :], b1c[e][:, :], AF.Gelu,
                                              bias=0.0, scale=1.0),
                  waits=wl(t_b1[e], ("s_pe", P.cnt["s_pe"])))
        crow = strow[0]
        t_cpass = None
        for half in range(2):
            tok = None
            for hc in range(32):
                tok = mm(psD[1][0:1, 0:512], hTc[:, hc:hc + 1],
                         w2sb[:, D * hc + 512 * half: D * hc + 512 * half + 512],
                         hc == 0, hc == 31,
                         waits=wl(rD[1].free, t_g, rw2.ready) if hc == 0 else (),
                         inc="s_pe" if hc == 31 else None)
            t_cr = vop(lambda v: v.tensor_copy(crow[:, :], psD[1][0:1, 0:512]),
                       waits=[tok])
            rD[1].free = t_cr
            for i4 in range(4):
                dc = 4 * half + i4
                t_tr = mm(psD[1][0:128, 0:1], crow[0:1, 128 * i4:128 * i4 + 128],
                          ident32[0:1, 0:1], True, True,
                          waits=wl(t_cr, rD[1].free), transpose=True, inc="s_pe")
                t_cc = vop(lambda v, dc=dc: v.tensor_copy(
                    ccols[:, dc:dc + 1], psD[1][0:128, 0:1]), waits=[t_tr])
                rD[1].free = t_cc
                t_cpass = vop(lambda v, dc=dc: v.tensor_add(
                    csum_c[:, dc:dc + 1], csum_c[:, dc:dc + 1],
                    ccols[:, dc:dc + 1]), waits=wl(t_cc, t_csum_b2))
        return t_cpass

    def ffn_expert(e, TW, bufs):
        x1t, hT, mb_sb, mrow_t, contrib, x1_ch, out_sb = bufs
        rx1t, rmb, rmrow = ffn_res["rx1t"], ffn_res["rmb"], ffn_res["rmrow"]
        rctr, racc, rx1c = ffn_res["rctr"], ffn_res["racc"], ffn_res["rx1c"]
        rout = ffn_res["rout"]
        t_cpass = None
        last_e = (e == E - 1)
        n_tt = TOK // TW
        ntc = TW // 128
        for tt in range(n_tt):
            b_of = (TW * tt) // SEQ
            c_of = (TW * tt) % SEQ
            t_x1 = dma("sync", x1t[:, :].rearrange("p (dc t) -> p dc t", t=TW),
                       sx1bf[b_of, :, :, c_of:c_of + TW].transpose([1, 0, 2]),
                       waits=wl(rx1t.free, t_attn["s_st"]))
            t_mr = dma("sync", mrow_t[:, :],
                       m_scr[e:e + 1, TW * tt:TW * tt + TW],
                       waits=wl(rmrow.free, t_attn["s_st"]))
            t_mb = mm(psD[0][:, 0:TW], ones_row_32[:, :], mrow_t[:, :],
                      True, True, waits=wl(rD[0].free, t_mr), inc="s_pe")
            rmrow.free = t_mb
            t_mbe = vop(lambda v: v.tensor_copy(mb_sb[:, :], psD[0][:, 0:TW]),
                        waits=wl(t_mb, rmb.free))
            rD[0].free = t_mbe
            last_tile = (tt == n_tt - 1)
            gl_last = None
            for hc in range(32):
                i = hc % 2
                tok = None
                for dc in range(8):
                    tok = mm(psA[i][:, 0:TW],
                             w1sb[:, HID * dc + 128 * hc: HID * dc + 128 * hc + 128],
                             x1t[:, TW * dc:TW * dc + TW],
                             dc == 0, dc == 7,
                             waits=wl(rA[i].free, t_x1, rw1.ready)
                             if dc == 0 else (),
                             inc="s_pe" if dc == 7 else None)
                gl_last = sop(lambda s, hc=hc, i=i, e=e: s.activation(
                    hT[:, TW * hc:TW * hc + TW], psA[i][:, 0:TW], AF.Gelu,
                    bias=b1c[e][:, hc:hc + 1], scale=1.0), waits=[tok])
                rA[i].free = gl_last
            if last_tile and e < E - 1:
                rw1.ready = load_w1(e + 1, [("s_pe", P.cnt["s_pe"])])
            if tt == 0:
                t_cpass = c_pass(e)
            for dc in range(8):
                i = dc % 2
                if last_e:
                    t_a = dma("gpsimd", x1_ch[i][:, TW:2 * TW],
                              sacc[dc, :, TW * tt:TW * tt + TW],
                              waits=wl(racc[i].free,
                                       ("s_gp", P.cnt["s_gp"])
                                       if (dc == 0 and tt == 0) else None))
                    t_x = dma("scalar", x1_ch[i][:, 0:TW],
                              sx32[b_of, dc, :, c_of:c_of + TW],
                              waits=wl(rx1c[i].free))
                tok = None
                for hc in range(32):
                    tok = mm(psC[i][:, 0:TW],
                             w2sb[:, D * hc + 128 * dc: D * hc + 128 * dc + 128],
                             hT[:, TW * hc:TW * hc + TW],
                             hc == 0, hc == 31,
                             waits=wl(rC[i].free, gl_last, rw2.ready)
                             if hc == 0 else (),
                             inc="s_pe" if hc == 31 else None)
                if not last_e:
                    t_ct = vop(lambda v, dc=dc, i=i: v.scalar_tensor_tensor(
                        contrib[i][:, :], psC[i][:, 0:TW], ccols[:, dc:dc + 1],
                        mb_sb[:, :], ALU.subtract, ALU.mult),
                        waits=wl(tok, t_mbe, t_cpass, rctr[i].free))
                    rC[i].free = t_ct
                    t_st = dma_store(sacc[dc, :, TW * tt:TW * tt + TW],
                                     contrib[i][:, :], waits=[t_ct],
                                     eng="gpsimd", accum=(e > 0))
                    rctr[i].free = t_st
                else:
                    r2 = big_scr[:, TW * dc:TW * dc + TW]
                    t_1 = vop(lambda v, dc=dc, i=i, r2=r2: v.scalar_tensor_tensor(
                        r2, psC[i][:, 0:TW], ccols[:, dc:dc + 1], mb_sb[:, :],
                        ALU.subtract, ALU.mult),
                        waits=wl(tok, t_mbe, t_cpass,
                                 rbig.free if dc == 0 else None))
                    rC[i].free = t_1
                    t_2 = vop(lambda v, r2=r2, i=i: v.tensor_add(
                        r2, r2, x1_ch[i][:, TW:2 * TW]), waits=[t_1, t_a])
                    racc[i].free = t_2
                    t_3 = vop(lambda v, r2=r2, dc=dc, i=i: v.scalar_tensor_tensor(
                        r2, x1_ch[i][:, 0:TW], csum_c[:, dc:dc + 1], r2,
                        ALU.add, ALU.add), waits=[t_2, t_x])
                    rx1c[i].free = t_3
            rx1t.free = ("s_pe", P.cnt["s_pe"])
            rmb.free = ("s_dve", P.cnt["s_dve"])
            if last_e:
                rt_ready = [("s_dve", P.cnt["s_dve"])] * 8
                r2t = [big_scr[:, TW * dc:TW * dc + TW] for dc in range(8)]
                ln_done = layer_norm(rt_ready, r2t, TW=TW)
                for tc4 in range(ntc):
                    i = tc4 % 2
                    for g in range(2):
                        tok = None
                        for j in range(4):
                            dc = 4 * g + j
                            tok = mm(psB32[g][:, 128 * j:128 * j + 128],
                                     big_scr[:, TW * dc + 128 * tc4:
                                             TW * dc + 128 * tc4 + 128],
                                     ident32[:, :], j == 0, j == 3,
                                     waits=wl(rB[g].free, ln_done[dc])
                                     if j == 0 else [ln_done[dc]],
                                     transpose=True,
                                     inc="s_pe" if j == 3 else None)
                        t_ev = vop(lambda v, g=g, i=i: v.tensor_copy(
                            out_sb[i][:, 512 * g:512 * g + 512],
                            psB32[g][:, 0:512]),
                            waits=wl(tok, rout[i].free if g == 0 else None))
                        rB[g].free = t_ev
                    row0 = TW * tt + 128 * tc4
                    t_o = dma_store(y[row0:row0 + 128, :], out_sb[i][:, :],
                                    waits=[("s_dve", P.cnt["s_dve"])])
                    rout[i].free = t_o
                rbig.free = ("s_pe", P.cnt["s_pe"])
        if not last_e:
            rw2.ready = load_w2(e + 1, [("s_pe", P.cnt["s_pe"])])

    ffn_res = {k: Res() for k in ("rx1t", "rmb", "rmrow")}
    ffn_res["rctr"] = [Res(), Res()]
    ffn_res["racc"] = [Res(), Res()]
    ffn_res["rx1c"] = [Res(), Res()]
    ffn_res["rout"] = [Res(), Res()]

    n_exp = max(0, min(3, _PHASE - 10))
    es_f1 = ExitStack()
    TW1 = 512
    bufs1 = (sb(es_f1, "x1t_a", [128, 8 * TW1], BF16),
             sb(es_f1, "hT_a", [128, 32 * TW1], BF16),
             sb(es_f1, "mb_a", [128, TW1], F32),
             sb(es_f1, "mrow_a", [1, TW1], F32),
             [sb(es_f1, f"ctr_a{i}", [128, TW1], F32) for i in range(2)],
             None, None)
    for e in range(n_exp):
        ffn_expert(e, TW1, bufs1)
    es_f1.close()

    es_f2 = ExitStack()
    TW2 = 256
    t_f1 = [(s, P.cnt[s]) for s in ("s_pe", "s_dve", "s_act", "s_gp", "s_st",
                                    "s_gld", "s_ld", "s_lda", "s_cast")]
    for r in ffn_res.values():
        if isinstance(r, list):
            for rr in r:
                rr.free = list(t_f1)
        else:
            r.free = list(t_f1)
    for r in (rA + rB + rC + rD):
        r.free = list(t_f1)
    rbig.free = list(t_f1)
    ffn_res["rx1t"].free = list(t_f1)
    bufs2 = (sb(es_f2, "x1t_b", [128, 8 * TW2], BF16),
             sb(es_f2, "hT_b", [128, 32 * TW2], BF16),
             sb(es_f2, "mb_b", [128, TW2], F32),
             sb(es_f2, "mrow_b", [1, TW2], F32),
             None,
             [sb(es_f2, f"x1c_b{i}", [128, 2 * TW2], F32) for i in range(2)],
             [sb(es_f2, f"out_b{i}", [128, D], F32) for i in range(2)])
    if _PHASE >= 14:
        ffn_expert(E - 1, TW2, bufs2)
    es_f2.close()
    es_fw.close()

    P.emit("sync", lambda E_: None,
           waits=[("s_st", P.cnt["s_st"]), ("s_gp", P.cnt["s_gp"])])

    P.replay()

    es.close()
    return nc


def _gi_semantics(gi, nb):
    gi = [int(v) for v in np.asarray(gi).tolist()]
    wrapped = [v + nb if v < 0 else v for v in gi]
    g_gather = [min(max(v, 0), nb - 1) for v in wrapped]
    g_keep = sorted({v for v in wrapped if 0 <= v < nb})
    return g_gather, g_keep


def kernel(L0_attn_x, L0_attn_Wq, L0_attn_bq, L0_attn_Wk, L0_attn_bk,
           L0_attn_Wv, L0_attn_bv, L0_attn_Wo, L0_attn_bo,
           L0_attn_global_indices, L0_ffn_w1, L0_ffn_b1, L0_ffn_w2, L0_ffn_b2,
           L0_ffn_wr, L0_ffn_br):
    x = np.ascontiguousarray(np.asarray(L0_attn_x, dtype=np.float32))
    B = x.shape[0]
    g_gather, g_keep = _gi_semantics(L0_attn_global_indices, NB)
    nc = build_program(g_gather, g_keep)
    common = {
        "wq": np.ascontiguousarray(L0_attn_Wq, dtype=np.float32),
        "wk": np.ascontiguousarray(L0_attn_Wk, dtype=np.float32),
        "wv": np.ascontiguousarray(L0_attn_Wv, dtype=np.float32),
        "wo": np.ascontiguousarray(L0_attn_Wo, dtype=np.float32),
        "bq": np.ascontiguousarray(L0_attn_bq, dtype=np.float32),
        "bk": np.ascontiguousarray(L0_attn_bk, dtype=np.float32),
        "bv": np.ascontiguousarray(L0_attn_bv, dtype=np.float32),
        "bo": np.ascontiguousarray(L0_attn_bo, dtype=np.float32),
        "w1": np.ascontiguousarray(L0_ffn_w1, dtype=np.float32),
        "b1": np.ascontiguousarray(L0_ffn_b1, dtype=np.float32),
        "w2": np.ascontiguousarray(L0_ffn_w2, dtype=np.float32),
        "b2": np.ascontiguousarray(L0_ffn_b2, dtype=np.float32),
        "wr": np.ascontiguousarray(L0_ffn_wr, dtype=np.float32),
        "br": np.ascontiguousarray(L0_ffn_br, dtype=np.float32),
    }
    in_maps = []
    for c in range(N_CORES):
        shard = x[B_CORE * c:B_CORE * (c + 1)].reshape(TOK, D)
        in_maps.append(dict(common, x=np.ascontiguousarray(shard)))
    res = run_bass_kernel_spmd(nc, in_maps, core_ids=list(range(N_CORES)))
    out = np.concatenate([np.asarray(r["y"]).reshape(B_CORE, SEQ, D)
                          for r in res.results], axis=0)
    return out.astype(np.float32)


# version 13
# speedup vs baseline: 1.1403x; 1.1403x over previous
from contextlib import ExitStack

import numpy as np
import concourse.bass as bass
import concourse.mybir as mybir
from concourse.bass_utils import run_bass_kernel_spmd

F32 = mybir.dt.float32
BF16 = mybir.dt.bfloat16
AF = mybir.ActivationFunctionType
ALU = mybir.AluOpType
AX = mybir.AxisListType

N_CORES = 8
B_CORE = 2
SEQ = 2048
D = 1024
NH = 16
DH = 64
BS = 64
NB = SEQ // BS
E = 4
HID = 4096
TOK = B_CORE * SEQ

ENGINES = ("sync", "gpsimd", "tensor", "scalar", "vector")

_DEBUG_TAPS = None
_PHASE = 99
_PSB_ZERO = True
_ING_TC = 16
_ING_SKIP = ()
_QT_PARTS = 9
_HL_LVL = 9
_NH_LIM = NH
_PV_LVL = 9


class Prog:
    def __init__(self, nc, es):
        self.nc = nc
        self.es = es
        self.blocks = []
        self.cnt = {}
        self.sems = {}
        self.cur = None

    def sem(self, name):
        self.sems[name] = self.es.enter_context(self.nc.semaphore(name))
        self.cnt[name] = 0

    def new_block(self):
        self.cur = {e: [] for e in ENGINES}
        self.blocks.append(self.cur)

    DMA_SEMS = ("s_ld", "s_gld", "s_st", "s_gp", "s_lda", "s_cast")

    def emit(self, eng, fn, waits=(), inc=None, dma=False):
        w = []
        for x in waits:
            if x is None:
                continue
            if isinstance(x, list):
                w.extend(t for t in x if t is not None)
            else:
                w.append(x)
        w = [(t[0], t[1]) if len(t) == 3 else
             (t[0], self.cnt[t[0]] if t[0] in self.DMA_SEMS else t[1])
             for t in w]
        tok = None
        incs = ()
        if inc is not None:
            k = 16 if dma else 1
            self.cnt[inc] += k
            tok = (inc, self.cnt[inc])
            incs = ((inc, k),)
        self.cur[eng].append((fn, tuple(w), incs))
        return tok

    def replay(self):
        for blk_ops in self.blocks:
            with self.nc.Block() as blk:
                for eng in ENGINES:
                    ops = blk_ops[eng]
                    if not ops:
                        continue
                    def make(ops=ops):
                        def body(engobj):
                            seen = {}
                            for fn, waits, incs in ops:
                                for s, v in waits:
                                    if seen.get(s, -1) >= v:
                                        continue
                                    engobj.wait_ge(self.sems[s], v)
                                    seen[s] = v
                                instr = fn(engobj)
                                for s, k in incs:
                                    instr.then_inc(self.sems[s], k)
                        return body
                    getattr(blk, eng)(make())


class Res:
    __slots__ = ("ready", "free")

    def __init__(self):
        self.ready = None
        self.free = None


def wl(*items):
    out = []
    for it in items:
        if it is None:
            continue
        if isinstance(it, list):
            out.extend(x for x in it if x is not None)
        else:
            out.append(it)
    return out


def build_program(g_gather, g_keep):
    NKEEP = max(1, len(g_keep))
    nc = bass.Bass("TRN2", target_bir_lowering=False)

    x = nc.dram_tensor("x", [TOK, D], F32, kind="ExternalInput").ap()
    wq_h = nc.dram_tensor("wq", [D, D], F32, kind="ExternalInput").ap()
    wk_h = nc.dram_tensor("wk", [D, D], F32, kind="ExternalInput").ap()
    wv_h = nc.dram_tensor("wv", [D, D], F32, kind="ExternalInput").ap()
    wo_h = nc.dram_tensor("wo", [D, D], F32, kind="ExternalInput").ap()
    bq_h = nc.dram_tensor("bq", [D], F32, kind="ExternalInput").ap()
    bk_h = nc.dram_tensor("bk", [D], F32, kind="ExternalInput").ap()
    bv_h = nc.dram_tensor("bv", [D], F32, kind="ExternalInput").ap()
    bo_h = nc.dram_tensor("bo", [D], F32, kind="ExternalInput").ap()
    w1_h = nc.dram_tensor("w1", [E, D, HID], F32, kind="ExternalInput").ap()
    b1_h = nc.dram_tensor("b1", [E, HID], F32, kind="ExternalInput").ap()
    w2_h = nc.dram_tensor("w2", [E, HID, D], F32, kind="ExternalInput").ap()
    b2_h = nc.dram_tensor("b2", [E, D], F32, kind="ExternalInput").ap()
    wr_h = nc.dram_tensor("wr", [D, E], F32, kind="ExternalInput").ap()
    br_h = nc.dram_tensor("br", [E], F32, kind="ExternalInput").ap()
    y = nc.dram_tensor("y", [TOK, D], F32, kind="ExternalOutput").ap()

    sx32 = nc.dram_tensor("sx32", [B_CORE, 8, 128, SEQ], F32, kind="Internal").ap()
    sx1bf = nc.dram_tensor("sx1bf", [B_CORE, 8, 128, SEQ], BF16, kind="Internal").ap()
    sacc = nc.dram_tensor("sacc", [8, 128, TOK], F32, kind="Internal").ap()
    swk = nc.dram_tensor("swk", [128, 8 * D], BF16, kind="Internal").ap()
    swv = nc.dram_tensor("swv", [128, 8 * D], BF16, kind="Internal").ap()
    swq = nc.dram_tensor("swq", [128, 8 * D], BF16, kind="Internal").ap()
    swo = nc.dram_tensor("swo", [128, 8 * D], BF16, kind="Internal").ap()
    m_scr = nc.dram_tensor("m_scr", [4, TOK], F32, kind="Internal").ap()
    lg_scr = nc.dram_tensor("lg_scr", [4, TOK], F32, kind="Internal").ap()
    sw1bf = nc.dram_tensor("sw1bf", [E, D, HID], BF16, kind="Internal").ap()
    sxbf = nc.dram_tensor("sxbf", [B_CORE, 8, 128, SEQ], BF16, kind="Internal").ap()
    sw2bf = nc.dram_tensor("sw2bf", [E, HID, D], BF16, kind="Internal").ap()

    taps = {}
    if _DEBUG_TAPS is not None:
        for name, shape, dt in _DEBUG_TAPS:
            taps[name] = nc.dram_tensor("tap_" + name, shape, dt,
                                        kind="ExternalOutput").ap()

    es = ExitStack()
    P = Prog(nc, es)
    for s in ("s_ld", "s_gld", "s_st", "s_pe", "s_act", "s_dve", "s_gp",
              "s_gpc", "s_lda", "s_cast"):
        P.sem(s)

    def sb(es_, name, shape, dtype):
        return es_.enter_context(nc.sbuf_tensor(name, shape, dtype))

    def psum(name, shape, dtype):
        return es.enter_context(nc.psum_tensor(name, shape, dtype))

    ident_bf = sb(es, "ident_bf", [128, 128], BF16)
    ident32 = sb(es, "ident32", [128, 128], F32)
    ones_row_bf = sb(es, "ones_row_bf", [1, 128], BF16)
    ones_row_32 = sb(es, "ones_row_32", [1, 128], F32)
    ones_col_bf = sb(es, "ones_col_bf", [128, 1], BF16)
    b1c = [sb(es, f"b1c{e}", [128, 32], F32) for e in range(E)]
    csum_c = sb(es, "csum_c", [128, 8], F32)
    ccols = sb(es, "ccols", [128, 8], F32)
    strow = [sb(es, f"strow{i}", [1, 512], F32) for i in range(3)]
    rbf = [sb(es, f"rbf{i}", [128, 512], BF16) for i in range(2)]
    sqbf = [sb(es, f"sqbf{i}", [128, 512], BF16) for i in range(2)]
    rrbf = [Res(), Res()]
    rsq = [Res(), Res()]
    big_scr = sb(es, "big_scr", [128, 4096], F32)
    big_bf = big_scr.bitcast(BF16)
    hTc = sb(es, "hTc", [128, 32], BF16)
    zeros64 = sb(es, "zeros64", [64, 64], BF16)

    es_attn = ExitStack()
    bqc = sb(es_attn, "bqc", [128, 8], F32)
    bkc = sb(es_attn, "bkc", [128, 8], F32)
    boc = sb(es_attn, "boc", [128, 8], F32)
    bv_row = sb(es_attn, "bv_row", [1, D], BF16)
    wr_sb = sb(es_attn, "wr_sb", [128, 32], F32)
    br_c = sb(es_attn, "br_c", [4, 1], F32)
    ztmp = sb(es_attn, "ztmp", [128, 24], F32)
    lg_sb = [sb(es_attn, f"lg_sb{i}", [4, 512], F32) for i in range(2)]
    rlg = [Res(), Res()]
    brow = sb(es_attn, "brow", [32, 128], F32)
    rbrow = Res()

    psA = [psum("psA0", [128, 512], F32), psum("psA1", [128, 512], F32)]
    psB = [psum("psB0", [128, 1024], BF16), psum("psB1", [128, 1024], BF16)]
    psB32 = [t.bitcast(F32) for t in psB]
    psC = [psum("psC0", [128, 512], F32), psum("psC1", [128, 512], F32)]
    psD = [psum("psD0", [128, 512], F32), psum("psD1", [128, 512], F32)]
    rA = [Res(), Res()]
    rB = [Res(), Res()]
    rC = [Res(), Res()]
    rD = [Res(), Res()]

    def dma(eng, out_ap, in_ap, waits=(), slow=False, sem=None):
        if sem is None:
            sem = {"gpsimd": "s_gld", "scalar": "s_lda"}.get(eng, "s_ld")
        kw = {"allow_slow_non_contiguous": True} if slow else {}
        w = wl(*waits) if sem == "s_cast" else wl(*waits, (sem, P.cnt[sem]))
        return P.emit(eng, lambda E_, o=out_ap, i=in_ap, kw=kw:
                      E_.dma_start(o, i, **kw),
                      waits=w, inc=sem, dma=True)

    def dma_store(out_ap, in_ap, waits=(), eng="sync", accum=False):
        kw = {"accum_op": ALU.add} if accum else {}
        sem = "s_gp" if eng == "gpsimd" else "s_st"
        return P.emit(eng, lambda E_, o=out_ap, i=in_ap, kw=kw:
                      E_.dma_start(o, i, **kw),
                      waits=wl(*waits, (sem, P.cnt[sem])), inc=sem, dma=True)

    def mm(out_ap, lhsT, rhs, start, stop, waits=(), inc=None, transpose=False):
        def fn(E_, o=out_ap, l=lhsT, r=rhs, st=start, sp=stop, tr=transpose):
            if tr:
                return E_.matmul(o, l, r, start=True, stop=True,
                                 is_transpose=True)
            return E_.matmul(o, l, r, start=st, stop=sp)
        return P.emit("tensor", fn, waits=waits, inc=inc)

    def vop(fn, waits=()):
        return P.emit("vector", fn, waits=waits, inc="s_dve")

    def sop(fn, waits=()):
        return P.emit("scalar", fn, waits=waits, inc="s_act")

    def gop(fn, waits=(), inc="s_gpc"):
        return P.emit("gpsimd", fn, waits=waits, inc=inc)

    def tap(name, src_ap, waits):
        if name in taps:
            dma_store(taps[name], src_ap, waits=waits)

    P.new_block()

    t0 = gop(lambda g: g.memset(ident_bf[:, :], 0.0))
    t0 = gop(lambda g: g.affine_select(out=ident_bf[:, :], in_=ident_bf[:, :],
                                       compare_op=ALU.not_equal, fill=1.0, base=0,
                                       pattern=[[-1, 128]], channel_multiplier=1),
             waits=[t0])
    t1_ = gop(lambda g: g.memset(ident32[:, :], 0.0), waits=[t0])
    t1_ = gop(lambda g: g.affine_select(out=ident32[:, :], in_=ident32[:, :],
                                        compare_op=ALU.not_equal, fill=1.0, base=0,
                                        pattern=[[-1, 128]], channel_multiplier=1),
              waits=[t1_])
    t2_ = gop(lambda g: g.memset(ones_row_bf[:, :], 1.0), waits=[t1_])
    t2_ = gop(lambda g: g.memset(ones_row_32[:, :], 1.0), waits=[t2_])
    t2_ = gop(lambda g: g.memset(zeros64[:, :], 0.0), waits=[t2_])
    t_const = gop(lambda g: g.memset(ones_col_bf[:, :], 1.0), waits=[t2_])

    def load_cols(dst, src_row_ap, nrows, dst_free=None):
        t_ld = dma("sync", brow[0:nrows, :src_row_ap.shape[1]], src_row_ap,
                   waits=wl(rbrow.free))
        t_tr = mm(psD[0][0:src_row_ap.shape[1], 0:nrows],
                  brow[0:nrows, :src_row_ap.shape[1]],
                  ident32[0:nrows, 0:nrows], True, True,
                  waits=wl(t_ld, rD[0].free, t_const), transpose=True, inc="s_pe")
        rbrow.free = t_tr
        t_ev = vop(lambda v, d=dst, n=nrows, m=src_row_ap.shape[1]:
                   v.tensor_copy(d[0:m, 0:n], psD[0][0:m, 0:n]),
                   waits=wl(t_tr, dst_free))
        rD[0].free = t_ev
        return t_ev

    t_bq = load_cols(bqc, bq_h.rearrange("(c p) -> c p", p=128), 8)
    t_bk = load_cols(bkc, bk_h.rearrange("(c p) -> c p", p=128), 8)
    t_bo = load_cols(boc, bo_h.rearrange("(c p) -> c p", p=128), 8)
    t_bv = dma("gpsimd", bv_row[:, :], bv_h.rearrange("(o d) -> o d", o=1))
    t_b1 = [load_cols(b1c[e], b1_h[e, :].rearrange("(c p) -> c p", p=128), 32)
            for e in range(E)]
    b2tmp = sb(es_attn, "b2tmp", [128, 8], F32)
    t_cs = load_cols(csum_c, b2_h[0, :].rearrange("(c p) -> c p", p=128), 8)
    for e in range(1, E):
        t_e = load_cols(b2tmp, b2_h[e, :].rearrange("(c p) -> c p", p=128), 8,
                        dst_free=t_cs)
        t_cs = vop(lambda v: v.tensor_add(csum_c[:, :], csum_c[:, :], b2tmp[:, :]),
                   waits=[t_e, t_cs])
    t_csum_b2 = t_cs
    t_wr = None
    for dc in range(8):
        t_wr = dma("sync", wr_sb[:, 4 * dc:4 * dc + 4],
                   wr_h[128 * dc:128 * dc + 128, :])
    t_br = load_cols(br_c, br_h.rearrange("(o e) -> o e", o=1), 1)

    wkv_sb = sb(es_attn, "wkv", [128, 8 * D], BF16)
    rwkv = Res()
    xn = sb(es_attn, "xn", [128, D], F32)
    rxn = Res()
    wnat = xn.bitcast(BF16)[:, 0:D]
    rwnat = rxn

    def stage_weight(w_hbm, dst_sb, scratch, dst_free=None):
        last = None
        for r in range(8):
            buf = wnat
            t_ld = dma("gpsimd", buf[:, :], w_hbm[128 * r:128 * r + 128, :],
                       waits=wl(rwnat.free, t_const))
            for g in range(2):
                toks = []
                for j in range(4):
                    dc = 4 * g + j
                    toks.append(mm(psB[g][:, 128 * j:128 * j + 128],
                                   buf[:, 128 * dc:128 * dc + 128], ident_bf[:, :],
                                   j == 0, j == 3,
                                   waits=wl(t_ld, rB[g].free) if j == 0 else (),
                                   transpose=True, inc="s_pe" if j == 3 else None))
                def ev(v, g=g, r=r, dst=dst_sb):
                    dst_ap = dst[:, :].rearrange("p (dc c) -> p dc c", c=D)[
                        :, 4 * g:4 * g + 4, 128 * r:128 * r + 128]
                    return v.tensor_copy(dst_ap, psB[g][:, 0:512].rearrange(
                        "p (j c) -> p j c", c=128))
                t_ev = vop(ev, waits=wl(toks[-1],
                                        dst_free if (r == 0 and g == 0) else None))
                rB[g].free = t_ev
                last = t_ev
            rwnat.free = last
        if scratch is not None:
            return dma_store(scratch, dst_sb[:, :], waits=[last])
        return last

    t_sq = stage_weight(wq_h, wkv_sb, swq)
    t_so = stage_weight(wo_h, wkv_sb, swo, dst_free=t_sq)
    rwkv.ready = stage_weight(wk_h, wkv_sb, swk, dst_free=t_so)

    cur = {}
    rqx = Res()
    kT = sb(es_attn, "kT", [128, 8 * SEQ], BF16)
    v_all = sb(es_attn, "v_all", [128, 16 * D], BF16)
    kTg = sb(es_attn, "kTg", [128, 8 * 128], BF16)
    vg = sb(es_attn, "vg", [128, D], BF16)
    qTg = sb(es_attn, "qTg", [128, 512 * NKEEP], BF16)
    qT = sb(es_attn, "qT", [128, 8 * 512], BF16)
    ctxT = sb(es_attn, "ctxT", [128, 8 * 512], BF16)
    NPB = 4
    P_l = [sb(es_attn, f"P_l{i}", [128, 192], BF16) for i in range(NPB)]
    P_c = [sb(es_attn, f"P_c{i}", [128, 128], BF16) for i in range(NPB)]
    PT_sb = [sb(es_attn, f"PT{i}", [128, 384], BF16) for i in range(NPB)]
    rPl = [Res() for _ in range(NPB)]
    rPT = [Res() for _ in range(NPB)]
    xr_sb = [sb(es_attn, f"xr{i}", [128, 512], F32) for i in range(2)]
    rxr = [Res(), Res()]
    x1bf_out = sb(es_attn, "x1o", [128, 4096], BF16)
    rx1o = Res()
    xe32 = [sb(es_attn, "xe32_0", [128, 512], F32)]
    rxe = [Res()]
    ctmp = sb(es_attn, "ctmp", [128, 512], F32)

    rqT = Res()
    r32 = [big_scr[:, 512 * dc:512 * dc + 512] for dc in range(8)]
    P_row = big_bf[:, 0:2048]
    PT_row = big_bf[:, 2048:4096]
    rbig = Res()

    scale = 1.0 / 8.0
    SEQB = SEQ

    def ingest(b):
        for tc in range(_ING_TC):
            t_ld = dma("sync", xn[:, :],
                       x[SEQ * b + 128 * tc: SEQ * b + 128 * tc + 128, :],
                       waits=wl(rxn.free))
            for g in range(2):
                toks = []
                for j in range(4):
                    dc = 4 * g + j
                    toks.append(mm(psB32[g][:, 128 * j:128 * j + 128],
                                   xn[:, 128 * dc:128 * dc + 128],
                                   ident32[:, :], j == 0, j == 3,
                                   waits=wl(t_ld, rB[g].free) if j == 0 else (),
                                   transpose=True, inc="s_pe" if j == 3 else None))
                if "vec" in _ING_SKIP:
                    t_v = toks[-1]
                else:
                    t_v = vop(lambda v, g=g: v.tensor_copy(xe32[0][:, :],
                                                           psB32[g][:, 0:512]),
                              waits=wl(toks[-1], rxe[0].free))
                def v_ev(v, g=g, tc=tc, xTbf=cur["xTbf"]):
                    dst = xTbf[:, :].rearrange("p (dc t) -> p dc t", t=SEQB)[
                        :, 4 * g:4 * g + 4, 128 * tc:128 * tc + 128]
                    return v.tensor_copy(dst, psB32[g][:, 0:512].rearrange(
                        "p (j t) -> p j t", t=128))
                t_s = vop(v_ev, waits=[toks[-1]])
                rB[g].free = [t_v, t_s]
                if "store" in _ING_SKIP:
                    rxe[0].free = t_v
                else:
                    t_st = dma_store(
                        sx32[b, 4 * g:4 * g + 4, :, 128 * tc:128 * tc + 128]
                        .transpose([1, 0, 2]),
                        xe32[0][:, 0:512].rearrange("p (j t) -> p j t", t=128),
                        waits=[t_v])
                    rxe[0].free = t_st
            rxn.free = ("s_pe", P.cnt["s_pe"])
        dma_store(sxbf[b].transpose([1, 0, 2]),
                  cur["xTbf"][:, :].rearrange("p (dc t) -> p dc t", t=SEQB),
                  waits=[("s_dve", P.cnt["s_dve"])])
        return ("s_st", P.cnt["s_st"])

    def kv_proj(b):
        if b > 0:
            rwkv.ready = dma("gpsimd", wkv_sb[:, :], swk, waits=wl(rwkv.free))
        t_xready = [("s_act", P.cnt["s_act"]), ("s_dve", P.cnt["s_dve"])]
        for doutc in range(8):
            for tt in range(4):
                i = (doutc * 4 + tt) % 2
                tok = None
                for dc in range(8):
                    tok = mm(psA[i][:, :],
                             wkv_sb[:, D * dc + 128 * doutc: D * dc + 128 * doutc + 128],
                             cur["xTbf"][:, SEQB * dc + 512 * tt: SEQB * dc + 512 * tt + 512],
                             dc == 0, dc == 7,
                             waits=wl(rA[i].free, rwkv.ready, t_bk, t_xready)
                             if dc == 0 else (),
                             inc="s_pe" if dc == 7 else None)
                t_ev = sop(lambda s, doutc=doutc, tt=tt, i=i: s.activation(
                    kT[:, SEQB * doutc + 512 * tt: SEQB * doutc + 512 * tt + 512],
                    psA[i][:, :], AF.Identity, bias=bkc[:, doutc:doutc + 1], scale=1.0),
                    waits=[tok])
                rA[i].free = t_ev
        rwkv.free = ("s_pe", P.cnt["s_pe"])
        if b == 0:
            rwkv.ready = stage_weight(wv_h, wkv_sb, swv)
        else:
            rwkv.ready = dma("gpsimd", wkv_sb[:, :], swv, waits=wl(rwkv.free))
        for tc in range(16):
            for half in range(2):
                i = (tc * 2 + half) % 2
                tok = None
                for dc in range(8):
                    tok = mm(psC[i][:, :],
                             cur["xTbf"][:, SEQB * dc + 128 * tc: SEQB * dc + 128 * tc + 128],
                             wkv_sb[:, D * dc + 512 * half: D * dc + 512 * half + 512],
                             dc == 0, False,
                             waits=wl(rC[i].free, rwkv.ready, t_bv, t_xready)
                             if dc == 0 else ())
                tok = mm(psC[i][:, :], ones_row_bf[:, :],
                         bv_row[:, 512 * half: 512 * half + 512],
                         False, True, inc="s_pe")
                t_ev = sop(lambda s, tc=tc, half=half, i=i: s.activation(
                    v_all[:, D * tc + 512 * half: D * tc + 512 * half + 512],
                    psC[i][:, :], AF.Copy, bias=0.0, scale=1.0), waits=[tok])
                rC[i].free = t_ev
        rwkv.free = ("s_pe", P.cnt["s_pe"])
        t_kv = ("s_act", P.cnt["s_act"])
        for j, g in enumerate(g_gather):
            for dd in range(8):
                vop(lambda v, j=j, g=g, dd=dd: v.tensor_copy(
                    kTg[:, 128 * dd + 64 * j: 128 * dd + 64 * j + 64],
                    kT[:, SEQB * dd + 64 * g: SEQB * dd + 64 * g + 64]),
                    waits=[t_kv])
            vop(lambda v, j=j, g=g: v.tensor_copy(
                vg[64 * j:64 * j + 64, :],
                v_all[64 * (g % 2):64 * (g % 2) + 64,
                      D * (g // 2):D * (g // 2) + D]), waits=[t_kv])
        return ("s_dve", P.cnt["s_dve"])

    def layer_norm(rt_ready, rtiles, TW=512):
        tok0 = tok1 = None
        for dc in range(8):
            i = dc % 2
            t_c = sop(lambda s, dc=dc, i=i: s.activation(
                rbf[i][:, 0:TW], rtiles[dc], AF.Copy, bias=0.0, scale=1.0),
                waits=wl(rt_ready[dc], rrbf[i].free))
            t_q = sop(lambda s, dc=dc, i=i: s.activation(
                sqbf[i][:, 0:TW], rtiles[dc], AF.Square, bias=0.0, scale=1.0),
                waits=wl(rsq[i].free))
            tok0 = mm(psD[0][0:1, 0:TW], ones_col_bf[:, :], rbf[i][:, 0:TW],
                      dc == 0, dc == 7,
                      waits=wl(t_c, rD[0].free) if dc == 0 else [t_c],
                      inc="s_pe")
            rrbf[i].free = wl(tok0, t_c)
            tok1 = mm(psD[1][0:1, 0:TW], ones_col_bf[:, :], sqbf[i][:, 0:TW],
                      dc == 0, dc == 7,
                      waits=wl(t_q, rD[1].free) if dc == 0 else [t_q],
                      inc="s_pe")
            rsq[i].free = wl(tok1, t_q)
        t1 = vop(lambda v: v.tensor_scalar(strow[0][:, 0:TW], psD[0][0:1, 0:TW],
                                           1.0 / D, None, ALU.mult), waits=[tok0])
        t2 = vop(lambda v: v.tensor_scalar(strow[1][:, 0:TW], psD[1][0:1, 0:TW],
                                           1.0 / D, 1e-5, ALU.mult, ALU.add),
                 waits=[tok1])
        t3 = vop(lambda v: v.tensor_mul(strow[2][:, 0:TW], strow[0][:, 0:TW],
                                        strow[0][:, 0:TW]), waits=[t1])
        t4 = vop(lambda v: v.tensor_sub(strow[1][:, 0:TW], strow[1][:, 0:TW],
                                        strow[2][:, 0:TW]), waits=[t2, t3])
        t5 = sop(lambda s: s.activation(strow[2][:, 0:TW], strow[1][:, 0:TW], AF.Ln,
                                        bias=0.0, scale=1.0), waits=[t4])
        t6 = sop(lambda s: s.activation(strow[2][:, 0:TW], strow[2][:, 0:TW], AF.Exp,
                                        bias=0.0, scale=-0.5), waits=[t5])
        t7 = vop(lambda v: v.tensor_mul(strow[0][:, 0:TW], strow[0][:, 0:TW],
                                        strow[2][:, 0:TW]), waits=[t6])
        tb0 = mm(psD[0][:, 0:TW], ones_row_32[:, :], strow[2][:, 0:TW], True, True,
                 waits=[t6, t1], inc="s_pe")
        tb1 = mm(psD[1][:, 0:TW], ones_row_32[:, :], strow[0][:, 0:TW], True, True,
                 waits=[t7, t2], inc="s_pe")
        done = []
        for dc in range(8):
            ta = vop(lambda v, dc=dc: v.tensor_mul(rtiles[dc], rtiles[dc],
                                                   psD[0][:, 0:TW]), waits=[tb0])
            tb = vop(lambda v, dc=dc: v.tensor_sub(rtiles[dc], rtiles[dc],
                                                   psD[1][:, 0:TW]), waits=[tb1, ta])
            done.append(tb)
        rD[0].free = done[-1]
        rD[1].free = done[-1]
        return done

    def attention_qtile(b, qt, t_ing_st):
        t_prev_st = ("s_st", P.cnt["s_st"])
        t_prev_gp = ("s_gp", P.cnt["s_gp"])
        t_prev_v = rqT.free
        qx = cur["qx"]
        t_qx = cur.pop("t_qx_next", None)
        if t_qx is None:
            t_qx = dma("scalar",
                       qx[:, :].rearrange("p (dc t) -> p dc t", t=512),
                       sxbf[b, :, :, 512 * qt:512 * qt + 512]
                       .transpose([1, 0, 2]),
                       waits=wl(rqx.free, t_ing_st))
        rqx.free = None
        for doutc in range(8):
            i = doutc % 2
            tok = None
            for dc in range(8):
                tok = mm(psA[i][:, :],
                         cur["wqt"][:, D * dc + 128 * doutc: D * dc + 128 * doutc + 128],
                         qx[:, 512 * dc:512 * dc + 512],
                         dc == 0, dc == 7,
                         waits=wl(rA[i].free, t_bq, cur["t_wq"], t_prev_v, t_qx)
                         if dc == 0 else (),
                         inc="s_pe" if dc == 7 else None)
            t_ev = sop(lambda s, doutc=doutc, i=i: s.activation(
                qT[:, 512 * doutc:512 * doutc + 512], psA[i][:, :], AF.Identity,
                bias=bqc[:, doutc:doutc + 1], scale=1.0), waits=[tok])
            rA[i].free = t_ev
        rqx.free = ("s_pe", P.cnt["s_pe"])
        cast_piece(2)
        if qt < 3:
            cur["t_qx_next"] = dma(
                "scalar", qx[:, :].rearrange("p (dc t) -> p dc t", t=512),
                sxbf[b, :, :, 512 * (qt + 1):512 * (qt + 1) + 512]
                .transpose([1, 0, 2]), waits=wl(rqx.free, t_ing_st))
            rqx.free = None
        t_q_done = ("s_act", P.cnt["s_act"])
        if _QT_PARTS < 2:
            return ("s_act", P.cnt["s_act"])

        keep_here = [g for g in g_keep if g // 8 == qt]
        for g in keep_here:
            j = g_keep.index(g)
            for dd in range(8):
                vop(lambda v, j=j, g=g, dd=dd: v.tensor_copy(
                    qTg[:, 512 * j + 64 * dd: 512 * j + 64 * dd + 64],
                    qT[:, 512 * dd + 64 * (g % 8): 512 * dd + 64 * (g % 8) + 64]),
                    waits=[t_q_done])
        t_qg = ("s_dve", P.cnt["s_dve"])
        rqT.free = t_qg
        if qt == 0 and _PSB_ZERO:
            for g2 in range(2):
                mm(psB[g2][64:128, 256:320], zeros64[:, :],
                   ident_bf[0:64, 0:64], True, True,
                   waits=wl(rB[g2].free, t_const), transpose=True)
                t_z0 = mm(psB[g2][0:64, 320:384], zeros64[:, :],
                          ident_bf[0:64, 0:64], True, True, transpose=True,
                          inc="s_pe")
                rB[g2].free = t_z0

        SBk = [psA[0], psA[1], psD[0], psD[1]]
        rS = [rA[0], rA[1], rD[0], rD[1]]
        t_ev_prev_guard = None
        pend = None

        def emit_tail(st):
            ia, ib, h, qc, qc4, pb, t_nl, t_nc = st
            pt2 = it_pt = ia % 2
            w0 = wl(rB[pt2].free, t_nl)
            mm(psB[pt2][:, 0:128], P_l[ib][:, 64:192], ident_bf[:, :],
               True, False, waits=w0, transpose=True)
            mm(psB[pt2][0:64, 256:320], P_l[ib][64:128, 0:64],
               ident_bf[64:128, 64:128], False, False, transpose=True)
            mm(psB[pt2][64:128, 320:384], P_l[ib][0:64, 0:64],
               ident_bf[0:64, 0:64], False, False, transpose=True)
            t_tr = mm(psB[pt2][:, 128:256], P_c[ib][:, :], ident_bf[:, :],
                      False, True, waits=[t_nc], transpose=True, inc="s_pe")
            rPl[ib].free = t_tr
            t_pte = vop(lambda v, pt2=pt2, ib=ib: v.tensor_copy(
                PT_sb[ib][:, :], psB[pt2][:, 0:384]),
                waits=wl(t_tr, rPT[ib].free))
            rB[pt2].free = t_pte
            ocol = 128 * qc4
            blo = (2 * qc - 1) % NB
            bhi = (2 * qc + 2) % NB

            def vsl(bk, hx=h):
                return v_all[64 * (bk % 2):64 * (bk % 2) + 64,
                             D * (bk // 2) + 64 * hx: D * (bk // 2) + 64 * hx + 64]
            firstX = (qc4 == 0)
            wX = wl(t_pte, rC[0].free if (firstX and h % 2 == 0) else None)
            wY = wl(t_pte, rC[1].free if (firstX and h % 2 == 0) else None)
            mm(psC[0][pb:pb + 64, ocol:ocol + 128], vsl(2 * qc),
               PT_sb[ib][0:64, 0:128], firstX, False, waits=wX)
            mm(psC[0][pb:pb + 64, ocol + 64:ocol + 128], vsl(bhi),
               PT_sb[ib][0:64, 256:320], False, False)
            mm(psC[0][pb:pb + 64, ocol:ocol + 128],
               vg[:, 64 * h:64 * h + 64], PT_sb[ib][:, 128:256],
               False, (qc4 == 3), inc="s_pe")
            t_pv = mm(psC[1][pb:pb + 64, ocol:ocol + 128], vsl(2 * qc + 1),
                      PT_sb[ib][64:128, 0:128], firstX, False, waits=wY)
            t_pv = mm(psC[1][pb:pb + 64, ocol:ocol + 64], vsl(blo),
                      PT_sb[ib][64:128, 320:384], False, (qc4 == 3),
                      inc="s_pe")
            rPT[ib].free = t_pv

        for h in range(_NH_LIM):
            hh = h // 2
            pb = 64 * (h % 2)
            for qc4 in range(4):
                qc = 4 * qt + qc4
                it = h * 4 + qc4
                ia = it % 4
                ib = it % 4
                Sb = SBk[ia]
                qap = qT[pb:pb + 64,
                         512 * hh + 128 * qc4: 512 * hh + 128 * qc4 + 128]
                w0 = wl(rS[ia].free, t_q_done, t_qg)
                mm(Sb[:, 64:192], qap,
                   kT[pb:pb + 64, SEQB * hh + 128 * qc: SEQB * hh + 128 * qc + 128],
                   True, False, waits=w0)
                lo = (64 * (2 * qc - 1)) % SEQ
                mm(Sb[0:64, 0:64], qap[:, 0:64],
                   kT[pb:pb + 64, SEQB * hh + lo: SEQB * hh + lo + 64],
                   False, False)
                hi = (64 * (2 * qc + 2)) % SEQ
                mm(Sb[64:128, 0:64], qap[:, 64:128],
                   kT[pb:pb + 64, SEQB * hh + hi: SEQB * hh + hi + 64],
                   False, False)
                t_s = mm(Sb[:, 192:320], qap,
                         kTg[pb:pb + 64, 128 * hh:128 * hh + 128],
                         False, True, inc="s_pe")
                if pend is not None and _HL_LVL >= 4:
                    emit_tail(pend)
                    pend = None
                zc = 4 * ib
                if _HL_LVL < 2:
                    rS[ia].free = vop(lambda v, ia=ia: v.tensor_copy(
                        ztmp[:, 0:2], SBk[ia][:, 0:2]), waits=[t_s])
                    continue
                t_el = sop(lambda s, ia=ia, ib=ib, zc=zc: s.activation(
                    P_l[ib][:, :], SBk[ia][:, 0:192], AF.Exp, bias=0.0,
                    scale=scale, accum_out=ztmp[:, zc:zc + 1]),
                    waits=wl(t_s, rPl[ib].free))
                t_ec = sop(lambda s, ia=ia, ib=ib, zc=zc: s.activation(
                    P_c[ib][:, :], SBk[ia][:, 192:320], AF.Exp, bias=0.0,
                    scale=scale, accum_out=ztmp[:, zc + 1:zc + 2]))
                rS[ia].free = t_ec
                t_z2 = vop(lambda v, zc=zc: v.reciprocal(
                    ztmp[:, zc + 2:zc + 4], ztmp[:, zc:zc + 2]), waits=[t_ec])
                t_nl = vop(lambda v, ib=ib, zc=zc: v.tensor_scalar(
                    P_l[ib][:, :], P_l[ib][:, :], ztmp[:, zc + 2:zc + 3], None,
                    ALU.mult), waits=[t_z2])
                t_nc = vop(lambda v, ib=ib, zc=zc: v.tensor_scalar(
                    P_c[ib][:, :], P_c[ib][:, :], ztmp[:, zc + 3:zc + 4], None,
                    ALU.mult), waits=[t_z2])
                if _HL_LVL < 3:
                    rPl[ib].free = t_nl
                    rS[ia].free = t_nc
                    continue
                if _HL_LVL < 4:
                    rPl[ib].free = t_nl
                    rPT[ib].free = t_nc
                    continue
                pend = (ia, ib, h, qc, qc4, pb, t_nl, t_nc)
            if h % 2 == 1 and _HL_LVL >= 4:
                if pend is not None:
                    emit_tail(pend)
                    pend = None
                t_cx = vop(lambda v: v.tensor_copy(ctmp[:, :], psC[0][:, :]),
                           waits=[("s_pe", P.cnt["s_pe"])])
                t_ev = vop(lambda v, hh=hh: v.tensor_add(
                    ctxT[:, 512 * hh:512 * hh + 512], ctmp[:, :],
                    psC[1][:, :]), waits=[t_ev_prev_guard, t_cx])
                rC[0].free = wl(t_cx, t_ev)
                rC[1].free = t_ev
                t_ev_prev_guard = t_ev
        t_ctx = ("s_dve", P.cnt["s_dve"])
        if _QT_PARTS < 3:
            return t_ctx

        for g in keep_here:
            j = g_keep.index(g)
            for hp in range(8):
                for kc in range(4):
                    ia = kc % 2
                    tok = None
                    for hx, pb2 in ((2 * hp, 0), (2 * hp + 1, 64)):
                        tok = mm(
                            psA[ia][pb2:pb2 + 64, :],
                            qTg[pb2:pb2 + 64,
                                512 * j + 64 * hp:512 * j + 64 * hp + 64],
                            kT[pb2:pb2 + 64,
                               SEQB * hp + 512 * kc: SEQB * hp + 512 * kc + 512],
                            True, True,
                            waits=wl(rA[ia].free, t_ctx) if pb2 == 0 else (),
                            inc="s_pe" if pb2 == 64 else None)
                    t_e = sop(lambda s, ia=ia, kc=kc: s.activation(
                        P_row[:, 512 * kc:512 * kc + 512], psA[ia][:, :], AF.Exp,
                        bias=0.0, scale=scale,
                        accum_out=ztmp[:, 16 + kc:17 + kc]),
                        waits=wl(tok, rbig.free if kc == 0 else None))
                    rA[ia].free = t_e
                t_zs = vop(lambda v: v.tensor_reduce(
                    ztmp[:, 20:21], ztmp[:, 16:20], AX.X, ALU.add),
                    waits=[("s_act", P.cnt["s_act"])])
                t_zr = vop(lambda v: v.reciprocal(ztmp[:, 20:21], ztmp[:, 20:21]),
                           waits=[t_zs])
                t_n = vop(lambda v: v.tensor_scalar(
                    P_row[:, :], P_row[:, :], ztmp[:, 20:21], None, ALU.mult),
                    waits=[t_zr])
                ev_toks = []
                for g2 in range(2):
                    w0 = wl(rB[g2].free, t_n)
                    tok = None
                    for i8 in range(8):
                        ci = 8 * g2 + i8
                        tok = mm(psB[g2][:, 128 * i8:128 * i8 + 128],
                                 P_row[:, 128 * ci:128 * ci + 128], ident_bf[:, :],
                                 i8 == 0, i8 == 7,
                                 waits=w0 if i8 == 0 else (), transpose=True,
                                 inc="s_pe" if i8 == 7 else None)
                    t_ev2 = vop(lambda v, g2=g2: v.tensor_copy(
                        PT_row[:, 1024 * g2:1024 * g2 + 1024], psB[g2][:, :]),
                        waits=[tok])
                    rB[g2].free = t_ev2
                    ev_toks.append(t_ev2)
                rbig.free = ev_toks[-1]
                w0 = wl(rD[0].free, ev_toks[0], ev_toks[1])
                tok = None
                for hx, ob in ((2 * hp, 0), (2 * hp + 1, 64)):
                    for tc in range(16):
                        firstpv = (tc == 0)
                        lastpv = (tc == 15)
                        tok = mm(psD[0][ob:ob + 64, 0:64],
                                 v_all[:, D * tc + 64 * hx: D * tc + 64 * hx + 64],
                                 PT_row[:, 128 * tc + ob:128 * tc + ob + 64],
                                 firstpv, lastpv,
                                 waits=w0 if (firstpv and ob == 0) else (),
                                 inc="s_pe" if (lastpv and ob == 64) else None)
                t_ov = vop(lambda v, hp=hp, g=g: v.tensor_copy(
                    ctxT[:, 512 * hp + 64 * (g % 8): 512 * hp + 64 * (g % 8) + 64],
                    psD[0][:, 0:64]), waits=[tok])
                rD[0].free = t_ov
        t_ctx = ("s_dve", P.cnt["s_dve"])
        if _QT_PARTS < 4:
            return t_ctx

        rt_ready = []
        for doutc in range(8):
            i = doutc % 2
            t_x = dma("sync", xr_sb[i][:, :],
                      sx32[b, doutc, :, 512 * qt:512 * qt + 512],
                      waits=wl(rxr[i].free, t_ing_st))
            tok = None
            for dc in range(8):
                tok = mm(psC[i][:, :],
                         cur["wot"][:, D * dc + 128 * doutc: D * dc + 128 * doutc + 128],
                         ctxT[:, 512 * dc:512 * dc + 512],
                         dc == 0, dc == 7,
                         waits=wl(rC[i].free, t_ctx, cur["t_wo"], t_bo)
                         if dc == 0 else (),
                         inc="s_pe" if dc == 7 else None)
            t_r = vop(lambda v, doutc=doutc, i=i: v.scalar_tensor_tensor(
                r32[doutc], psC[i][:, :], boc[:, doutc:doutc + 1], xr_sb[i][:, :],
                ALU.add, ALU.add),
                waits=[tok, t_x, t_prev_st, t_prev_gp, rbig.free])
            rC[i].free = t_r
            rxr[i].free = t_r
            rt_ready.append(t_r)
        rbig.free = None
        if _QT_PARTS < 5:
            return ("s_dve", P.cnt["s_dve"])

        ln_done = layer_norm(rt_ready, r32)

        if _QT_PARTS >= 7:
            il = (4 * b + qt) % 2
            tok = None
            for dc in range(8):
                tok = mm(psD[0][0:4, 0:512], wr_sb[:, 4 * dc:4 * dc + 4],
                         r32[dc], dc == 0, dc == 7,
                         waits=wl(rD[0].free, ln_done[dc], t_wr, t_br)
                         if dc == 0 else [ln_done[dc]],
                         inc="s_pe" if dc == 7 else None)
            t_lg = vop(lambda v, il=il: v.tensor_scalar(
                lg_sb[il][:, :], psD[0][0:4, 0:512], br_c[:, :], None,
                ALU.add), waits=wl(tok, rlg[il].free))
            rD[0].free = t_lg
            rlg[il].free = dma_store(
                lg_scr[0:4, SEQ * b + 512 * qt: SEQ * b + 512 * qt + 512],
                lg_sb[il][:, :], waits=[t_lg])

        t_c = None
        for dc in range(8):
            t_c = sop(lambda s, dc=dc: s.activation(
                x1bf_out[:, 512 * dc:512 * dc + 512], r32[dc], AF.Copy,
                bias=0.0, scale=1.0),
                waits=wl(ln_done[dc], rx1o.free if dc == 0 else None))
        rx1o.free = dma_store(
            sx1bf[b, :, :, 512 * qt:512 * qt + 512].transpose([1, 0, 2]),
            x1bf_out[:, :].rearrange("p (dc t) -> p dc t", t=512),
            waits=[t_c])
        dma_store(sx32[b, :, :, 512 * qt:512 * qt + 512].transpose([1, 0, 2]),
                  big_scr[:, :].rearrange("p (dc t) -> p dc t", t=512),
                  waits=[("s_dve", P.cnt["s_dve"])])
        rbig.free = ("s_st", P.cnt["s_st"])
        return ("s_dve", P.cnt["s_dve"])

    cast_jobs = []
    for e in range(E):
        for h in range(2):
            cast_jobs.append((sw1bf[e, 512 * h:512 * h + 512, :],
                              w1_h[e, 512 * h:512 * h + 512, :]))
            cast_jobs.append((sw2bf[e, 2048 * h:2048 * h + 2048, :],
                              w2_h[e, 2048 * h:2048 * h + 2048, :]))

    def cast_piece(n=1):
        for _ in range(n):
            if cast_jobs:
                dst, srcp = cast_jobs.pop(0)
                dma("gpsimd", dst, srcp, sem="s_cast")

    for b in range(B_CORE):
        if b * 5 >= _PHASE:
            break
        es_x = ExitStack()
        cur["xTbf"] = sb(es_x, f"xTbf{b}", [128, 8 * SEQ], BF16)
        t_ing_st = ingest(b)
        if _PHASE < b * 5 + 2:
            es_x.close()
            break
        kv_proj(b)
        cast_piece(2)
        if b == 0:
            tap("xTbf", cur["xTbf"][:, :], [("s_act", P.cnt["s_act"])])
            tap("kT", kT[:, :], [("s_act", P.cnt["s_act"])])
            tap("v_all", v_all[:, :], [("s_act", P.cnt["s_act"])])
        es_x.close()
        if _PHASE < b * 5 + 3:
            break
        es_qo = ExitStack()
        cur["wqt"] = sb(es_qo, f"wqt{b}", [128, 8 * D], BF16)
        cur["wot"] = sb(es_qo, f"wot{b}", [128, 8 * D], BF16)
        cur["qx"] = sb(es_qo, f"qx{b}", [128, 8 * 512], BF16)
        t_pe_now = ("s_pe", P.cnt["s_pe"])
        t_st_now = ("s_st", P.cnt["s_st"])
        cur["t_wq"] = dma("gpsimd", cur["wqt"][:, :], swq,
                          waits=[t_pe_now, t_st_now, t_sq])
        cur["t_wo"] = dma("gpsimd", cur["wot"][:, :], swo,
                          waits=[t_so, t_st_now])
        rqx.free = None
        for qt in range(4):
            attention_qtile(b, qt, t_ing_st)
            if b == 0 and qt == 0:
                tap("qT", qT[:, :], [("s_act", P.cnt["s_act"])])
                tap("ctxT", ctxT[:, :], [("s_dve", P.cnt["s_dve"])])
            if _PHASE < b * 5 + 4 and qt == 0:
                break
        es_qo.close()

    cast_piece(len(cast_jobs))
    t_attn = {s: ("" + s, P.cnt[s]) for s in
              ("s_dve", "s_act", "s_pe", "s_st", "s_gld", "s_ld", "s_gp",
               "s_lda", "s_cast")}
    t_attn_all = list(t_attn.values())

    es_attn.close()

    P.new_block()
    for r in (rA + rC + rD):
        r.free = list(t_attn_all)
    rbig.free = list(t_attn_all)
    for r in (rrbf + rsq):
        r.free = None

    es_fw = ExitStack()
    w1sb = sb(es_fw, "w1sb", [128, 8 * HID], BF16)
    w2sb = sb(es_fw, "w2sb", [128, 32 * D], BF16)
    rw1 = Res()
    rw2 = Res()

    def load_w1(e, waits):
        tk = None
        w = wl(waits, ("s_cast", P.cnt["s_cast"]))
        for dc in range(8):
            tk = dma("sync", w1sb[:, HID * dc:HID * dc + HID],
                     sw1bf[e, 128 * dc:128 * dc + 128, :],
                     waits=w if dc == 0 else ())
        return tk

    def load_w2(e, waits):
        tk = None
        w = wl(waits, ("s_cast", P.cnt["s_cast"]))
        for r in range(8):
            dst = w2sb[:, :].rearrange("p (g d) -> p g d", d=D)[
                :, 4 * r:4 * r + 4, :]
            srcv = sw2bf[e, 512 * r:512 * r + 512, :].rearrange(
                "(g p) d -> p g d", p=128)
            tk = dma("scalar", dst, srcv, waits=w if r == 0 else ())
        return tk

    def gate_pass():
        lg_all = big_scr[0:4, 0:4096]
        m4q = sb(es_fw, "m4qf", [4, 512], F32)
        gt = [sb(es_fw, f"gtf{i}", [128, 4], F32) for i in range(4)]
        gcol = [sb(es_fw, f"gcf{i}", [128, 1], F32) for i in range(3)]
        rm4 = Res()
        t_lg = dma("scalar", lg_all, lg_scr[0:4, :],
                   waits=wl(t_attn["s_st"], t_attn["s_gp"], t_attn["s_pe"]))
        for g in range(8):
            for i4 in range(4):
                pb = psB32[(4 * g + i4) % 2]
                t_tr = mm(pb[0:128, 0:4],
                          lg_all[0:4, 512 * g + 128 * i4:
                                 512 * g + 128 * i4 + 128],
                          ident32[0:4, 0:4], True, True,
                          waits=wl(t_lg, rB[(4 * g + i4) % 2].free),
                          transpose=True, inc="s_pe")
                t_l = vop(lambda v, pb=pb: v.tensor_copy(
                    gt[0][:, :], pb[0:128, 0:4]), waits=[t_tr])
                t_m1 = vop(lambda v: v.reduce_max(gcol[0][:, :], gt[0][:, :],
                                                  AX.X, negate=True),
                           waits=[t_l])
                t_msk = vop(lambda v: v.tensor_scalar(gt[1][:, :], gt[0][:, :],
                                                      gcol[0][:, :], 0.0,
                                                      ALU.add, ALU.is_ge),
                            waits=[t_m1])
                t_t = vop(lambda v: v.scalar_tensor_tensor(
                    gt[2][:, :], gt[1][:, :], -1e30, gt[0][:, :],
                    ALU.mult, ALU.add), waits=[t_msk])
                t_m2 = vop(lambda v: v.reduce_max(gcol[1][:, :], gt[2][:, :],
                                                  AX.X, negate=True),
                           waits=[t_t])
                t_ind = vop(lambda v: v.tensor_scalar(gt[3][:, :], gt[0][:, :],
                                                      gcol[1][:, :], 0.0,
                                                      ALU.add, ALU.is_ge),
                            waits=[t_m2])
                t_e = sop(lambda s: s.activation(gt[2][:, :], gt[0][:, :],
                                                 AF.Exp, bias=gcol[0][:, :],
                                                 scale=1.0), waits=[t_m2])
                t_en = vop(lambda v: v.tensor_mul(gt[2][:, :], gt[2][:, :],
                                                  gt[3][:, :]),
                           waits=[t_e, t_ind])
                t_zz = vop(lambda v: v.reduce_sum(gcol[2][:, :], gt[2][:, :],
                                                  AX.X), waits=[t_en])
                t_w = vop(lambda v: v.reciprocal(gcol[2][:, :], gcol[2][:, :]),
                          waits=[t_zz])
                t_m = vop(lambda v: v.tensor_scalar(gt[2][:, :], gt[2][:, :],
                                                    gcol[2][:, :], None,
                                                    ALU.mult), waits=[t_w])
                t_tr2 = mm(pb[0:4, 0:128], gt[2][:, :], ident32[:, :], True,
                           True, waits=[t_m], transpose=True, inc="s_pe")
                t_cp = vop(lambda v, pb=pb, i4=i4: v.tensor_copy(
                    m4q[0:4, 128 * i4:128 * i4 + 128], pb[0:4, 0:128]),
                    waits=wl(t_tr2, rm4.free if i4 == 0 else None))
                rB[(4 * g + i4) % 2].free = t_cp
            rm4.free = dma_store(m_scr[0:4, 512 * g:512 * g + 512],
                                 m4q[0:4, :], waits=[t_cp])
        return [("s_st", P.cnt["s_st"]), ("s_dve", P.cnt["s_dve"]),
                ("s_pe", P.cnt["s_pe"])]

    if _PHASE >= 11:
        rw1.ready = load_w1(0, t_attn_all)
        rw2.ready = load_w2(0, t_attn_all)
        t_gates = gate_pass()
        rbig.free = wl(rbig.free, t_gates)

    def c_pass(e):
        t_g = sop(lambda s, e=e: s.activation(hTc[:, :], b1c[e][:, :], AF.Gelu,
                                              bias=0.0, scale=1.0),
                  waits=wl(t_b1[e], ("s_pe", P.cnt["s_pe"])))
        crow = strow[0]
        t_cpass = None
        for half in range(2):
            tok = None
            for hc in range(32):
                tok = mm(psD[1][0:1, 0:512], hTc[:, hc:hc + 1],
                         w2sb[:, D * hc + 512 * half: D * hc + 512 * half + 512],
                         hc == 0, hc == 31,
                         waits=wl(rD[1].free, t_g, rw2.ready) if hc == 0 else (),
                         inc="s_pe" if hc == 31 else None)
            t_cr = vop(lambda v: v.tensor_copy(crow[:, :], psD[1][0:1, 0:512]),
                       waits=[tok])
            rD[1].free = t_cr
            for i4 in range(4):
                dc = 4 * half + i4
                t_tr = mm(psD[1][0:128, 0:1], crow[0:1, 128 * i4:128 * i4 + 128],
                          ident32[0:1, 0:1], True, True,
                          waits=wl(t_cr, rD[1].free), transpose=True, inc="s_pe")
                t_cc = vop(lambda v, dc=dc: v.tensor_copy(
                    ccols[:, dc:dc + 1], psD[1][0:128, 0:1]), waits=[t_tr])
                rD[1].free = t_cc
                t_cpass = vop(lambda v, dc=dc: v.tensor_add(
                    csum_c[:, dc:dc + 1], csum_c[:, dc:dc + 1],
                    ccols[:, dc:dc + 1]), waits=wl(t_cc, t_csum_b2))
        return t_cpass

    def ffn_expert(e, TW, bufs):
        x1t, hT, mb_sb, mrow_t, contrib, x1_ch, out_sb = bufs
        rx1t, rmb, rmrow = ffn_res["rx1t"], ffn_res["rmb"], ffn_res["rmrow"]
        rctr, racc, rx1c = ffn_res["rctr"], ffn_res["racc"], ffn_res["rx1c"]
        rout = ffn_res["rout"]
        t_cpass = None
        last_e = (e == E - 1)
        n_tt = TOK // TW
        ntc = TW // 128
        for tt in range(n_tt):
            b_of = (TW * tt) // SEQ
            c_of = (TW * tt) % SEQ
            t_x1 = dma("sync", x1t[:, :].rearrange("p (dc t) -> p dc t", t=TW),
                       sx1bf[b_of, :, :, c_of:c_of + TW].transpose([1, 0, 2]),
                       waits=wl(rx1t.free, t_attn["s_st"]))
            t_mr = dma("sync", mrow_t[:, :],
                       m_scr[e:e + 1, TW * tt:TW * tt + TW],
                       waits=wl(rmrow.free, t_attn["s_st"]))
            t_mb = mm(psD[0][:, 0:TW], ones_row_32[:, :], mrow_t[:, :],
                      True, True, waits=wl(rD[0].free, t_mr), inc="s_pe")
            rmrow.free = t_mb
            t_mbe = vop(lambda v: v.tensor_copy(mb_sb[:, :], psD[0][:, 0:TW]),
                        waits=wl(t_mb, rmb.free))
            rD[0].free = t_mbe
            last_tile = (tt == n_tt - 1)
            gl_last = None
            for hc in range(32):
                i = hc % 2
                tok = None
                for dc in range(8):
                    tok = mm(psA[i][:, 0:TW],
                             w1sb[:, HID * dc + 128 * hc: HID * dc + 128 * hc + 128],
                             x1t[:, TW * dc:TW * dc + TW],
                             dc == 0, dc == 7,
                             waits=wl(rA[i].free, t_x1, rw1.ready)
                             if dc == 0 else (),
                             inc="s_pe" if dc == 7 else None)
                gl_last = sop(lambda s, hc=hc, i=i, e=e: s.activation(
                    hT[:, TW * hc:TW * hc + TW], psA[i][:, 0:TW], AF.Gelu,
                    bias=b1c[e][:, hc:hc + 1], scale=1.0), waits=[tok])
                rA[i].free = gl_last
            if last_tile and e < E - 1:
                rw1.ready = load_w1(e + 1, [("s_pe", P.cnt["s_pe"])])
            if tt == 0:
                t_cpass = c_pass(e)
            for dc in range(8):
                i = dc % 2
                if last_e:
                    t_a = dma("gpsimd", x1_ch[i][:, TW:2 * TW],
                              sacc[dc, :, TW * tt:TW * tt + TW],
                              waits=wl(racc[i].free,
                                       ("s_gp", P.cnt["s_gp"])
                                       if (dc == 0 and tt == 0) else None))
                    t_x = dma("scalar", x1_ch[i][:, 0:TW],
                              sx32[b_of, dc, :, c_of:c_of + TW],
                              waits=wl(rx1c[i].free))
                tok = None
                for hc in range(32):
                    tok = mm(psC[i][:, 0:TW],
                             w2sb[:, D * hc + 128 * dc: D * hc + 128 * dc + 128],
                             hT[:, TW * hc:TW * hc + TW],
                             hc == 0, hc == 31,
                             waits=wl(rC[i].free, gl_last, rw2.ready)
                             if hc == 0 else (),
                             inc="s_pe" if hc == 31 else None)
                if not last_e:
                    t_ct = vop(lambda v, dc=dc, i=i: v.scalar_tensor_tensor(
                        contrib[i][:, :], psC[i][:, 0:TW], ccols[:, dc:dc + 1],
                        mb_sb[:, :], ALU.subtract, ALU.mult),
                        waits=wl(tok, t_mbe, t_cpass, rctr[i].free))
                    rC[i].free = t_ct
                    t_st = dma_store(sacc[dc, :, TW * tt:TW * tt + TW],
                                     contrib[i][:, :], waits=[t_ct],
                                     eng="gpsimd", accum=(e > 0))
                    rctr[i].free = t_st
                else:
                    r2 = big_scr[:, TW * dc:TW * dc + TW]
                    t_1 = vop(lambda v, dc=dc, i=i, r2=r2: v.scalar_tensor_tensor(
                        r2, psC[i][:, 0:TW], ccols[:, dc:dc + 1], mb_sb[:, :],
                        ALU.subtract, ALU.mult),
                        waits=wl(tok, t_mbe, t_cpass,
                                 rbig.free if dc == 0 else None))
                    rC[i].free = t_1
                    t_2 = vop(lambda v, r2=r2, i=i: v.tensor_add(
                        r2, r2, x1_ch[i][:, TW:2 * TW]), waits=[t_1, t_a])
                    racc[i].free = t_2
                    t_3 = vop(lambda v, r2=r2, dc=dc, i=i: v.scalar_tensor_tensor(
                        r2, x1_ch[i][:, 0:TW], csum_c[:, dc:dc + 1], r2,
                        ALU.add, ALU.add), waits=[t_2, t_x])
                    rx1c[i].free = t_3
            rx1t.free = ("s_pe", P.cnt["s_pe"])
            rmb.free = ("s_dve", P.cnt["s_dve"])
            if last_e:
                rt_ready = [("s_dve", P.cnt["s_dve"])] * 8
                r2t = [big_scr[:, TW * dc:TW * dc + TW] for dc in range(8)]
                ln_done = layer_norm(rt_ready, r2t, TW=TW)
                for tc4 in range(ntc):
                    i = tc4 % 2
                    for g in range(2):
                        tok = None
                        for j in range(4):
                            dc = 4 * g + j
                            tok = mm(psB32[g][:, 128 * j:128 * j + 128],
                                     big_scr[:, TW * dc + 128 * tc4:
                                             TW * dc + 128 * tc4 + 128],
                                     ident32[:, :], j == 0, j == 3,
                                     waits=wl(rB[g].free, ln_done[dc])
                                     if j == 0 else [ln_done[dc]],
                                     transpose=True,
                                     inc="s_pe" if j == 3 else None)
                        t_ev = vop(lambda v, g=g, i=i: v.tensor_copy(
                            out_sb[i][:, 512 * g:512 * g + 512],
                            psB32[g][:, 0:512]),
                            waits=wl(tok, rout[i].free if g == 0 else None))
                        rB[g].free = t_ev
                    row0 = TW * tt + 128 * tc4
                    t_o = dma_store(y[row0:row0 + 128, :], out_sb[i][:, :],
                                    waits=[("s_dve", P.cnt["s_dve"])])
                    rout[i].free = t_o
                rbig.free = ("s_pe", P.cnt["s_pe"])
        if not last_e:
            rw2.ready = load_w2(e + 1, [("s_pe", P.cnt["s_pe"])])

    ffn_res = {k: Res() for k in ("rx1t", "rmb", "rmrow")}
    ffn_res["rctr"] = [Res(), Res()]
    ffn_res["racc"] = [Res(), Res()]
    ffn_res["rx1c"] = [Res(), Res()]
    ffn_res["rout"] = [Res(), Res()]

    n_exp = max(0, min(3, _PHASE - 10))
    es_f1 = ExitStack()
    TW1 = 512
    bufs1 = (sb(es_f1, "x1t_a", [128, 8 * TW1], BF16),
             sb(es_f1, "hT_a", [128, 32 * TW1], BF16),
             sb(es_f1, "mb_a", [128, TW1], F32),
             sb(es_f1, "mrow_a", [1, TW1], F32),
             [sb(es_f1, f"ctr_a{i}", [128, TW1], F32) for i in range(2)],
             None, None)
    for e in range(n_exp):
        ffn_expert(e, TW1, bufs1)
    es_f1.close()

    es_f2 = ExitStack()
    TW2 = 256
    t_f1 = [(s, P.cnt[s]) for s in ("s_pe", "s_dve", "s_act", "s_gp", "s_st",
                                    "s_gld", "s_ld", "s_lda", "s_cast")]
    for r in ffn_res.values():
        if isinstance(r, list):
            for rr in r:
                rr.free = list(t_f1)
        else:
            r.free = list(t_f1)
    for r in (rA + rB + rC + rD):
        r.free = list(t_f1)
    rbig.free = list(t_f1)
    ffn_res["rx1t"].free = list(t_f1)
    bufs2 = (sb(es_f2, "x1t_b", [128, 8 * TW2], BF16),
             sb(es_f2, "hT_b", [128, 32 * TW2], BF16),
             sb(es_f2, "mb_b", [128, TW2], F32),
             sb(es_f2, "mrow_b", [1, TW2], F32),
             None,
             [sb(es_f2, f"x1c_b{i}", [128, 2 * TW2], F32) for i in range(2)],
             [sb(es_f2, f"out_b{i}", [128, D], F32) for i in range(2)])
    if _PHASE >= 14:
        ffn_expert(E - 1, TW2, bufs2)
    es_f2.close()
    es_fw.close()

    P.emit("sync", lambda E_: None,
           waits=[("s_st", P.cnt["s_st"]), ("s_gp", P.cnt["s_gp"])])

    P.replay()

    es.close()
    return nc


def _gi_semantics(gi, nb):
    gi = [int(v) for v in np.asarray(gi).tolist()]
    wrapped = [v + nb if v < 0 else v for v in gi]
    g_gather = [min(max(v, 0), nb - 1) for v in wrapped]
    g_keep = sorted({v for v in wrapped if 0 <= v < nb})
    return g_gather, g_keep


def kernel(L0_attn_x, L0_attn_Wq, L0_attn_bq, L0_attn_Wk, L0_attn_bk,
           L0_attn_Wv, L0_attn_bv, L0_attn_Wo, L0_attn_bo,
           L0_attn_global_indices, L0_ffn_w1, L0_ffn_b1, L0_ffn_w2, L0_ffn_b2,
           L0_ffn_wr, L0_ffn_br):
    x = np.ascontiguousarray(np.asarray(L0_attn_x, dtype=np.float32))
    B = x.shape[0]
    g_gather, g_keep = _gi_semantics(L0_attn_global_indices, NB)
    nc = build_program(g_gather, g_keep)
    common = {
        "wq": np.ascontiguousarray(L0_attn_Wq, dtype=np.float32),
        "wk": np.ascontiguousarray(L0_attn_Wk, dtype=np.float32),
        "wv": np.ascontiguousarray(L0_attn_Wv, dtype=np.float32),
        "wo": np.ascontiguousarray(L0_attn_Wo, dtype=np.float32),
        "bq": np.ascontiguousarray(L0_attn_bq, dtype=np.float32),
        "bk": np.ascontiguousarray(L0_attn_bk, dtype=np.float32),
        "bv": np.ascontiguousarray(L0_attn_bv, dtype=np.float32),
        "bo": np.ascontiguousarray(L0_attn_bo, dtype=np.float32),
        "w1": np.ascontiguousarray(L0_ffn_w1, dtype=np.float32),
        "b1": np.ascontiguousarray(L0_ffn_b1, dtype=np.float32),
        "w2": np.ascontiguousarray(L0_ffn_w2, dtype=np.float32),
        "b2": np.ascontiguousarray(L0_ffn_b2, dtype=np.float32),
        "wr": np.ascontiguousarray(L0_ffn_wr, dtype=np.float32),
        "br": np.ascontiguousarray(L0_ffn_br, dtype=np.float32),
    }
    in_maps = []
    for c in range(N_CORES):
        shard = x[B_CORE * c:B_CORE * (c + 1)].reshape(TOK, D)
        in_maps.append(dict(common, x=np.ascontiguousarray(shard)))
    res = run_bass_kernel_spmd(nc, in_maps, core_ids=list(range(N_CORES)))
    out = np.concatenate([np.asarray(r["y"]).reshape(B_CORE, SEQ, D)
                          for r in res.results], axis=0)
    return out.astype(np.float32)


# version 14
# speedup vs baseline: 1.1821x; 1.0366x over previous
from contextlib import ExitStack

import numpy as np
import concourse.bass as bass
import concourse.mybir as mybir
from concourse.bass_utils import run_bass_kernel_spmd

F32 = mybir.dt.float32
BF16 = mybir.dt.bfloat16
AF = mybir.ActivationFunctionType
ALU = mybir.AluOpType
AX = mybir.AxisListType

N_CORES = 8
B_CORE = 2
SEQ = 2048
D = 1024
NH = 16
DH = 64
BS = 64
NB = SEQ // BS
E = 4
HID = 4096
TOK = B_CORE * SEQ

ENGINES = ("sync", "gpsimd", "tensor", "scalar", "vector")

_DEBUG_TAPS = None
_PHASE = 99
_PSB_ZERO = True
_ING_TC = 16
_ING_SKIP = ()
_QT_PARTS = 9
_HL_LVL = 9
_NH_LIM = NH
_PV_LVL = 9


class Prog:
    def __init__(self, nc, es):
        self.nc = nc
        self.es = es
        self.blocks = []
        self.cnt = {}
        self.sems = {}
        self.cur = None

    def sem(self, name):
        self.sems[name] = self.es.enter_context(self.nc.semaphore(name))
        self.cnt[name] = 0

    def new_block(self):
        self.cur = {e: [] for e in ENGINES}
        self.blocks.append(self.cur)

    DMA_SEMS = ("s_ld", "s_gld", "s_st", "s_gp", "s_lda", "s_cast")

    def emit(self, eng, fn, waits=(), inc=None, dma=False):
        w = []
        for x in waits:
            if x is None:
                continue
            if isinstance(x, list):
                w.extend(t for t in x if t is not None)
            else:
                w.append(x)
        w = [(t[0], t[1]) if len(t) == 3 else
             (t[0], self.cnt[t[0]] if t[0] in self.DMA_SEMS else t[1])
             for t in w]
        tok = None
        incs = ()
        if inc is not None:
            k = 16 if dma else 1
            self.cnt[inc] += k
            tok = (inc, self.cnt[inc])
            incs = ((inc, k),)
        self.cur[eng].append((fn, tuple(w), incs))
        return tok

    def replay(self):
        for blk_ops in self.blocks:
            with self.nc.Block() as blk:
                for eng in ENGINES:
                    ops = blk_ops[eng]
                    if not ops:
                        continue
                    def make(ops=ops):
                        def body(engobj):
                            seen = {}
                            for fn, waits, incs in ops:
                                for s, v in waits:
                                    if seen.get(s, -1) >= v:
                                        continue
                                    engobj.wait_ge(self.sems[s], v)
                                    seen[s] = v
                                instr = fn(engobj)
                                for s, k in incs:
                                    instr.then_inc(self.sems[s], k)
                        return body
                    getattr(blk, eng)(make())


class Res:
    __slots__ = ("ready", "free")

    def __init__(self):
        self.ready = None
        self.free = None


def wl(*items):
    out = []
    for it in items:
        if it is None:
            continue
        if isinstance(it, list):
            out.extend(x for x in it if x is not None)
        else:
            out.append(it)
    return out


def build_program(g_gather, g_keep):
    NKEEP = max(1, len(g_keep))
    nc = bass.Bass("TRN2", target_bir_lowering=False)

    x = nc.dram_tensor("x", [TOK, D], F32, kind="ExternalInput").ap()
    wq_h = nc.dram_tensor("wq", [D, D], F32, kind="ExternalInput").ap()
    wk_h = nc.dram_tensor("wk", [D, D], F32, kind="ExternalInput").ap()
    wv_h = nc.dram_tensor("wv", [D, D], F32, kind="ExternalInput").ap()
    wo_h = nc.dram_tensor("wo", [D, D], F32, kind="ExternalInput").ap()
    bq_h = nc.dram_tensor("bq", [D], F32, kind="ExternalInput").ap()
    bk_h = nc.dram_tensor("bk", [D], F32, kind="ExternalInput").ap()
    bv_h = nc.dram_tensor("bv", [D], F32, kind="ExternalInput").ap()
    bo_h = nc.dram_tensor("bo", [D], F32, kind="ExternalInput").ap()
    w1_h = nc.dram_tensor("w1", [E, D, HID], F32, kind="ExternalInput").ap()
    b1_h = nc.dram_tensor("b1", [E, HID], F32, kind="ExternalInput").ap()
    w2_h = nc.dram_tensor("w2", [E, HID, D], F32, kind="ExternalInput").ap()
    b2_h = nc.dram_tensor("b2", [E, D], F32, kind="ExternalInput").ap()
    wr_h = nc.dram_tensor("wr", [D, E], F32, kind="ExternalInput").ap()
    br_h = nc.dram_tensor("br", [E], F32, kind="ExternalInput").ap()
    y = nc.dram_tensor("y", [TOK, D], F32, kind="ExternalOutput").ap()

    sx32 = nc.dram_tensor("sx32", [B_CORE, 8, 128, SEQ], F32, kind="Internal").ap()
    sx1bf = nc.dram_tensor("sx1bf", [B_CORE, 8, 128, SEQ], BF16, kind="Internal").ap()
    sacc = nc.dram_tensor("sacc", [8, 128, TOK], F32, kind="Internal").ap()
    swk = nc.dram_tensor("swk", [128, 8 * D], BF16, kind="Internal").ap()
    swv = nc.dram_tensor("swv", [128, 8 * D], BF16, kind="Internal").ap()
    swq = nc.dram_tensor("swq", [128, 8 * D], BF16, kind="Internal").ap()
    swo = nc.dram_tensor("swo", [128, 8 * D], BF16, kind="Internal").ap()
    m_scr = nc.dram_tensor("m_scr", [4, TOK], F32, kind="Internal").ap()
    lg_scr = nc.dram_tensor("lg_scr", [4, TOK], F32, kind="Internal").ap()
    sw1bf = nc.dram_tensor("sw1bf", [E, D, HID], BF16, kind="Internal").ap()
    sxbf = nc.dram_tensor("sxbf", [B_CORE, 8, 128, SEQ], BF16, kind="Internal").ap()
    sw2bf = nc.dram_tensor("sw2bf", [E, HID, D], BF16, kind="Internal").ap()

    taps = {}
    if _DEBUG_TAPS is not None:
        for name, shape, dt in _DEBUG_TAPS:
            taps[name] = nc.dram_tensor("tap_" + name, shape, dt,
                                        kind="ExternalOutput").ap()

    es = ExitStack()
    P = Prog(nc, es)
    for s in ("s_ld", "s_gld", "s_st", "s_pe", "s_act", "s_dve", "s_gp",
              "s_gpc", "s_lda", "s_cast"):
        P.sem(s)

    def sb(es_, name, shape, dtype):
        return es_.enter_context(nc.sbuf_tensor(name, shape, dtype))

    def psum(name, shape, dtype):
        return es.enter_context(nc.psum_tensor(name, shape, dtype))

    ident_bf = sb(es, "ident_bf", [128, 128], BF16)
    ident32 = sb(es, "ident32", [128, 128], F32)
    ones_row_bf = sb(es, "ones_row_bf", [1, 128], BF16)
    ones_row_32 = sb(es, "ones_row_32", [1, 128], F32)
    ones_col_bf = sb(es, "ones_col_bf", [128, 1], BF16)
    b1c = [sb(es, f"b1c{e}", [128, 32], F32) for e in range(E)]
    csum_c = sb(es, "csum_c", [128, 8], F32)
    ccols = sb(es, "ccols", [128, 8], F32)
    strow = [sb(es, f"strow{i}", [1, 512], F32) for i in range(3)]
    rbf = [sb(es, f"rbf{i}", [128, 512], BF16) for i in range(2)]
    sqbf = [sb(es, f"sqbf{i}", [128, 512], BF16) for i in range(2)]
    rrbf = [Res(), Res()]
    rsq = [Res(), Res()]
    big_scr = sb(es, "big_scr", [128, 4096], F32)
    big_bf = big_scr.bitcast(BF16)
    hTc = sb(es, "hTc", [128, 32], BF16)
    zeros64 = sb(es, "zeros64", [64, 64], BF16)

    es_attn = ExitStack()
    bqc = sb(es_attn, "bqc", [128, 8], F32)
    bkc = sb(es_attn, "bkc", [128, 8], F32)
    boc = sb(es_attn, "boc", [128, 8], F32)
    bv_row = sb(es_attn, "bv_row", [1, D], BF16)
    wr_sb = sb(es_attn, "wr_sb", [128, 32], F32)
    br_c = sb(es_attn, "br_c", [4, 1], F32)
    ztmp = sb(es_attn, "ztmp", [128, 24], F32)
    lg_sb = [sb(es_attn, f"lg_sb{i}", [4, 512], F32) for i in range(2)]
    rlg = [Res(), Res()]
    brow = sb(es_attn, "brow", [32, 128], F32)
    rbrow = Res()

    psA = [psum("psA0", [128, 512], F32), psum("psA1", [128, 512], F32)]
    psB = [psum("psB0", [128, 1024], BF16), psum("psB1", [128, 1024], BF16)]
    psB32 = [t.bitcast(F32) for t in psB]
    psC = [psum("psC0", [128, 512], F32), psum("psC1", [128, 512], F32)]
    psD = [psum("psD0", [128, 512], F32), psum("psD1", [128, 512], F32)]
    rA = [Res(), Res()]
    rB = [Res(), Res()]
    rC = [Res(), Res()]
    rD = [Res(), Res()]

    def dma(eng, out_ap, in_ap, waits=(), slow=False, sem=None):
        if sem is None:
            sem = {"gpsimd": "s_gld", "scalar": "s_lda"}.get(eng, "s_ld")
        kw = {"allow_slow_non_contiguous": True} if slow else {}
        w = wl(*waits) if sem == "s_cast" else wl(*waits, (sem, P.cnt[sem]))
        return P.emit(eng, lambda E_, o=out_ap, i=in_ap, kw=kw:
                      E_.dma_start(o, i, **kw),
                      waits=w, inc=sem, dma=True)

    def dma_store(out_ap, in_ap, waits=(), eng="sync", accum=False):
        kw = {"accum_op": ALU.add} if accum else {}
        sem = "s_gp" if eng == "gpsimd" else "s_st"
        return P.emit(eng, lambda E_, o=out_ap, i=in_ap, kw=kw:
                      E_.dma_start(o, i, **kw),
                      waits=wl(*waits, (sem, P.cnt[sem])), inc=sem, dma=True)

    def mm(out_ap, lhsT, rhs, start, stop, waits=(), inc=None, transpose=False):
        def fn(E_, o=out_ap, l=lhsT, r=rhs, st=start, sp=stop, tr=transpose):
            if tr:
                return E_.matmul(o, l, r, start=True, stop=True,
                                 is_transpose=True)
            return E_.matmul(o, l, r, start=st, stop=sp)
        return P.emit("tensor", fn, waits=waits, inc=inc)

    def vop(fn, waits=()):
        return P.emit("vector", fn, waits=waits, inc="s_dve")

    def sop(fn, waits=()):
        return P.emit("scalar", fn, waits=waits, inc="s_act")

    def gop(fn, waits=(), inc="s_gpc"):
        return P.emit("gpsimd", fn, waits=waits, inc=inc)

    def tap(name, src_ap, waits):
        if name in taps:
            dma_store(taps[name], src_ap, waits=waits)

    P.new_block()

    t0 = gop(lambda g: g.memset(ident_bf[:, :], 0.0))
    t0 = gop(lambda g: g.affine_select(out=ident_bf[:, :], in_=ident_bf[:, :],
                                       compare_op=ALU.not_equal, fill=1.0, base=0,
                                       pattern=[[-1, 128]], channel_multiplier=1),
             waits=[t0])
    t1_ = gop(lambda g: g.memset(ident32[:, :], 0.0), waits=[t0])
    t1_ = gop(lambda g: g.affine_select(out=ident32[:, :], in_=ident32[:, :],
                                        compare_op=ALU.not_equal, fill=1.0, base=0,
                                        pattern=[[-1, 128]], channel_multiplier=1),
              waits=[t1_])
    t2_ = gop(lambda g: g.memset(ones_row_bf[:, :], 1.0), waits=[t1_])
    t2_ = gop(lambda g: g.memset(ones_row_32[:, :], 1.0), waits=[t2_])
    t2_ = gop(lambda g: g.memset(zeros64[:, :], 0.0), waits=[t2_])
    t_const = gop(lambda g: g.memset(ones_col_bf[:, :], 1.0), waits=[t2_])

    def load_cols(dst, src_row_ap, nrows, dst_free=None):
        t_ld = dma("sync", brow[0:nrows, :src_row_ap.shape[1]], src_row_ap,
                   waits=wl(rbrow.free))
        t_tr = mm(psD[0][0:src_row_ap.shape[1], 0:nrows],
                  brow[0:nrows, :src_row_ap.shape[1]],
                  ident32[0:nrows, 0:nrows], True, True,
                  waits=wl(t_ld, rD[0].free, t_const), transpose=True, inc="s_pe")
        rbrow.free = t_tr
        t_ev = vop(lambda v, d=dst, n=nrows, m=src_row_ap.shape[1]:
                   v.tensor_copy(d[0:m, 0:n], psD[0][0:m, 0:n]),
                   waits=wl(t_tr, dst_free))
        rD[0].free = t_ev
        return t_ev

    t_bq = load_cols(bqc, bq_h.rearrange("(c p) -> c p", p=128), 8)
    t_bk = load_cols(bkc, bk_h.rearrange("(c p) -> c p", p=128), 8)
    t_bo = load_cols(boc, bo_h.rearrange("(c p) -> c p", p=128), 8)
    t_bv = dma("gpsimd", bv_row[:, :], bv_h.rearrange("(o d) -> o d", o=1))
    t_b1 = [load_cols(b1c[e], b1_h[e, :].rearrange("(c p) -> c p", p=128), 32)
            for e in range(E)]
    b2tmp = sb(es_attn, "b2tmp", [128, 8], F32)
    t_cs = load_cols(csum_c, b2_h[0, :].rearrange("(c p) -> c p", p=128), 8)
    for e in range(1, E):
        t_e = load_cols(b2tmp, b2_h[e, :].rearrange("(c p) -> c p", p=128), 8,
                        dst_free=t_cs)
        t_cs = vop(lambda v: v.tensor_add(csum_c[:, :], csum_c[:, :], b2tmp[:, :]),
                   waits=[t_e, t_cs])
    t_csum_b2 = t_cs
    t_wr = None
    for dc in range(8):
        t_wr = dma("sync", wr_sb[:, 4 * dc:4 * dc + 4],
                   wr_h[128 * dc:128 * dc + 128, :])
    t_br = load_cols(br_c, br_h.rearrange("(o e) -> o e", o=1), 1)

    wkv_sb = sb(es_attn, "wkv", [128, 8 * D], BF16)
    rwkv = Res()
    xn = sb(es_attn, "xn", [128, D], F32)
    rxn = Res()
    wnat = xn.bitcast(BF16)[:, 0:D]
    rwnat = rxn

    def stage_weight(w_hbm, dst_sb, scratch, dst_free=None):
        last = None
        for r in range(8):
            buf = wnat
            t_ld = dma("gpsimd", buf[:, :], w_hbm[128 * r:128 * r + 128, :],
                       waits=wl(rwnat.free, t_const))
            for g in range(2):
                toks = []
                for j in range(4):
                    dc = 4 * g + j
                    toks.append(mm(psB[g][:, 128 * j:128 * j + 128],
                                   buf[:, 128 * dc:128 * dc + 128], ident_bf[:, :],
                                   j == 0, j == 3,
                                   waits=wl(t_ld, rB[g].free) if j == 0 else (),
                                   transpose=True, inc="s_pe" if j == 3 else None))
                def ev(v, g=g, r=r, dst=dst_sb):
                    dst_ap = dst[:, :].rearrange("p (dc c) -> p dc c", c=D)[
                        :, 4 * g:4 * g + 4, 128 * r:128 * r + 128]
                    return v.tensor_copy(dst_ap, psB[g][:, 0:512].rearrange(
                        "p (j c) -> p j c", c=128))
                t_ev = vop(ev, waits=wl(toks[-1],
                                        dst_free if (r == 0 and g == 0) else None))
                rB[g].free = t_ev
                last = t_ev
            rwnat.free = last
        if scratch is not None:
            return dma_store(scratch, dst_sb[:, :], waits=[last])
        return last

    t_sq = stage_weight(wq_h, wkv_sb, swq)
    t_so = stage_weight(wo_h, wkv_sb, swo, dst_free=t_sq)
    rwkv.ready = stage_weight(wk_h, wkv_sb, swk, dst_free=t_so)

    cur = {}
    rqx = Res()
    kT = sb(es_attn, "kT", [128, 8 * SEQ], BF16)
    v_all = sb(es_attn, "v_all", [128, 16 * D], BF16)
    kTg = sb(es_attn, "kTg", [128, 8 * 128], BF16)
    vg = sb(es_attn, "vg", [128, D], BF16)
    qTg = sb(es_attn, "qTg", [128, 512 * NKEEP], BF16)
    qT = sb(es_attn, "qT", [128, 8 * 512], BF16)
    ctxT = sb(es_attn, "ctxT", [128, 8 * 512], BF16)
    NPB = 4
    P_l = [sb(es_attn, f"P_l{i}", [128, 192], BF16) for i in range(NPB)]
    P_c = [sb(es_attn, f"P_c{i}", [128, 128], BF16) for i in range(NPB)]
    PT_sb = [sb(es_attn, f"PT{i}", [128, 384], BF16) for i in range(NPB)]
    rPl = [Res() for _ in range(NPB)]
    rPT = [Res() for _ in range(NPB)]
    xr_sb = [sb(es_attn, f"xr{i}", [128, 512], F32) for i in range(2)]
    rxr = [Res(), Res()]
    x1bf_out = sb(es_attn, "x1o", [128, 4096], BF16)
    rx1o = Res()
    xe32 = [sb(es_attn, "xe32_0", [128, 512], F32)]
    rxe = [Res()]
    ctmp = sb(es_attn, "ctmp", [128, 512], F32)

    rqT = Res()
    r32 = [big_scr[:, 512 * dc:512 * dc + 512] for dc in range(8)]
    P_row = big_bf[:, 0:2048]
    PT_row = big_bf[:, 2048:4096]
    rbig = Res()

    scale = 1.0 / 8.0
    SEQB = SEQ

    def ingest(b):
        for tc in range(_ING_TC):
            t_ld = dma("sync", xn[:, :],
                       x[SEQ * b + 128 * tc: SEQ * b + 128 * tc + 128, :],
                       waits=wl(rxn.free))
            for g in range(2):
                toks = []
                for j in range(4):
                    dc = 4 * g + j
                    toks.append(mm(psB32[g][:, 128 * j:128 * j + 128],
                                   xn[:, 128 * dc:128 * dc + 128],
                                   ident32[:, :], j == 0, j == 3,
                                   waits=wl(t_ld, rB[g].free) if j == 0 else (),
                                   transpose=True, inc="s_pe" if j == 3 else None))
                if "vec" in _ING_SKIP:
                    t_v = toks[-1]
                else:
                    t_v = vop(lambda v, g=g: v.tensor_copy(xe32[0][:, :],
                                                           psB32[g][:, 0:512]),
                              waits=wl(toks[-1], rxe[0].free))
                def v_ev(v, g=g, tc=tc, xTbf=cur["xTbf"]):
                    dst = xTbf[:, :].rearrange("p (dc t) -> p dc t", t=SEQB)[
                        :, 4 * g:4 * g + 4, 128 * tc:128 * tc + 128]
                    return v.tensor_copy(dst, psB32[g][:, 0:512].rearrange(
                        "p (j t) -> p j t", t=128))
                t_s = vop(v_ev, waits=[toks[-1]])
                rB[g].free = [t_v, t_s]
                if "store" in _ING_SKIP:
                    rxe[0].free = t_v
                else:
                    t_st = dma_store(
                        sx32[b, 4 * g:4 * g + 4, :, 128 * tc:128 * tc + 128]
                        .transpose([1, 0, 2]),
                        xe32[0][:, 0:512].rearrange("p (j t) -> p j t", t=128),
                        waits=[t_v])
                    rxe[0].free = t_st
            rxn.free = ("s_pe", P.cnt["s_pe"])
        dma_store(sxbf[b].transpose([1, 0, 2]),
                  cur["xTbf"][:, :].rearrange("p (dc t) -> p dc t", t=SEQB),
                  waits=[("s_dve", P.cnt["s_dve"])])
        return ("s_st", P.cnt["s_st"])

    def kv_proj(b):
        if b > 0:
            rwkv.ready = dma("gpsimd", wkv_sb[:, :], swk, waits=wl(rwkv.free))
        t_xready = [("s_act", P.cnt["s_act"]), ("s_dve", P.cnt["s_dve"])]
        for doutc in range(8):
            for tt in range(4):
                i = (doutc * 4 + tt) % 2
                tok = None
                for dc in range(8):
                    tok = mm(psA[i][:, :],
                             wkv_sb[:, D * dc + 128 * doutc: D * dc + 128 * doutc + 128],
                             cur["xTbf"][:, SEQB * dc + 512 * tt: SEQB * dc + 512 * tt + 512],
                             dc == 0, dc == 7,
                             waits=wl(rA[i].free, rwkv.ready, t_bk, t_xready)
                             if dc == 0 else (),
                             inc="s_pe" if dc == 7 else None)
                t_ev = sop(lambda s, doutc=doutc, tt=tt, i=i: s.activation(
                    kT[:, SEQB * doutc + 512 * tt: SEQB * doutc + 512 * tt + 512],
                    psA[i][:, :], AF.Identity, bias=bkc[:, doutc:doutc + 1], scale=1.0),
                    waits=[tok])
                rA[i].free = t_ev
        rwkv.free = ("s_pe", P.cnt["s_pe"])
        if b == 0:
            rwkv.ready = stage_weight(wv_h, wkv_sb, swv)
        else:
            rwkv.ready = dma("gpsimd", wkv_sb[:, :], swv, waits=wl(rwkv.free))
        for tc in range(16):
            for half in range(2):
                i = (tc * 2 + half) % 2
                tok = None
                for dc in range(8):
                    tok = mm(psC[i][:, :],
                             cur["xTbf"][:, SEQB * dc + 128 * tc: SEQB * dc + 128 * tc + 128],
                             wkv_sb[:, D * dc + 512 * half: D * dc + 512 * half + 512],
                             dc == 0, False,
                             waits=wl(rC[i].free, rwkv.ready, t_bv, t_xready)
                             if dc == 0 else ())
                tok = mm(psC[i][:, :], ones_row_bf[:, :],
                         bv_row[:, 512 * half: 512 * half + 512],
                         False, True, inc="s_pe")
                t_ev = sop(lambda s, tc=tc, half=half, i=i: s.activation(
                    v_all[:, D * tc + 512 * half: D * tc + 512 * half + 512],
                    psC[i][:, :], AF.Copy, bias=0.0, scale=1.0), waits=[tok])
                rC[i].free = t_ev
        rwkv.free = ("s_pe", P.cnt["s_pe"])
        t_kv = ("s_act", P.cnt["s_act"])
        for j, g in enumerate(g_gather):
            for dd in range(8):
                vop(lambda v, j=j, g=g, dd=dd: v.tensor_copy(
                    kTg[:, 128 * dd + 64 * j: 128 * dd + 64 * j + 64],
                    kT[:, SEQB * dd + 64 * g: SEQB * dd + 64 * g + 64]),
                    waits=[t_kv])
            vop(lambda v, j=j, g=g: v.tensor_copy(
                vg[64 * j:64 * j + 64, :],
                v_all[64 * (g % 2):64 * (g % 2) + 64,
                      D * (g // 2):D * (g // 2) + D]), waits=[t_kv])
        return ("s_dve", P.cnt["s_dve"])

    def layer_norm(rt_ready, rtiles, TW=512):
        tok0 = tok1 = None
        for dc in range(8):
            i = dc % 2
            t_c = sop(lambda s, dc=dc, i=i: s.activation(
                rbf[i][:, 0:TW], rtiles[dc], AF.Copy, bias=0.0, scale=1.0),
                waits=wl(rt_ready[dc], rrbf[i].free))
            t_q = sop(lambda s, dc=dc, i=i: s.activation(
                sqbf[i][:, 0:TW], rtiles[dc], AF.Square, bias=0.0, scale=1.0),
                waits=wl(rsq[i].free))
            tok0 = mm(psD[0][0:1, 0:TW], ones_col_bf[:, :], rbf[i][:, 0:TW],
                      dc == 0, dc == 7,
                      waits=wl(t_c, rD[0].free) if dc == 0 else [t_c],
                      inc="s_pe")
            rrbf[i].free = wl(tok0, t_c)
            tok1 = mm(psD[1][0:1, 0:TW], ones_col_bf[:, :], sqbf[i][:, 0:TW],
                      dc == 0, dc == 7,
                      waits=wl(t_q, rD[1].free) if dc == 0 else [t_q],
                      inc="s_pe")
            rsq[i].free = wl(tok1, t_q)
        t1 = vop(lambda v: v.tensor_scalar(strow[0][:, 0:TW], psD[0][0:1, 0:TW],
                                           1.0 / D, None, ALU.mult), waits=[tok0])
        t2 = vop(lambda v: v.tensor_scalar(strow[1][:, 0:TW], psD[1][0:1, 0:TW],
                                           1.0 / D, 1e-5, ALU.mult, ALU.add),
                 waits=[tok1])
        t3 = vop(lambda v: v.tensor_mul(strow[2][:, 0:TW], strow[0][:, 0:TW],
                                        strow[0][:, 0:TW]), waits=[t1])
        t4 = vop(lambda v: v.tensor_sub(strow[1][:, 0:TW], strow[1][:, 0:TW],
                                        strow[2][:, 0:TW]), waits=[t2, t3])
        t5 = sop(lambda s: s.activation(strow[2][:, 0:TW], strow[1][:, 0:TW], AF.Ln,
                                        bias=0.0, scale=1.0), waits=[t4])
        t6 = sop(lambda s: s.activation(strow[2][:, 0:TW], strow[2][:, 0:TW], AF.Exp,
                                        bias=0.0, scale=-0.5), waits=[t5])
        t7 = vop(lambda v: v.tensor_mul(strow[0][:, 0:TW], strow[0][:, 0:TW],
                                        strow[2][:, 0:TW]), waits=[t6])
        tb0 = mm(psD[0][:, 0:TW], ones_row_32[:, :], strow[2][:, 0:TW], True, True,
                 waits=[t6, t1], inc="s_pe")
        tb1 = mm(psD[1][:, 0:TW], ones_row_32[:, :], strow[0][:, 0:TW], True, True,
                 waits=[t7, t2], inc="s_pe")
        done = []
        for dc in range(8):
            ta = vop(lambda v, dc=dc: v.tensor_mul(rtiles[dc], rtiles[dc],
                                                   psD[0][:, 0:TW]), waits=[tb0])
            tb = vop(lambda v, dc=dc: v.tensor_sub(rtiles[dc], rtiles[dc],
                                                   psD[1][:, 0:TW]), waits=[tb1, ta])
            done.append(tb)
        rD[0].free = done[-1]
        rD[1].free = done[-1]
        return done

    def attention_qtile(b, qt, t_ing_st):
        t_prev_st = ("s_st", P.cnt["s_st"])
        t_prev_gp = ("s_gp", P.cnt["s_gp"])
        t_prev_v = rqT.free
        qx = cur["qx"]
        t_qx = cur.pop("t_qx_next", None)
        if t_qx is None:
            t_qx = dma("scalar",
                       qx[:, :].rearrange("p (dc t) -> p dc t", t=512),
                       sxbf[b, :, :, 512 * qt:512 * qt + 512]
                       .transpose([1, 0, 2]),
                       waits=wl(rqx.free, t_ing_st))
        rqx.free = None
        for doutc in range(8):
            i = doutc % 2
            tok = None
            for dc in range(8):
                tok = mm(psA[i][:, :],
                         cur["wqt"][:, D * dc + 128 * doutc: D * dc + 128 * doutc + 128],
                         qx[:, 512 * dc:512 * dc + 512],
                         dc == 0, dc == 7,
                         waits=wl(rA[i].free, t_bq, cur["t_wq"], t_prev_v, t_qx)
                         if dc == 0 else (),
                         inc="s_pe" if dc == 7 else None)
            t_ev = sop(lambda s, doutc=doutc, i=i: s.activation(
                qT[:, 512 * doutc:512 * doutc + 512], psA[i][:, :], AF.Identity,
                bias=bqc[:, doutc:doutc + 1], scale=1.0), waits=[tok])
            rA[i].free = t_ev
        rqx.free = ("s_pe", P.cnt["s_pe"])
        cast_piece(2)
        if qt < 3:
            cur["t_qx_next"] = dma(
                "scalar", qx[:, :].rearrange("p (dc t) -> p dc t", t=512),
                sxbf[b, :, :, 512 * (qt + 1):512 * (qt + 1) + 512]
                .transpose([1, 0, 2]), waits=wl(rqx.free, t_ing_st))
            rqx.free = None
        t_q_done = ("s_act", P.cnt["s_act"])
        if _QT_PARTS < 2:
            return ("s_act", P.cnt["s_act"])

        keep_here = [g for g in g_keep if g // 8 == qt]
        for g in keep_here:
            j = g_keep.index(g)
            for dd in range(8):
                vop(lambda v, j=j, g=g, dd=dd: v.tensor_copy(
                    qTg[:, 512 * j + 64 * dd: 512 * j + 64 * dd + 64],
                    qT[:, 512 * dd + 64 * (g % 8): 512 * dd + 64 * (g % 8) + 64]),
                    waits=[t_q_done])
        t_qg = ("s_dve", P.cnt["s_dve"])
        rqT.free = t_qg
        if qt == 0 and _PSB_ZERO:
            for g2 in range(2):
                mm(psB[g2][64:128, 256:320], zeros64[:, :],
                   ident_bf[0:64, 0:64], True, True,
                   waits=wl(rB[g2].free, t_const), transpose=True)
                t_z0 = mm(psB[g2][0:64, 320:384], zeros64[:, :],
                          ident_bf[0:64, 0:64], True, True, transpose=True,
                          inc="s_pe")
                rB[g2].free = t_z0

        SBk = [psA[0], psA[1], psD[0], psD[1]]
        rS = [rA[0], rA[1], rD[0], rD[1]]
        t_ev_prev_guard = None
        pend = None

        def emit_tail(st):
            ia, ib, h, qc, qc4, pb, t_nl, t_nc = st
            pt2 = it_pt = ia % 2
            w0 = wl(rB[pt2].free, t_nl)
            mm(psB[pt2][:, 0:128], P_l[ib][:, 64:192], ident_bf[:, :],
               True, False, waits=w0, transpose=True)
            mm(psB[pt2][0:64, 256:320], P_l[ib][64:128, 0:64],
               ident_bf[64:128, 64:128], False, False, transpose=True)
            mm(psB[pt2][64:128, 320:384], P_l[ib][0:64, 0:64],
               ident_bf[0:64, 0:64], False, False, transpose=True)
            t_tr = mm(psB[pt2][:, 128:256], P_c[ib][:, :], ident_bf[:, :],
                      False, True, waits=[t_nc], transpose=True, inc="s_pe")
            rPl[ib].free = t_tr
            t_pte = vop(lambda v, pt2=pt2, ib=ib: v.tensor_copy(
                PT_sb[ib][:, :], psB[pt2][:, 0:384]),
                waits=wl(t_tr, rPT[ib].free))
            rB[pt2].free = t_pte
            ocol = 128 * qc4
            blo = (2 * qc - 1) % NB
            bhi = (2 * qc + 2) % NB

            def vsl(bk, hx=h):
                return v_all[64 * (bk % 2):64 * (bk % 2) + 64,
                             D * (bk // 2) + 64 * hx: D * (bk // 2) + 64 * hx + 64]
            firstX = (qc4 == 0)
            wX = wl(t_pte, rC[0].free if (firstX and h % 2 == 0) else None)
            wY = wl(t_pte, rC[1].free if (firstX and h % 2 == 0) else None)
            mm(psC[0][pb:pb + 64, ocol:ocol + 128], vsl(2 * qc),
               PT_sb[ib][0:64, 0:128], firstX, False, waits=wX)
            mm(psC[0][pb:pb + 64, ocol + 64:ocol + 128], vsl(bhi),
               PT_sb[ib][0:64, 256:320], False, False)
            mm(psC[0][pb:pb + 64, ocol:ocol + 128],
               vg[:, 64 * h:64 * h + 64], PT_sb[ib][:, 128:256],
               False, (qc4 == 3), inc="s_pe")
            t_pv = mm(psC[1][pb:pb + 64, ocol:ocol + 128], vsl(2 * qc + 1),
                      PT_sb[ib][64:128, 0:128], firstX, False, waits=wY)
            t_pv = mm(psC[1][pb:pb + 64, ocol:ocol + 64], vsl(blo),
                      PT_sb[ib][64:128, 320:384], False, (qc4 == 3),
                      inc="s_pe")
            rPT[ib].free = t_pv

        for h in range(_NH_LIM):
            hh = h // 2
            pb = 64 * (h % 2)
            for qc4 in range(4):
                qc = 4 * qt + qc4
                it = h * 4 + qc4
                ia = it % 4
                ib = it % 4
                Sb = SBk[ia]
                qap = qT[pb:pb + 64,
                         512 * hh + 128 * qc4: 512 * hh + 128 * qc4 + 128]
                w0 = wl(rS[ia].free, t_q_done, t_qg)
                mm(Sb[:, 64:192], qap,
                   kT[pb:pb + 64, SEQB * hh + 128 * qc: SEQB * hh + 128 * qc + 128],
                   True, False, waits=w0)
                lo = (64 * (2 * qc - 1)) % SEQ
                mm(Sb[0:64, 0:64], qap[:, 0:64],
                   kT[pb:pb + 64, SEQB * hh + lo: SEQB * hh + lo + 64],
                   False, False)
                hi = (64 * (2 * qc + 2)) % SEQ
                mm(Sb[64:128, 0:64], qap[:, 64:128],
                   kT[pb:pb + 64, SEQB * hh + hi: SEQB * hh + hi + 64],
                   False, False)
                t_s = mm(Sb[:, 192:320], qap,
                         kTg[pb:pb + 64, 128 * hh:128 * hh + 128],
                         False, True, inc="s_pe")
                if pend is not None and _HL_LVL >= 4:
                    emit_tail(pend)
                    pend = None
                zc = 4 * ib
                if _HL_LVL < 2:
                    rS[ia].free = vop(lambda v, ia=ia: v.tensor_copy(
                        ztmp[:, 0:2], SBk[ia][:, 0:2]), waits=[t_s])
                    continue
                t_el = sop(lambda s, ia=ia, ib=ib, zc=zc: s.activation(
                    P_l[ib][:, :], SBk[ia][:, 0:192], AF.Exp, bias=0.0,
                    scale=scale, accum_out=ztmp[:, zc:zc + 1]),
                    waits=wl(t_s, rPl[ib].free))
                t_ec = sop(lambda s, ia=ia, ib=ib, zc=zc: s.activation(
                    P_c[ib][:, :], SBk[ia][:, 192:320], AF.Exp, bias=0.0,
                    scale=scale, accum_out=ztmp[:, zc + 1:zc + 2]))
                rS[ia].free = t_ec
                t_z2 = vop(lambda v, zc=zc: v.reciprocal(
                    ztmp[:, zc + 2:zc + 4], ztmp[:, zc:zc + 2]), waits=[t_ec])
                t_nl = vop(lambda v, ib=ib, zc=zc: v.tensor_scalar(
                    P_l[ib][:, :], P_l[ib][:, :], ztmp[:, zc + 2:zc + 3], None,
                    ALU.mult), waits=[t_z2])
                t_nc = vop(lambda v, ib=ib, zc=zc: v.tensor_scalar(
                    P_c[ib][:, :], P_c[ib][:, :], ztmp[:, zc + 3:zc + 4], None,
                    ALU.mult), waits=[t_z2])
                if _HL_LVL < 3:
                    rPl[ib].free = t_nl
                    rS[ia].free = t_nc
                    continue
                if _HL_LVL < 4:
                    rPl[ib].free = t_nl
                    rPT[ib].free = t_nc
                    continue
                pend = (ia, ib, h, qc, qc4, pb, t_nl, t_nc)
            if h % 2 == 1 and _HL_LVL >= 4:
                if pend is not None:
                    emit_tail(pend)
                    pend = None
                t_cx = vop(lambda v: v.tensor_copy(ctmp[:, :], psC[0][:, :]),
                           waits=[("s_pe", P.cnt["s_pe"])])
                t_ev = vop(lambda v, hh=hh: v.tensor_add(
                    ctxT[:, 512 * hh:512 * hh + 512], ctmp[:, :],
                    psC[1][:, :]), waits=[t_ev_prev_guard, t_cx])
                rC[0].free = wl(t_cx, t_ev)
                rC[1].free = t_ev
                t_ev_prev_guard = t_ev
        t_ctx = ("s_dve", P.cnt["s_dve"])
        if _QT_PARTS < 3:
            return t_ctx

        for g in keep_here:
            j = g_keep.index(g)
            for hp in range(8):
                for kc in range(4):
                    ia = kc % 2
                    tok = None
                    for hx, pb2 in ((2 * hp, 0), (2 * hp + 1, 64)):
                        tok = mm(
                            psA[ia][pb2:pb2 + 64, :],
                            qTg[pb2:pb2 + 64,
                                512 * j + 64 * hp:512 * j + 64 * hp + 64],
                            kT[pb2:pb2 + 64,
                               SEQB * hp + 512 * kc: SEQB * hp + 512 * kc + 512],
                            True, True,
                            waits=wl(rA[ia].free, t_ctx) if pb2 == 0 else (),
                            inc="s_pe" if pb2 == 64 else None)
                    t_e = sop(lambda s, ia=ia, kc=kc: s.activation(
                        P_row[:, 512 * kc:512 * kc + 512], psA[ia][:, :], AF.Exp,
                        bias=0.0, scale=scale,
                        accum_out=ztmp[:, 16 + kc:17 + kc]),
                        waits=wl(tok, rbig.free if kc == 0 else None))
                    rA[ia].free = t_e
                t_zs = vop(lambda v: v.tensor_reduce(
                    ztmp[:, 20:21], ztmp[:, 16:20], AX.X, ALU.add),
                    waits=[("s_act", P.cnt["s_act"])])
                t_zr = vop(lambda v: v.reciprocal(ztmp[:, 20:21], ztmp[:, 20:21]),
                           waits=[t_zs])
                t_n = vop(lambda v: v.tensor_scalar(
                    P_row[:, :], P_row[:, :], ztmp[:, 20:21], None, ALU.mult),
                    waits=[t_zr])
                ev_toks = []
                for g2 in range(2):
                    w0 = wl(rB[g2].free, t_n)
                    tok = None
                    for i8 in range(8):
                        ci = 8 * g2 + i8
                        tok = mm(psB[g2][:, 128 * i8:128 * i8 + 128],
                                 P_row[:, 128 * ci:128 * ci + 128], ident_bf[:, :],
                                 i8 == 0, i8 == 7,
                                 waits=w0 if i8 == 0 else (), transpose=True,
                                 inc="s_pe" if i8 == 7 else None)
                    t_ev2 = vop(lambda v, g2=g2: v.tensor_copy(
                        PT_row[:, 1024 * g2:1024 * g2 + 1024], psB[g2][:, :]),
                        waits=[tok])
                    rB[g2].free = t_ev2
                    ev_toks.append(t_ev2)
                rbig.free = ev_toks[-1]
                w0 = wl(rD[0].free, ev_toks[0], ev_toks[1])
                tok = None
                for hx, ob in ((2 * hp, 0), (2 * hp + 1, 64)):
                    for tc in range(16):
                        firstpv = (tc == 0)
                        lastpv = (tc == 15)
                        tok = mm(psD[0][ob:ob + 64, 0:64],
                                 v_all[:, D * tc + 64 * hx: D * tc + 64 * hx + 64],
                                 PT_row[:, 128 * tc + ob:128 * tc + ob + 64],
                                 firstpv, lastpv,
                                 waits=w0 if (firstpv and ob == 0) else (),
                                 inc="s_pe" if (lastpv and ob == 64) else None)
                t_ov = vop(lambda v, hp=hp, g=g: v.tensor_copy(
                    ctxT[:, 512 * hp + 64 * (g % 8): 512 * hp + 64 * (g % 8) + 64],
                    psD[0][:, 0:64]), waits=[tok])
                rD[0].free = t_ov
        t_ctx = ("s_dve", P.cnt["s_dve"])
        if _QT_PARTS < 4:
            return t_ctx

        rt_ready = []
        for doutc in range(8):
            i = doutc % 2
            t_x = dma("sync", xr_sb[i][:, :],
                      sx32[b, doutc, :, 512 * qt:512 * qt + 512],
                      waits=wl(rxr[i].free, t_ing_st))
            tok = None
            for dc in range(8):
                tok = mm(psC[i][:, :],
                         cur["wot"][:, D * dc + 128 * doutc: D * dc + 128 * doutc + 128],
                         ctxT[:, 512 * dc:512 * dc + 512],
                         dc == 0, dc == 7,
                         waits=wl(rC[i].free, t_ctx, cur["t_wo"], t_bo)
                         if dc == 0 else (),
                         inc="s_pe" if dc == 7 else None)
            t_r = vop(lambda v, doutc=doutc, i=i: v.scalar_tensor_tensor(
                r32[doutc], psC[i][:, :], boc[:, doutc:doutc + 1], xr_sb[i][:, :],
                ALU.add, ALU.add),
                waits=[tok, t_x, t_prev_st, t_prev_gp, rbig.free])
            rC[i].free = t_r
            rxr[i].free = t_r
            rt_ready.append(t_r)
        rbig.free = None
        if _QT_PARTS < 5:
            return ("s_dve", P.cnt["s_dve"])

        ln_done = layer_norm(rt_ready, r32)

        if _QT_PARTS >= 7:
            il = (4 * b + qt) % 2
            tok = None
            for dc in range(8):
                tok = mm(psD[0][0:4, 0:512], wr_sb[:, 4 * dc:4 * dc + 4],
                         r32[dc], dc == 0, dc == 7,
                         waits=wl(rD[0].free, ln_done[dc], t_wr, t_br)
                         if dc == 0 else [ln_done[dc]],
                         inc="s_pe" if dc == 7 else None)
            t_lg = vop(lambda v, il=il: v.tensor_scalar(
                lg_sb[il][:, :], psD[0][0:4, 0:512], br_c[:, :], None,
                ALU.add), waits=wl(tok, rlg[il].free))
            rD[0].free = t_lg
            rlg[il].free = dma_store(
                lg_scr[0:4, SEQ * b + 512 * qt: SEQ * b + 512 * qt + 512],
                lg_sb[il][:, :], waits=[t_lg])

        t_c = None
        for dc in range(8):
            t_c = sop(lambda s, dc=dc: s.activation(
                x1bf_out[:, 512 * dc:512 * dc + 512], r32[dc], AF.Copy,
                bias=0.0, scale=1.0),
                waits=wl(ln_done[dc], rx1o.free if dc == 0 else None))
        rx1o.free = dma_store(
            sx1bf[b, :, :, 512 * qt:512 * qt + 512].transpose([1, 0, 2]),
            x1bf_out[:, :].rearrange("p (dc t) -> p dc t", t=512),
            waits=[t_c])
        dma_store(sx32[b, :, :, 512 * qt:512 * qt + 512].transpose([1, 0, 2]),
                  big_scr[:, :].rearrange("p (dc t) -> p dc t", t=512),
                  waits=[("s_dve", P.cnt["s_dve"])])
        rbig.free = ("s_st", P.cnt["s_st"])
        return ("s_dve", P.cnt["s_dve"])

    cast_jobs = []
    for e in range(E):
        for h in range(2):
            cast_jobs.append((sw1bf[e, 512 * h:512 * h + 512, :],
                              w1_h[e, 512 * h:512 * h + 512, :]))
            cast_jobs.append((sw2bf[e, 2048 * h:2048 * h + 2048, :],
                              w2_h[e, 2048 * h:2048 * h + 2048, :]))

    def cast_piece(n=1):
        for _ in range(n):
            if cast_jobs:
                dst, srcp = cast_jobs.pop(0)
                dma("gpsimd", dst, srcp, sem="s_cast")

    for b in range(B_CORE):
        if b * 5 >= _PHASE:
            break
        es_x = ExitStack()
        cur["xTbf"] = sb(es_x, f"xTbf{b}", [128, 8 * SEQ], BF16)
        t_ing_st = ingest(b)
        if _PHASE < b * 5 + 2:
            es_x.close()
            break
        kv_proj(b)
        cast_piece(2)
        if b == 0:
            tap("xTbf", cur["xTbf"][:, :], [("s_act", P.cnt["s_act"])])
            tap("kT", kT[:, :], [("s_act", P.cnt["s_act"])])
            tap("v_all", v_all[:, :], [("s_act", P.cnt["s_act"])])
        es_x.close()
        if _PHASE < b * 5 + 3:
            break
        es_qo = ExitStack()
        cur["wqt"] = sb(es_qo, f"wqt{b}", [128, 8 * D], BF16)
        cur["wot"] = sb(es_qo, f"wot{b}", [128, 8 * D], BF16)
        cur["qx"] = sb(es_qo, f"qx{b}", [128, 8 * 512], BF16)
        t_pe_now = ("s_pe", P.cnt["s_pe"])
        t_st_now = ("s_st", P.cnt["s_st"])
        cur["t_wq"] = dma("gpsimd", cur["wqt"][:, :], swq,
                          waits=[t_pe_now, t_st_now, t_sq])
        cur["t_wo"] = dma("gpsimd", cur["wot"][:, :], swo,
                          waits=[t_so, t_st_now])
        rqx.free = None
        for qt in range(4):
            attention_qtile(b, qt, t_ing_st)
            if b == 0 and qt == 0:
                tap("qT", qT[:, :], [("s_act", P.cnt["s_act"])])
                tap("ctxT", ctxT[:, :], [("s_dve", P.cnt["s_dve"])])
            if _PHASE < b * 5 + 4 and qt == 0:
                break
        es_qo.close()

    cast_piece(len(cast_jobs))
    t_attn = {s: ("" + s, P.cnt[s]) for s in
              ("s_dve", "s_act", "s_pe", "s_st", "s_gld", "s_ld", "s_gp",
               "s_lda", "s_cast")}
    t_attn_all = list(t_attn.values())

    es_attn.close()

    P.new_block()
    for r in (rA + rC + rD):
        r.free = list(t_attn_all)
    rbig.free = list(t_attn_all)
    for r in (rrbf + rsq):
        r.free = None

    es_fw = ExitStack()
    w1sb = sb(es_fw, "w1sb", [128, 8 * HID], BF16)
    w2sb = sb(es_fw, "w2sb", [128, 32 * D], BF16)
    rw1 = Res()
    rw2 = Res()

    def load_w1(e, waits):
        tk = None
        w = wl(waits, ("s_cast", P.cnt["s_cast"]))
        for dc in range(8):
            tk = dma("sync", w1sb[:, HID * dc:HID * dc + HID],
                     sw1bf[e, 128 * dc:128 * dc + 128, :],
                     waits=w if dc == 0 else ())
        return tk

    def load_w2(e, waits):
        tk = None
        w = wl(waits, ("s_cast", P.cnt["s_cast"]))
        for r in range(8):
            dst = w2sb[:, :].rearrange("p (g d) -> p g d", d=D)[
                :, 4 * r:4 * r + 4, :]
            srcv = sw2bf[e, 512 * r:512 * r + 512, :].rearrange(
                "(g p) d -> p g d", p=128)
            tk = dma("scalar", dst, srcv, waits=w if r == 0 else ())
        return tk

    def gate_pass():
        lg_all = big_scr[0:4, 0:4096]
        m4q = sb(es_fw, "m4qf", [4, 512], F32)
        gt = [sb(es_fw, f"gtf{i}", [128, 4], F32) for i in range(4)]
        gcol = [sb(es_fw, f"gcf{i}", [128, 1], F32) for i in range(3)]
        rm4 = Res()
        t_lg = dma("scalar", lg_all, lg_scr[0:4, :],
                   waits=wl(t_attn["s_st"], t_attn["s_gp"], t_attn["s_pe"]))
        for g in range(8):
            for i4 in range(4):
                pb = psB32[(4 * g + i4) % 2]
                t_tr = mm(pb[0:128, 0:4],
                          lg_all[0:4, 512 * g + 128 * i4:
                                 512 * g + 128 * i4 + 128],
                          ident32[0:4, 0:4], True, True,
                          waits=wl(t_lg, rB[(4 * g + i4) % 2].free),
                          transpose=True, inc="s_pe")
                t_l = vop(lambda v, pb=pb: v.tensor_copy(
                    gt[0][:, :], pb[0:128, 0:4]), waits=[t_tr])
                t_m1 = vop(lambda v: v.reduce_max(gcol[0][:, :], gt[0][:, :],
                                                  AX.X, negate=True),
                           waits=[t_l])
                t_msk = vop(lambda v: v.tensor_scalar(gt[1][:, :], gt[0][:, :],
                                                      gcol[0][:, :], 0.0,
                                                      ALU.add, ALU.is_ge),
                            waits=[t_m1])
                t_t = vop(lambda v: v.scalar_tensor_tensor(
                    gt[2][:, :], gt[1][:, :], -1e30, gt[0][:, :],
                    ALU.mult, ALU.add), waits=[t_msk])
                t_m2 = vop(lambda v: v.reduce_max(gcol[1][:, :], gt[2][:, :],
                                                  AX.X, negate=True),
                           waits=[t_t])
                t_ind = vop(lambda v: v.tensor_scalar(gt[3][:, :], gt[0][:, :],
                                                      gcol[1][:, :], 0.0,
                                                      ALU.add, ALU.is_ge),
                            waits=[t_m2])
                t_e = sop(lambda s: s.activation(gt[2][:, :], gt[0][:, :],
                                                 AF.Exp, bias=gcol[0][:, :],
                                                 scale=1.0), waits=[t_m2])
                t_en = vop(lambda v: v.tensor_mul(gt[2][:, :], gt[2][:, :],
                                                  gt[3][:, :]),
                           waits=[t_e, t_ind])
                t_zz = vop(lambda v: v.reduce_sum(gcol[2][:, :], gt[2][:, :],
                                                  AX.X), waits=[t_en])
                t_w = vop(lambda v: v.reciprocal(gcol[2][:, :], gcol[2][:, :]),
                          waits=[t_zz])
                t_m = vop(lambda v: v.tensor_scalar(gt[2][:, :], gt[2][:, :],
                                                    gcol[2][:, :], None,
                                                    ALU.mult), waits=[t_w])
                t_tr2 = mm(pb[0:4, 0:128], gt[2][:, :], ident32[:, :], True,
                           True, waits=[t_m], transpose=True, inc="s_pe")
                t_cp = vop(lambda v, pb=pb, i4=i4: v.tensor_copy(
                    m4q[0:4, 128 * i4:128 * i4 + 128], pb[0:4, 0:128]),
                    waits=wl(t_tr2, rm4.free if i4 == 0 else None))
                rB[(4 * g + i4) % 2].free = t_cp
            rm4.free = dma_store(m_scr[0:4, 512 * g:512 * g + 512],
                                 m4q[0:4, :], waits=[t_cp])
        return [("s_st", P.cnt["s_st"]), ("s_dve", P.cnt["s_dve"]),
                ("s_pe", P.cnt["s_pe"])]

    if _PHASE >= 11:
        rw1.ready = load_w1(0, t_attn_all)
        rw2.ready = load_w2(0, t_attn_all)
        t_gates = gate_pass()
        rbig.free = wl(rbig.free, t_gates)

    def c_pass(e):
        t_g = sop(lambda s, e=e: s.activation(hTc[:, :], b1c[e][:, :], AF.Gelu,
                                              bias=0.0, scale=1.0),
                  waits=wl(t_b1[e], ("s_pe", P.cnt["s_pe"])))
        crow = strow[0]
        t_cpass = None
        for half in range(2):
            tok = None
            for hc in range(32):
                tok = mm(psD[1][0:1, 0:512], hTc[:, hc:hc + 1],
                         w2sb[:, D * hc + 512 * half: D * hc + 512 * half + 512],
                         hc == 0, hc == 31,
                         waits=wl(rD[1].free, t_g, rw2.ready) if hc == 0 else (),
                         inc="s_pe" if hc == 31 else None)
            t_cr = vop(lambda v: v.tensor_copy(crow[:, :], psD[1][0:1, 0:512]),
                       waits=[tok])
            rD[1].free = t_cr
            for i4 in range(4):
                dc = 4 * half + i4
                t_tr = mm(psD[1][0:128, 0:1], crow[0:1, 128 * i4:128 * i4 + 128],
                          ident32[0:1, 0:1], True, True,
                          waits=wl(t_cr, rD[1].free), transpose=True, inc="s_pe")
                t_cc = vop(lambda v, dc=dc: v.tensor_copy(
                    ccols[:, dc:dc + 1], psD[1][0:128, 0:1]), waits=[t_tr])
                rD[1].free = t_cc
                t_cpass = vop(lambda v, dc=dc: v.tensor_add(
                    csum_c[:, dc:dc + 1], csum_c[:, dc:dc + 1],
                    ccols[:, dc:dc + 1]), waits=wl(t_cc, t_csum_b2))
        return t_cpass

    def ffn_expert(e, TW, bufs):
        x1t, hT, mb_sb, mrow_t, contrib, x1_ch, out_sb = bufs
        rx1t, rmb, rmrow = ffn_res["rx1t"], ffn_res["rmb"], ffn_res["rmrow"]
        rctr, racc, rx1c = ffn_res["rctr"], ffn_res["racc"], ffn_res["rx1c"]
        rout = ffn_res["rout"]
        t_cpass = None
        last_e = (e == E - 1)
        n_tt = TOK // TW
        ntc = TW // 128
        pend_ln = []
        for tt in range(n_tt):
            b_of = (TW * tt) // SEQ
            c_of = (TW * tt) % SEQ
            t_x1 = dma("sync", x1t[:, :].rearrange("p (dc t) -> p dc t", t=TW),
                       sx1bf[b_of, :, :, c_of:c_of + TW].transpose([1, 0, 2]),
                       waits=wl(rx1t.free, t_attn["s_st"]))
            t_mr = dma("sync", mrow_t[:, :],
                       m_scr[e:e + 1, TW * tt:TW * tt + TW],
                       waits=wl(rmrow.free, t_attn["s_st"]))
            t_mb = mm(psD[0][:, 0:TW], ones_row_32[:, :], mrow_t[:, :],
                      True, True, waits=wl(rD[0].free, t_mr), inc="s_pe")
            rmrow.free = t_mb
            t_mbe = vop(lambda v: v.tensor_copy(mb_sb[:, :], psD[0][:, 0:TW]),
                        waits=wl(t_mb, rmb.free))
            rD[0].free = t_mbe
            last_tile = (tt == n_tt - 1)
            gl_last = None
            for hc in range(32):
                i = hc % 2
                tok = None
                for dc in range(8):
                    tok = mm(psA[i][:, 0:TW],
                             w1sb[:, HID * dc + 128 * hc: HID * dc + 128 * hc + 128],
                             x1t[:, TW * dc:TW * dc + TW],
                             dc == 0, dc == 7,
                             waits=wl(rA[i].free, t_x1, rw1.ready)
                             if dc == 0 else (),
                             inc="s_pe" if dc == 7 else None)
                gl_last = sop(lambda s, hc=hc, i=i, e=e: s.activation(
                    hT[:, TW * hc:TW * hc + TW], psA[i][:, 0:TW], AF.Gelu,
                    bias=b1c[e][:, hc:hc + 1], scale=1.0), waits=[tok])
                rA[i].free = gl_last
            if last_tile and e < E - 1:
                rw1.ready = load_w1(e + 1, [("s_pe", P.cnt["s_pe"])])
            if tt == 0:
                t_cpass = c_pass(e)
            if last_e and pend_ln:
                pend_ln.pop(0)()
            for dc in range(8):
                i = dc % 2
                if last_e:
                    t_a = dma("gpsimd", x1_ch[i][:, TW:2 * TW],
                              sacc[dc, :, TW * tt:TW * tt + TW],
                              waits=wl(racc[i].free,
                                       ("s_gp", P.cnt["s_gp"])
                                       if (dc == 0 and tt == 0) else None))
                    t_x = dma("scalar", x1_ch[i][:, 0:TW],
                              sx32[b_of, dc, :, c_of:c_of + TW],
                              waits=wl(rx1c[i].free))
                tok = None
                for hc in range(32):
                    tok = mm(psC[i][:, 0:TW],
                             w2sb[:, D * hc + 128 * dc: D * hc + 128 * dc + 128],
                             hT[:, TW * hc:TW * hc + TW],
                             hc == 0, hc == 31,
                             waits=wl(rC[i].free, gl_last, rw2.ready)
                             if hc == 0 else (),
                             inc="s_pe" if hc == 31 else None)
                if not last_e:
                    t_ct = vop(lambda v, dc=dc, i=i: v.scalar_tensor_tensor(
                        contrib[i][:, :], psC[i][:, 0:TW], ccols[:, dc:dc + 1],
                        mb_sb[:, :], ALU.subtract, ALU.mult),
                        waits=wl(tok, t_mbe, t_cpass, rctr[i].free))
                    rC[i].free = t_ct
                    t_st = dma_store(sacc[dc, :, TW * tt:TW * tt + TW],
                                     contrib[i][:, :], waits=[t_ct],
                                     eng="gpsimd", accum=(e > 0))
                    rctr[i].free = t_st
                else:
                    r2 = big_scr[:, TW * dc:TW * dc + TW]
                    t_1 = vop(lambda v, dc=dc, i=i, r2=r2: v.scalar_tensor_tensor(
                        r2, psC[i][:, 0:TW], ccols[:, dc:dc + 1], mb_sb[:, :],
                        ALU.subtract, ALU.mult),
                        waits=wl(tok, t_mbe, t_cpass,
                                 rbig.free if dc == 0 else None))
                    rC[i].free = t_1
                    t_2 = vop(lambda v, r2=r2, i=i: v.tensor_add(
                        r2, r2, x1_ch[i][:, TW:2 * TW]), waits=[t_1, t_a])
                    racc[i].free = t_2
                    t_3 = vop(lambda v, r2=r2, dc=dc, i=i: v.scalar_tensor_tensor(
                        r2, x1_ch[i][:, 0:TW], csum_c[:, dc:dc + 1], r2,
                        ALU.add, ALU.add), waits=[t_2, t_x])
                    rx1c[i].free = t_3
            rx1t.free = ("s_pe", P.cnt["s_pe"])
            rmb.free = ("s_dve", P.cnt["s_dve"])
            if last_e:
                rt_ready = [("s_dve", P.cnt["s_dve"])] * 8

                def ln_out_tail(tt=tt, rt_ready=rt_ready):
                    r2t = [big_scr[:, TW * dc:TW * dc + TW] for dc in range(8)]
                    ln_done = layer_norm(rt_ready, r2t, TW=TW)
                    for tc4 in range(ntc):
                        i = tc4 % 2
                        for g in range(2):
                            tok = None
                            for j in range(4):
                                dc = 4 * g + j
                                tok = mm(psB32[g][:, 128 * j:128 * j + 128],
                                         big_scr[:, TW * dc + 128 * tc4:
                                                 TW * dc + 128 * tc4 + 128],
                                         ident32[:, :], j == 0, j == 3,
                                         waits=wl(rB[g].free, ln_done[dc])
                                         if j == 0 else [ln_done[dc]],
                                         transpose=True,
                                         inc="s_pe" if j == 3 else None)
                            t_ev = vop(lambda v, g=g, i=i: v.tensor_copy(
                                out_sb[i][:, 512 * g:512 * g + 512],
                                psB32[g][:, 0:512]),
                                waits=wl(tok, rout[i].free if g == 0 else None))
                            rB[g].free = t_ev
                        row0 = TW * tt + 128 * tc4
                        t_o = dma_store(y[row0:row0 + 128, :], out_sb[i][:, :],
                                        waits=[("s_dve", P.cnt["s_dve"])])
                        rout[i].free = t_o
                    rbig.free = ("s_pe", P.cnt["s_pe"])
                pend_ln.append(ln_out_tail)
        if last_e:
            while pend_ln:
                pend_ln.pop(0)()
        if not last_e:
            rw2.ready = load_w2(e + 1, [("s_pe", P.cnt["s_pe"])])

    ffn_res = {k: Res() for k in ("rx1t", "rmb", "rmrow")}
    ffn_res["rctr"] = [Res(), Res()]
    ffn_res["racc"] = [Res(), Res()]
    ffn_res["rx1c"] = [Res(), Res()]
    ffn_res["rout"] = [Res(), Res()]

    n_exp = max(0, min(3, _PHASE - 10))
    es_f1 = ExitStack()
    TW1 = 512
    bufs1 = (sb(es_f1, "x1t_a", [128, 8 * TW1], BF16),
             sb(es_f1, "hT_a", [128, 32 * TW1], BF16),
             sb(es_f1, "mb_a", [128, TW1], F32),
             sb(es_f1, "mrow_a", [1, TW1], F32),
             [sb(es_f1, f"ctr_a{i}", [128, TW1], F32) for i in range(2)],
             None, None)
    for e in range(n_exp):
        ffn_expert(e, TW1, bufs1)
    es_f1.close()

    es_f2 = ExitStack()
    TW2 = 256
    t_f1 = [(s, P.cnt[s]) for s in ("s_pe", "s_dve", "s_act", "s_gp", "s_st",
                                    "s_gld", "s_ld", "s_lda", "s_cast")]
    for r in ffn_res.values():
        if isinstance(r, list):
            for rr in r:
                rr.free = list(t_f1)
        else:
            r.free = list(t_f1)
    for r in (rA + rB + rC + rD):
        r.free = list(t_f1)
    rbig.free = list(t_f1)
    ffn_res["rx1t"].free = list(t_f1)
    bufs2 = (sb(es_f2, "x1t_b", [128, 8 * TW2], BF16),
             sb(es_f2, "hT_b", [128, 32 * TW2], BF16),
             sb(es_f2, "mb_b", [128, TW2], F32),
             sb(es_f2, "mrow_b", [1, TW2], F32),
             None,
             [sb(es_f2, f"x1c_b{i}", [128, 2 * TW2], F32) for i in range(2)],
             [sb(es_f2, f"out_b{i}", [128, D], F32) for i in range(2)])
    if _PHASE >= 14:
        ffn_expert(E - 1, TW2, bufs2)
    es_f2.close()
    es_fw.close()

    P.emit("sync", lambda E_: None,
           waits=[("s_st", P.cnt["s_st"]), ("s_gp", P.cnt["s_gp"])])

    P.replay()

    es.close()
    return nc


def _gi_semantics(gi, nb):
    gi = [int(v) for v in np.asarray(gi).tolist()]
    wrapped = [v + nb if v < 0 else v for v in gi]
    g_gather = [min(max(v, 0), nb - 1) for v in wrapped]
    g_keep = sorted({v for v in wrapped if 0 <= v < nb})
    return g_gather, g_keep


def kernel(L0_attn_x, L0_attn_Wq, L0_attn_bq, L0_attn_Wk, L0_attn_bk,
           L0_attn_Wv, L0_attn_bv, L0_attn_Wo, L0_attn_bo,
           L0_attn_global_indices, L0_ffn_w1, L0_ffn_b1, L0_ffn_w2, L0_ffn_b2,
           L0_ffn_wr, L0_ffn_br):
    x = np.ascontiguousarray(np.asarray(L0_attn_x, dtype=np.float32))
    B = x.shape[0]
    g_gather, g_keep = _gi_semantics(L0_attn_global_indices, NB)
    nc = build_program(g_gather, g_keep)
    common = {
        "wq": np.ascontiguousarray(L0_attn_Wq, dtype=np.float32),
        "wk": np.ascontiguousarray(L0_attn_Wk, dtype=np.float32),
        "wv": np.ascontiguousarray(L0_attn_Wv, dtype=np.float32),
        "wo": np.ascontiguousarray(L0_attn_Wo, dtype=np.float32),
        "bq": np.ascontiguousarray(L0_attn_bq, dtype=np.float32),
        "bk": np.ascontiguousarray(L0_attn_bk, dtype=np.float32),
        "bv": np.ascontiguousarray(L0_attn_bv, dtype=np.float32),
        "bo": np.ascontiguousarray(L0_attn_bo, dtype=np.float32),
        "w1": np.ascontiguousarray(L0_ffn_w1, dtype=np.float32),
        "b1": np.ascontiguousarray(L0_ffn_b1, dtype=np.float32),
        "w2": np.ascontiguousarray(L0_ffn_w2, dtype=np.float32),
        "b2": np.ascontiguousarray(L0_ffn_b2, dtype=np.float32),
        "wr": np.ascontiguousarray(L0_ffn_wr, dtype=np.float32),
        "br": np.ascontiguousarray(L0_ffn_br, dtype=np.float32),
    }
    in_maps = []
    for c in range(N_CORES):
        shard = x[B_CORE * c:B_CORE * (c + 1)].reshape(TOK, D)
        in_maps.append(dict(common, x=np.ascontiguousarray(shard)))
    res = run_bass_kernel_spmd(nc, in_maps, core_ids=list(range(N_CORES)))
    out = np.concatenate([np.asarray(r["y"]).reshape(B_CORE, SEQ, D)
                          for r in res.results], axis=0)
    return out.astype(np.float32)


# version 15 (final)
# speedup vs baseline: 1.1994x; 1.0147x over previous
from contextlib import ExitStack

import numpy as np
import concourse.bass as bass
import concourse.mybir as mybir
from concourse.bass_utils import run_bass_kernel_spmd

F32 = mybir.dt.float32
BF16 = mybir.dt.bfloat16
AF = mybir.ActivationFunctionType
ALU = mybir.AluOpType
AX = mybir.AxisListType

N_CORES = 8
B_CORE = 2
SEQ = 2048
D = 1024
NH = 16
DH = 64
BS = 64
NB = SEQ // BS
E = 4
HID = 4096
TOK = B_CORE * SEQ

ENGINES = ("sync", "gpsimd", "tensor", "scalar", "vector")

_DEBUG_TAPS = None
_PHASE = 99
_PSB_ZERO = True
_ING_TC = 16
_ING_SKIP = ()
_QT_PARTS = 9
_HL_LVL = 9
_NH_LIM = NH
_PV_LVL = 9


class Prog:
    def __init__(self, nc, es):
        self.nc = nc
        self.es = es
        self.blocks = []
        self.cnt = {}
        self.sems = {}
        self.cur = None

    def sem(self, name):
        self.sems[name] = self.es.enter_context(self.nc.semaphore(name))
        self.cnt[name] = 0

    def new_block(self):
        self.cur = {e: [] for e in ENGINES}
        self.blocks.append(self.cur)

    DMA_SEMS = ("s_ld", "s_gld", "s_st", "s_gp", "s_lda", "s_cast")

    def emit(self, eng, fn, waits=(), inc=None, dma=False):
        w = []
        for x in waits:
            if x is None:
                continue
            if isinstance(x, list):
                w.extend(t for t in x if t is not None)
            else:
                w.append(x)
        w = [(t[0], t[1]) if len(t) == 3 else
             (t[0], self.cnt[t[0]] if t[0] in self.DMA_SEMS else t[1])
             for t in w]
        tok = None
        incs = ()
        if inc is not None:
            k = 16 if dma else 1
            self.cnt[inc] += k
            tok = (inc, self.cnt[inc])
            incs = ((inc, k),)
        self.cur[eng].append((fn, tuple(w), incs))
        return tok

    def replay(self):
        for blk_ops in self.blocks:
            with self.nc.Block() as blk:
                for eng in ENGINES:
                    ops = blk_ops[eng]
                    if not ops:
                        continue
                    def make(ops=ops):
                        def body(engobj):
                            seen = {}
                            for fn, waits, incs in ops:
                                for s, v in waits:
                                    if seen.get(s, -1) >= v:
                                        continue
                                    engobj.wait_ge(self.sems[s], v)
                                    seen[s] = v
                                instr = fn(engobj)
                                for s, k in incs:
                                    instr.then_inc(self.sems[s], k)
                        return body
                    getattr(blk, eng)(make())


class Res:
    __slots__ = ("ready", "free")

    def __init__(self):
        self.ready = None
        self.free = None


def wl(*items):
    out = []
    for it in items:
        if it is None:
            continue
        if isinstance(it, list):
            out.extend(x for x in it if x is not None)
        else:
            out.append(it)
    return out


def build_program(g_gather, g_keep):
    NKEEP = max(1, len(g_keep))
    nc = bass.Bass("TRN2", target_bir_lowering=False)

    x = nc.dram_tensor("x", [TOK, D], F32, kind="ExternalInput").ap()
    wq_h = nc.dram_tensor("wq", [D, D], F32, kind="ExternalInput").ap()
    wk_h = nc.dram_tensor("wk", [D, D], F32, kind="ExternalInput").ap()
    wv_h = nc.dram_tensor("wv", [D, D], F32, kind="ExternalInput").ap()
    wo_h = nc.dram_tensor("wo", [D, D], F32, kind="ExternalInput").ap()
    bq_h = nc.dram_tensor("bq", [D], F32, kind="ExternalInput").ap()
    bk_h = nc.dram_tensor("bk", [D], F32, kind="ExternalInput").ap()
    bv_h = nc.dram_tensor("bv", [D], F32, kind="ExternalInput").ap()
    bo_h = nc.dram_tensor("bo", [D], F32, kind="ExternalInput").ap()
    w1_h = nc.dram_tensor("w1", [E, D, HID], F32, kind="ExternalInput").ap()
    b1_h = nc.dram_tensor("b1", [E, HID], F32, kind="ExternalInput").ap()
    w2_h = nc.dram_tensor("w2", [E, HID, D], F32, kind="ExternalInput").ap()
    b2_h = nc.dram_tensor("b2", [E, D], F32, kind="ExternalInput").ap()
    wr_h = nc.dram_tensor("wr", [D, E], F32, kind="ExternalInput").ap()
    br_h = nc.dram_tensor("br", [E], F32, kind="ExternalInput").ap()
    y = nc.dram_tensor("y", [TOK, D], F32, kind="ExternalOutput").ap()

    sx32 = nc.dram_tensor("sx32", [B_CORE, 8, 128, SEQ], F32, kind="Internal").ap()
    sx1bf = nc.dram_tensor("sx1bf", [B_CORE, 8, 128, SEQ], BF16, kind="Internal").ap()
    sacc = nc.dram_tensor("sacc", [8, 128, TOK], F32, kind="Internal").ap()
    swk = nc.dram_tensor("swk", [128, 8 * D], BF16, kind="Internal").ap()
    swv = nc.dram_tensor("swv", [128, 8 * D], BF16, kind="Internal").ap()
    swq = nc.dram_tensor("swq", [128, 8 * D], BF16, kind="Internal").ap()
    swo = nc.dram_tensor("swo", [128, 8 * D], BF16, kind="Internal").ap()
    m_scr = nc.dram_tensor("m_scr", [4, TOK], F32, kind="Internal").ap()
    lg_scr = nc.dram_tensor("lg_scr", [4, TOK], F32, kind="Internal").ap()
    sw1bf = nc.dram_tensor("sw1bf", [E, D, HID], BF16, kind="Internal").ap()
    sxbf = nc.dram_tensor("sxbf", [B_CORE, 8, 128, SEQ], BF16, kind="Internal").ap()
    sw2bf = nc.dram_tensor("sw2bf", [E, HID, D], BF16, kind="Internal").ap()

    taps = {}
    if _DEBUG_TAPS is not None:
        for name, shape, dt in _DEBUG_TAPS:
            taps[name] = nc.dram_tensor("tap_" + name, shape, dt,
                                        kind="ExternalOutput").ap()

    es = ExitStack()
    P = Prog(nc, es)
    for s in ("s_ld", "s_gld", "s_st", "s_pe", "s_act", "s_dve", "s_gp",
              "s_gpc", "s_lda", "s_cast"):
        P.sem(s)

    def sb(es_, name, shape, dtype):
        return es_.enter_context(nc.sbuf_tensor(name, shape, dtype))

    def psum(name, shape, dtype):
        return es.enter_context(nc.psum_tensor(name, shape, dtype))

    ident_bf = sb(es, "ident_bf", [128, 128], BF16)
    ident32 = sb(es, "ident32", [128, 128], F32)
    ones_row_bf = sb(es, "ones_row_bf", [1, 128], BF16)
    ones_row_32 = sb(es, "ones_row_32", [1, 128], F32)
    ones_col_bf = sb(es, "ones_col_bf", [128, 1], BF16)
    b1c = [sb(es, f"b1c{e}", [128, 32], F32) for e in range(E)]
    csum_c = sb(es, "csum_c", [128, 8], F32)
    ccols = sb(es, "ccols", [128, 8], F32)
    strow = [sb(es, f"strow{i}", [1, 512], F32) for i in range(3)]
    rbf = [sb(es, f"rbf{i}", [128, 512], BF16) for i in range(2)]
    sqbf = [sb(es, f"sqbf{i}", [128, 512], BF16) for i in range(2)]
    rrbf = [Res(), Res()]
    rsq = [Res(), Res()]
    big_scr = sb(es, "big_scr", [128, 4096], F32)
    big_bf = big_scr.bitcast(BF16)
    hTc = sb(es, "hTc", [128, 32], BF16)
    zeros64 = sb(es, "zeros64", [64, 64], BF16)

    es_attn = ExitStack()
    bqc = sb(es_attn, "bqc", [128, 8], F32)
    bkc = sb(es_attn, "bkc", [128, 8], F32)
    boc = sb(es_attn, "boc", [128, 8], F32)
    bv_row = sb(es_attn, "bv_row", [1, D], BF16)
    wr_sb = sb(es_attn, "wr_sb", [128, 32], F32)
    br_c = sb(es_attn, "br_c", [4, 1], F32)
    ztmp = sb(es_attn, "ztmp", [128, 24], F32)
    lg_sb = [sb(es_attn, f"lg_sb{i}", [4, 512], F32) for i in range(2)]
    rlg = [Res(), Res()]
    brow = sb(es_attn, "brow", [32, 128], F32)
    rbrow = Res()

    psA = [psum("psA0", [128, 512], F32), psum("psA1", [128, 512], F32)]
    psB = [psum("psB0", [128, 1024], BF16), psum("psB1", [128, 1024], BF16)]
    psB32 = [t.bitcast(F32) for t in psB]
    psC = [psum("psC0", [128, 512], F32), psum("psC1", [128, 512], F32)]
    psD = [psum("psD0", [128, 512], F32), psum("psD1", [128, 512], F32)]
    rA = [Res(), Res()]
    rB = [Res(), Res()]
    rC = [Res(), Res()]
    rD = [Res(), Res()]

    def dma(eng, out_ap, in_ap, waits=(), slow=False, sem=None):
        if sem is None:
            sem = {"gpsimd": "s_gld", "scalar": "s_lda"}.get(eng, "s_ld")
        kw = {"allow_slow_non_contiguous": True} if slow else {}
        w = wl(*waits) if sem == "s_cast" else wl(*waits, (sem, P.cnt[sem]))
        return P.emit(eng, lambda E_, o=out_ap, i=in_ap, kw=kw:
                      E_.dma_start(o, i, **kw),
                      waits=w, inc=sem, dma=True)

    def dma_store(out_ap, in_ap, waits=(), eng="sync", accum=False):
        kw = {"accum_op": ALU.add} if accum else {}
        sem = "s_gp" if eng == "gpsimd" else "s_st"
        return P.emit(eng, lambda E_, o=out_ap, i=in_ap, kw=kw:
                      E_.dma_start(o, i, **kw),
                      waits=wl(*waits, (sem, P.cnt[sem])), inc=sem, dma=True)

    def mm(out_ap, lhsT, rhs, start, stop, waits=(), inc=None, transpose=False):
        def fn(E_, o=out_ap, l=lhsT, r=rhs, st=start, sp=stop, tr=transpose):
            if tr:
                return E_.matmul(o, l, r, start=True, stop=True,
                                 is_transpose=True)
            return E_.matmul(o, l, r, start=st, stop=sp)
        return P.emit("tensor", fn, waits=waits, inc=inc)

    def vop(fn, waits=()):
        return P.emit("vector", fn, waits=waits, inc="s_dve")

    def sop(fn, waits=()):
        return P.emit("scalar", fn, waits=waits, inc="s_act")

    def gop(fn, waits=(), inc="s_gpc"):
        return P.emit("gpsimd", fn, waits=waits, inc=inc)

    def tap(name, src_ap, waits):
        if name in taps:
            dma_store(taps[name], src_ap, waits=waits)

    P.new_block()

    t0 = gop(lambda g: g.memset(ident_bf[:, :], 0.0))
    t0 = gop(lambda g: g.affine_select(out=ident_bf[:, :], in_=ident_bf[:, :],
                                       compare_op=ALU.not_equal, fill=1.0, base=0,
                                       pattern=[[-1, 128]], channel_multiplier=1),
             waits=[t0])
    t1_ = gop(lambda g: g.memset(ident32[:, :], 0.0), waits=[t0])
    t1_ = gop(lambda g: g.affine_select(out=ident32[:, :], in_=ident32[:, :],
                                        compare_op=ALU.not_equal, fill=1.0, base=0,
                                        pattern=[[-1, 128]], channel_multiplier=1),
              waits=[t1_])
    t2_ = gop(lambda g: g.memset(ones_row_bf[:, :], 1.0), waits=[t1_])
    t2_ = gop(lambda g: g.memset(ones_row_32[:, :], 1.0), waits=[t2_])
    t2_ = gop(lambda g: g.memset(zeros64[:, :], 0.0), waits=[t2_])
    t_const = gop(lambda g: g.memset(ones_col_bf[:, :], 1.0), waits=[t2_])

    def load_cols(dst, src_row_ap, nrows, dst_free=None):
        t_ld = dma("sync", brow[0:nrows, :src_row_ap.shape[1]], src_row_ap,
                   waits=wl(rbrow.free))
        t_tr = mm(psD[0][0:src_row_ap.shape[1], 0:nrows],
                  brow[0:nrows, :src_row_ap.shape[1]],
                  ident32[0:nrows, 0:nrows], True, True,
                  waits=wl(t_ld, rD[0].free, t_const), transpose=True, inc="s_pe")
        rbrow.free = t_tr
        t_ev = vop(lambda v, d=dst, n=nrows, m=src_row_ap.shape[1]:
                   v.tensor_copy(d[0:m, 0:n], psD[0][0:m, 0:n]),
                   waits=wl(t_tr, dst_free))
        rD[0].free = t_ev
        return t_ev

    t_bq = load_cols(bqc, bq_h.rearrange("(c p) -> c p", p=128), 8)
    t_bk = load_cols(bkc, bk_h.rearrange("(c p) -> c p", p=128), 8)
    t_bo = load_cols(boc, bo_h.rearrange("(c p) -> c p", p=128), 8)
    t_bv = dma("gpsimd", bv_row[:, :], bv_h.rearrange("(o d) -> o d", o=1))
    t_b1 = [load_cols(b1c[e], b1_h[e, :].rearrange("(c p) -> c p", p=128), 32)
            for e in range(E)]
    b2tmp = sb(es_attn, "b2tmp", [128, 8], F32)
    t_cs = load_cols(csum_c, b2_h[0, :].rearrange("(c p) -> c p", p=128), 8)
    for e in range(1, E):
        t_e = load_cols(b2tmp, b2_h[e, :].rearrange("(c p) -> c p", p=128), 8,
                        dst_free=t_cs)
        t_cs = vop(lambda v: v.tensor_add(csum_c[:, :], csum_c[:, :], b2tmp[:, :]),
                   waits=[t_e, t_cs])
    t_csum_b2 = t_cs
    t_wr = None
    for dc in range(8):
        t_wr = dma("sync", wr_sb[:, 4 * dc:4 * dc + 4],
                   wr_h[128 * dc:128 * dc + 128, :])
    t_br = load_cols(br_c, br_h.rearrange("(o e) -> o e", o=1), 1)

    wkv_sb = sb(es_attn, "wkv", [128, 8 * D], BF16)
    rwkv = Res()
    xn = sb(es_attn, "xn", [128, D], F32)
    rxn = Res()
    wnat = xn.bitcast(BF16)[:, 0:D]
    rwnat = rxn

    def stage_weight(w_hbm, dst_sb, scratch, dst_free=None):
        last = None
        for r in range(8):
            buf = wnat
            t_ld = dma("gpsimd", buf[:, :], w_hbm[128 * r:128 * r + 128, :],
                       waits=wl(rwnat.free, t_const))
            for g in range(2):
                toks = []
                for j in range(4):
                    dc = 4 * g + j
                    toks.append(mm(psB[g][:, 128 * j:128 * j + 128],
                                   buf[:, 128 * dc:128 * dc + 128], ident_bf[:, :],
                                   j == 0, j == 3,
                                   waits=wl(t_ld, rB[g].free) if j == 0 else (),
                                   transpose=True, inc="s_pe" if j == 3 else None))
                def ev(v, g=g, r=r, dst=dst_sb):
                    dst_ap = dst[:, :].rearrange("p (dc c) -> p dc c", c=D)[
                        :, 4 * g:4 * g + 4, 128 * r:128 * r + 128]
                    return v.tensor_copy(dst_ap, psB[g][:, 0:512].rearrange(
                        "p (j c) -> p j c", c=128))
                t_ev = vop(ev, waits=wl(toks[-1],
                                        dst_free if (r == 0 and g == 0) else None))
                rB[g].free = t_ev
                last = t_ev
            rwnat.free = last
        if scratch is not None:
            return dma_store(scratch, dst_sb[:, :], waits=[last])
        return last

    t_sq = stage_weight(wq_h, wkv_sb, swq)
    t_so = stage_weight(wo_h, wkv_sb, swo, dst_free=t_sq)
    rwkv.ready = stage_weight(wk_h, wkv_sb, swk, dst_free=t_so)

    cur = {}
    rqx = Res()
    kT = sb(es_attn, "kT", [128, 8 * SEQ], BF16)
    v_all = sb(es_attn, "v_all", [128, 16 * D], BF16)
    kTg = sb(es_attn, "kTg", [128, 8 * 128], BF16)
    vg = sb(es_attn, "vg", [128, D], BF16)
    qTg = sb(es_attn, "qTg", [128, 512 * NKEEP], BF16)
    qT = sb(es_attn, "qT", [128, 8 * 512], BF16)
    ctxT = sb(es_attn, "ctxT", [128, 8 * 512], BF16)
    NPB = 4
    P_l = [sb(es_attn, f"P_l{i}", [128, 192], BF16) for i in range(NPB)]
    P_c = [sb(es_attn, f"P_c{i}", [128, 128], BF16) for i in range(NPB)]
    PT_sb = [sb(es_attn, f"PT{i}", [128, 384], BF16) for i in range(NPB)]
    rPl = [Res() for _ in range(NPB)]
    rPT = [Res() for _ in range(NPB)]
    xr_sb = [sb(es_attn, f"xr{i}", [128, 512], F32) for i in range(2)]
    rxr = [Res(), Res()]
    x1bf_out = sb(es_attn, "x1o", [128, 4096], BF16)
    rx1o = Res()
    xe32 = [sb(es_attn, "xe32_0", [128, 512], F32)]
    rxe = [Res()]
    ctmp = sb(es_attn, "ctmp", [128, 512], F32)

    rqT = Res()
    r32 = [big_scr[:, 512 * dc:512 * dc + 512] for dc in range(8)]
    P_row = big_bf[:, 0:2048]
    PT_row = big_bf[:, 2048:4096]
    rbig = Res()

    scale = 1.0 / 8.0
    SEQB = SEQ

    def ingest(b):
        for tc in range(_ING_TC):
            t_ld = dma("sync", xn[:, :],
                       x[SEQ * b + 128 * tc: SEQ * b + 128 * tc + 128, :],
                       waits=wl(rxn.free))
            for g in range(2):
                toks = []
                for j in range(4):
                    dc = 4 * g + j
                    toks.append(mm(psB32[g][:, 128 * j:128 * j + 128],
                                   xn[:, 128 * dc:128 * dc + 128],
                                   ident32[:, :], j == 0, j == 3,
                                   waits=wl(t_ld, rB[g].free) if j == 0 else (),
                                   transpose=True, inc="s_pe" if j == 3 else None))
                if "vec" in _ING_SKIP:
                    t_v = toks[-1]
                else:
                    t_v = vop(lambda v, g=g: v.tensor_copy(xe32[0][:, :],
                                                           psB32[g][:, 0:512]),
                              waits=wl(toks[-1], rxe[0].free))
                def v_ev(v, g=g, tc=tc, xTbf=cur["xTbf"]):
                    dst = xTbf[:, :].rearrange("p (dc t) -> p dc t", t=SEQB)[
                        :, 4 * g:4 * g + 4, 128 * tc:128 * tc + 128]
                    return v.tensor_copy(dst, psB32[g][:, 0:512].rearrange(
                        "p (j t) -> p j t", t=128))
                t_s = vop(v_ev, waits=[toks[-1]])
                rB[g].free = [t_v, t_s]
                if "store" in _ING_SKIP:
                    rxe[0].free = t_v
                else:
                    t_st = dma_store(
                        sx32[b, 4 * g:4 * g + 4, :, 128 * tc:128 * tc + 128]
                        .transpose([1, 0, 2]),
                        xe32[0][:, 0:512].rearrange("p (j t) -> p j t", t=128),
                        waits=[t_v])
                    rxe[0].free = t_st
            rxn.free = ("s_pe", P.cnt["s_pe"])
        dma_store(sxbf[b].transpose([1, 0, 2]),
                  cur["xTbf"][:, :].rearrange("p (dc t) -> p dc t", t=SEQB),
                  waits=[("s_dve", P.cnt["s_dve"])])
        return ("s_st", P.cnt["s_st"])

    def kv_proj(b):
        if b > 0:
            rwkv.ready = dma("gpsimd", wkv_sb[:, :], swk, waits=wl(rwkv.free))
        t_xready = [("s_act", P.cnt["s_act"]), ("s_dve", P.cnt["s_dve"])]
        for doutc in range(8):
            for tt in range(4):
                i = (doutc * 4 + tt) % 2
                tok = None
                for dc in range(8):
                    tok = mm(psA[i][:, :],
                             wkv_sb[:, D * dc + 128 * doutc: D * dc + 128 * doutc + 128],
                             cur["xTbf"][:, SEQB * dc + 512 * tt: SEQB * dc + 512 * tt + 512],
                             dc == 0, dc == 7,
                             waits=wl(rA[i].free, rwkv.ready, t_bk, t_xready)
                             if dc == 0 else (),
                             inc="s_pe" if dc == 7 else None)
                t_ev = sop(lambda s, doutc=doutc, tt=tt, i=i: s.activation(
                    kT[:, SEQB * doutc + 512 * tt: SEQB * doutc + 512 * tt + 512],
                    psA[i][:, :], AF.Identity, bias=bkc[:, doutc:doutc + 1], scale=1.0),
                    waits=[tok])
                rA[i].free = t_ev
        rwkv.free = ("s_pe", P.cnt["s_pe"])
        if b == 0:
            rwkv.ready = stage_weight(wv_h, wkv_sb, swv)
        else:
            rwkv.ready = dma("gpsimd", wkv_sb[:, :], swv, waits=wl(rwkv.free))
        for tc in range(16):
            for half in range(2):
                i = (tc * 2 + half) % 2
                tok = None
                for dc in range(8):
                    tok = mm(psC[i][:, :],
                             cur["xTbf"][:, SEQB * dc + 128 * tc: SEQB * dc + 128 * tc + 128],
                             wkv_sb[:, D * dc + 512 * half: D * dc + 512 * half + 512],
                             dc == 0, False,
                             waits=wl(rC[i].free, rwkv.ready, t_bv, t_xready)
                             if dc == 0 else ())
                tok = mm(psC[i][:, :], ones_row_bf[:, :],
                         bv_row[:, 512 * half: 512 * half + 512],
                         False, True, inc="s_pe")
                t_ev = sop(lambda s, tc=tc, half=half, i=i: s.activation(
                    v_all[:, D * tc + 512 * half: D * tc + 512 * half + 512],
                    psC[i][:, :], AF.Copy, bias=0.0, scale=1.0), waits=[tok])
                rC[i].free = t_ev
        rwkv.free = ("s_pe", P.cnt["s_pe"])
        t_kv = ("s_act", P.cnt["s_act"])
        for j, g in enumerate(g_gather):
            for dd in range(8):
                vop(lambda v, j=j, g=g, dd=dd: v.tensor_copy(
                    kTg[:, 128 * dd + 64 * j: 128 * dd + 64 * j + 64],
                    kT[:, SEQB * dd + 64 * g: SEQB * dd + 64 * g + 64]),
                    waits=[t_kv])
            vop(lambda v, j=j, g=g: v.tensor_copy(
                vg[64 * j:64 * j + 64, :],
                v_all[64 * (g % 2):64 * (g % 2) + 64,
                      D * (g // 2):D * (g // 2) + D]), waits=[t_kv])
        return ("s_dve", P.cnt["s_dve"])

    def layer_norm(rt_ready, rtiles, TW=512):
        tok0 = tok1 = None
        for dc in range(8):
            i = dc % 2
            t_c = sop(lambda s, dc=dc, i=i: s.activation(
                rbf[i][:, 0:TW], rtiles[dc], AF.Copy, bias=0.0, scale=1.0),
                waits=wl(rt_ready[dc], rrbf[i].free))
            t_q = sop(lambda s, dc=dc, i=i: s.activation(
                sqbf[i][:, 0:TW], rtiles[dc], AF.Square, bias=0.0, scale=1.0),
                waits=wl(rsq[i].free))
            tok0 = mm(psD[0][0:1, 0:TW], ones_col_bf[:, :], rbf[i][:, 0:TW],
                      dc == 0, dc == 7,
                      waits=wl(t_c, rD[0].free) if dc == 0 else [t_c],
                      inc="s_pe")
            rrbf[i].free = wl(tok0, t_c)
            tok1 = mm(psD[1][0:1, 0:TW], ones_col_bf[:, :], sqbf[i][:, 0:TW],
                      dc == 0, dc == 7,
                      waits=wl(t_q, rD[1].free) if dc == 0 else [t_q],
                      inc="s_pe")
            rsq[i].free = wl(tok1, t_q)
        t1 = vop(lambda v: v.tensor_scalar(strow[0][:, 0:TW], psD[0][0:1, 0:TW],
                                           1.0 / D, None, ALU.mult), waits=[tok0])
        t2 = vop(lambda v: v.tensor_scalar(strow[1][:, 0:TW], psD[1][0:1, 0:TW],
                                           1.0 / D, 1e-5, ALU.mult, ALU.add),
                 waits=[tok1])
        t3 = vop(lambda v: v.tensor_mul(strow[2][:, 0:TW], strow[0][:, 0:TW],
                                        strow[0][:, 0:TW]), waits=[t1])
        t4 = vop(lambda v: v.tensor_sub(strow[1][:, 0:TW], strow[1][:, 0:TW],
                                        strow[2][:, 0:TW]), waits=[t2, t3])
        t5 = sop(lambda s: s.activation(strow[2][:, 0:TW], strow[1][:, 0:TW], AF.Ln,
                                        bias=0.0, scale=1.0), waits=[t4])
        t6 = sop(lambda s: s.activation(strow[2][:, 0:TW], strow[2][:, 0:TW], AF.Exp,
                                        bias=0.0, scale=-0.5), waits=[t5])
        t7 = vop(lambda v: v.tensor_mul(strow[0][:, 0:TW], strow[0][:, 0:TW],
                                        strow[2][:, 0:TW]), waits=[t6])
        tb0 = mm(psD[0][:, 0:TW], ones_row_32[:, :], strow[2][:, 0:TW], True, True,
                 waits=[t6, t1], inc="s_pe")
        tb1 = mm(psD[1][:, 0:TW], ones_row_32[:, :], strow[0][:, 0:TW], True, True,
                 waits=[t7, t2], inc="s_pe")
        done = []
        for dc in range(8):
            ta = vop(lambda v, dc=dc: v.tensor_mul(rtiles[dc], rtiles[dc],
                                                   psD[0][:, 0:TW]), waits=[tb0])
            tb = vop(lambda v, dc=dc: v.tensor_sub(rtiles[dc], rtiles[dc],
                                                   psD[1][:, 0:TW]), waits=[tb1, ta])
            done.append(tb)
        rD[0].free = done[-1]
        rD[1].free = done[-1]
        return done

    def attention_qtile(b, qt, t_ing_st):
        t_prev_st = ("s_st", P.cnt["s_st"])
        t_prev_gp = ("s_gp", P.cnt["s_gp"])
        t_prev_v = rqT.free
        qx = cur["qx"]
        t_qx = cur.pop("t_qx_next", None)
        if t_qx is None:
            t_qx = dma("scalar",
                       qx[:, :].rearrange("p (dc t) -> p dc t", t=512),
                       sxbf[b, :, :, 512 * qt:512 * qt + 512]
                       .transpose([1, 0, 2]),
                       waits=wl(rqx.free, t_ing_st))
        rqx.free = None
        for doutc in range(8):
            i = doutc % 2
            tok = None
            for dc in range(8):
                tok = mm(psA[i][:, :],
                         cur["wqt"][:, D * dc + 128 * doutc: D * dc + 128 * doutc + 128],
                         qx[:, 512 * dc:512 * dc + 512],
                         dc == 0, dc == 7,
                         waits=wl(rA[i].free, t_bq, cur["t_wq"], t_prev_v, t_qx)
                         if dc == 0 else (),
                         inc="s_pe" if dc == 7 else None)
            t_ev = sop(lambda s, doutc=doutc, i=i: s.activation(
                qT[:, 512 * doutc:512 * doutc + 512], psA[i][:, :], AF.Identity,
                bias=bqc[:, doutc:doutc + 1], scale=1.0), waits=[tok])
            rA[i].free = t_ev
        rqx.free = ("s_pe", P.cnt["s_pe"])
        cast_piece(2)
        if qt < 3:
            cur["t_qx_next"] = dma(
                "scalar", qx[:, :].rearrange("p (dc t) -> p dc t", t=512),
                sxbf[b, :, :, 512 * (qt + 1):512 * (qt + 1) + 512]
                .transpose([1, 0, 2]), waits=wl(rqx.free, t_ing_st))
            rqx.free = None
        t_q_done = ("s_act", P.cnt["s_act"])
        if _QT_PARTS < 2:
            return ("s_act", P.cnt["s_act"])

        keep_here = [g for g in g_keep if g // 8 == qt]
        for g in keep_here:
            j = g_keep.index(g)
            for dd in range(8):
                vop(lambda v, j=j, g=g, dd=dd: v.tensor_copy(
                    qTg[:, 512 * j + 64 * dd: 512 * j + 64 * dd + 64],
                    qT[:, 512 * dd + 64 * (g % 8): 512 * dd + 64 * (g % 8) + 64]),
                    waits=[t_q_done])
        t_qg = ("s_dve", P.cnt["s_dve"])
        rqT.free = t_qg
        if qt == 0 and _PSB_ZERO:
            for g2 in range(2):
                mm(psB[g2][64:128, 256:320], zeros64[:, :],
                   ident_bf[0:64, 0:64], True, True,
                   waits=wl(rB[g2].free, t_const), transpose=True)
                t_z0 = mm(psB[g2][0:64, 320:384], zeros64[:, :],
                          ident_bf[0:64, 0:64], True, True, transpose=True,
                          inc="s_pe")
                rB[g2].free = t_z0

        SBk = [psA[0], psA[1], psD[0], psD[1]]
        rS = [rA[0], rA[1], rD[0], rD[1]]
        t_ev_prev_guard = None
        pends = []

        def emit_tail(st):
            ia, ib, h, qc, qc4, pb, t_nl, t_nc = st
            pt2 = it_pt = ia % 2
            w0 = wl(rB[pt2].free, t_nl)
            mm(psB[pt2][:, 0:128], P_l[ib][:, 64:192], ident_bf[:, :],
               True, False, waits=w0, transpose=True)
            mm(psB[pt2][0:64, 256:320], P_l[ib][64:128, 0:64],
               ident_bf[64:128, 64:128], False, False, transpose=True)
            mm(psB[pt2][64:128, 320:384], P_l[ib][0:64, 0:64],
               ident_bf[0:64, 0:64], False, False, transpose=True)
            t_tr = mm(psB[pt2][:, 128:256], P_c[ib][:, :], ident_bf[:, :],
                      False, True, waits=[t_nc], transpose=True, inc="s_pe")
            rPl[ib].free = t_tr
            t_pte = vop(lambda v, pt2=pt2, ib=ib: v.tensor_copy(
                PT_sb[ib][:, :], psB[pt2][:, 0:384]),
                waits=wl(t_tr, rPT[ib].free))
            rB[pt2].free = t_pte
            ocol = 128 * qc4
            blo = (2 * qc - 1) % NB
            bhi = (2 * qc + 2) % NB

            def vsl(bk, hx=h):
                return v_all[64 * (bk % 2):64 * (bk % 2) + 64,
                             D * (bk // 2) + 64 * hx: D * (bk // 2) + 64 * hx + 64]
            firstX = (qc4 == 0)
            wX = wl(t_pte, rC[0].free if (firstX and h % 2 == 0) else None)
            wY = wl(t_pte, rC[1].free if (firstX and h % 2 == 0) else None)
            mm(psC[0][pb:pb + 64, ocol:ocol + 128], vsl(2 * qc),
               PT_sb[ib][0:64, 0:128], firstX, False, waits=wX)
            mm(psC[0][pb:pb + 64, ocol + 64:ocol + 128], vsl(bhi),
               PT_sb[ib][0:64, 256:320], False, False)
            mm(psC[0][pb:pb + 64, ocol:ocol + 128],
               vg[:, 64 * h:64 * h + 64], PT_sb[ib][:, 128:256],
               False, (qc4 == 3), inc="s_pe")
            t_pv = mm(psC[1][pb:pb + 64, ocol:ocol + 128], vsl(2 * qc + 1),
                      PT_sb[ib][64:128, 0:128], firstX, False, waits=wY)
            t_pv = mm(psC[1][pb:pb + 64, ocol:ocol + 64], vsl(blo),
                      PT_sb[ib][64:128, 320:384], False, (qc4 == 3),
                      inc="s_pe")
            rPT[ib].free = t_pv

        for h in range(_NH_LIM):
            hh = h // 2
            pb = 64 * (h % 2)
            for qc4 in range(4):
                qc = 4 * qt + qc4
                it = h * 4 + qc4
                ia = it % 4
                ib = it % 4
                Sb = SBk[ia]
                qap = qT[pb:pb + 64,
                         512 * hh + 128 * qc4: 512 * hh + 128 * qc4 + 128]
                w0 = wl(rS[ia].free, t_q_done, t_qg)
                mm(Sb[:, 64:192], qap,
                   kT[pb:pb + 64, SEQB * hh + 128 * qc: SEQB * hh + 128 * qc + 128],
                   True, False, waits=w0)
                lo = (64 * (2 * qc - 1)) % SEQ
                mm(Sb[0:64, 0:64], qap[:, 0:64],
                   kT[pb:pb + 64, SEQB * hh + lo: SEQB * hh + lo + 64],
                   False, False)
                hi = (64 * (2 * qc + 2)) % SEQ
                mm(Sb[64:128, 0:64], qap[:, 64:128],
                   kT[pb:pb + 64, SEQB * hh + hi: SEQB * hh + hi + 64],
                   False, False)
                t_s = mm(Sb[:, 192:320], qap,
                         kTg[pb:pb + 64, 128 * hh:128 * hh + 128],
                         False, True, inc="s_pe")
                if len(pends) >= 2 and _HL_LVL >= 4:
                    emit_tail(pends.pop(0))
                zc = 4 * ib
                if _HL_LVL < 2:
                    rS[ia].free = vop(lambda v, ia=ia: v.tensor_copy(
                        ztmp[:, 0:2], SBk[ia][:, 0:2]), waits=[t_s])
                    continue
                t_el = sop(lambda s, ia=ia, ib=ib, zc=zc: s.activation(
                    P_l[ib][:, :], SBk[ia][:, 0:192], AF.Exp, bias=0.0,
                    scale=scale, accum_out=ztmp[:, zc:zc + 1]),
                    waits=wl(t_s, rPl[ib].free))
                t_ec = sop(lambda s, ia=ia, ib=ib, zc=zc: s.activation(
                    P_c[ib][:, :], SBk[ia][:, 192:320], AF.Exp, bias=0.0,
                    scale=scale, accum_out=ztmp[:, zc + 1:zc + 2]))
                rS[ia].free = t_ec
                t_z2 = vop(lambda v, zc=zc: v.reciprocal(
                    ztmp[:, zc + 2:zc + 4], ztmp[:, zc:zc + 2]), waits=[t_ec])
                t_nl = vop(lambda v, ib=ib, zc=zc: v.tensor_scalar(
                    P_l[ib][:, :], P_l[ib][:, :], ztmp[:, zc + 2:zc + 3], None,
                    ALU.mult), waits=[t_z2])
                t_nc = vop(lambda v, ib=ib, zc=zc: v.tensor_scalar(
                    P_c[ib][:, :], P_c[ib][:, :], ztmp[:, zc + 3:zc + 4], None,
                    ALU.mult), waits=[t_z2])
                if _HL_LVL < 3:
                    rPl[ib].free = t_nl
                    rS[ia].free = t_nc
                    continue
                if _HL_LVL < 4:
                    rPl[ib].free = t_nl
                    rPT[ib].free = t_nc
                    continue
                pends.append((ia, ib, h, qc, qc4, pb, t_nl, t_nc))
            if h % 2 == 1 and _HL_LVL >= 4:
                while pends:
                    emit_tail(pends.pop(0))
                t_cx = vop(lambda v: v.tensor_copy(ctmp[:, :], psC[0][:, :]),
                           waits=[("s_pe", P.cnt["s_pe"])])
                t_ev = vop(lambda v, hh=hh: v.tensor_add(
                    ctxT[:, 512 * hh:512 * hh + 512], ctmp[:, :],
                    psC[1][:, :]), waits=[t_ev_prev_guard, t_cx])
                rC[0].free = wl(t_cx, t_ev)
                rC[1].free = t_ev
                t_ev_prev_guard = t_ev
        t_ctx = ("s_dve", P.cnt["s_dve"])
        if _QT_PARTS < 3:
            return t_ctx

        for g in keep_here:
            j = g_keep.index(g)
            for hp in range(8):
                for kc in range(4):
                    ia = kc % 2
                    tok = None
                    for hx, pb2 in ((2 * hp, 0), (2 * hp + 1, 64)):
                        tok = mm(
                            psA[ia][pb2:pb2 + 64, :],
                            qTg[pb2:pb2 + 64,
                                512 * j + 64 * hp:512 * j + 64 * hp + 64],
                            kT[pb2:pb2 + 64,
                               SEQB * hp + 512 * kc: SEQB * hp + 512 * kc + 512],
                            True, True,
                            waits=wl(rA[ia].free, t_ctx) if pb2 == 0 else (),
                            inc="s_pe" if pb2 == 64 else None)
                    t_e = sop(lambda s, ia=ia, kc=kc: s.activation(
                        P_row[:, 512 * kc:512 * kc + 512], psA[ia][:, :], AF.Exp,
                        bias=0.0, scale=scale,
                        accum_out=ztmp[:, 16 + kc:17 + kc]),
                        waits=wl(tok, rbig.free if kc == 0 else None))
                    rA[ia].free = t_e
                t_zs = vop(lambda v: v.tensor_reduce(
                    ztmp[:, 20:21], ztmp[:, 16:20], AX.X, ALU.add),
                    waits=[("s_act", P.cnt["s_act"])])
                t_zr = vop(lambda v: v.reciprocal(ztmp[:, 20:21], ztmp[:, 20:21]),
                           waits=[t_zs])
                t_n = vop(lambda v: v.tensor_scalar(
                    P_row[:, :], P_row[:, :], ztmp[:, 20:21], None, ALU.mult),
                    waits=[t_zr])
                ev_toks = []
                for g2 in range(2):
                    w0 = wl(rB[g2].free, t_n)
                    tok = None
                    for i8 in range(8):
                        ci = 8 * g2 + i8
                        tok = mm(psB[g2][:, 128 * i8:128 * i8 + 128],
                                 P_row[:, 128 * ci:128 * ci + 128], ident_bf[:, :],
                                 i8 == 0, i8 == 7,
                                 waits=w0 if i8 == 0 else (), transpose=True,
                                 inc="s_pe" if i8 == 7 else None)
                    t_ev2 = vop(lambda v, g2=g2: v.tensor_copy(
                        PT_row[:, 1024 * g2:1024 * g2 + 1024], psB[g2][:, :]),
                        waits=[tok])
                    rB[g2].free = t_ev2
                    ev_toks.append(t_ev2)
                rbig.free = ev_toks[-1]
                w0 = wl(rD[0].free, ev_toks[0], ev_toks[1])
                tok = None
                for hx, ob in ((2 * hp, 0), (2 * hp + 1, 64)):
                    for tc in range(16):
                        firstpv = (tc == 0)
                        lastpv = (tc == 15)
                        tok = mm(psD[0][ob:ob + 64, 0:64],
                                 v_all[:, D * tc + 64 * hx: D * tc + 64 * hx + 64],
                                 PT_row[:, 128 * tc + ob:128 * tc + ob + 64],
                                 firstpv, lastpv,
                                 waits=w0 if (firstpv and ob == 0) else (),
                                 inc="s_pe" if (lastpv and ob == 64) else None)
                t_ov = vop(lambda v, hp=hp, g=g: v.tensor_copy(
                    ctxT[:, 512 * hp + 64 * (g % 8): 512 * hp + 64 * (g % 8) + 64],
                    psD[0][:, 0:64]), waits=[tok])
                rD[0].free = t_ov
        t_ctx = ("s_dve", P.cnt["s_dve"])
        if _QT_PARTS < 4:
            return t_ctx

        rt_ready = []
        for doutc in range(8):
            i = doutc % 2
            t_x = dma("sync", xr_sb[i][:, :],
                      sx32[b, doutc, :, 512 * qt:512 * qt + 512],
                      waits=wl(rxr[i].free, t_ing_st))
            tok = None
            for dc in range(8):
                tok = mm(psC[i][:, :],
                         cur["wot"][:, D * dc + 128 * doutc: D * dc + 128 * doutc + 128],
                         ctxT[:, 512 * dc:512 * dc + 512],
                         dc == 0, dc == 7,
                         waits=wl(rC[i].free, t_ctx, cur["t_wo"], t_bo)
                         if dc == 0 else (),
                         inc="s_pe" if dc == 7 else None)
            t_r = vop(lambda v, doutc=doutc, i=i: v.scalar_tensor_tensor(
                r32[doutc], psC[i][:, :], boc[:, doutc:doutc + 1], xr_sb[i][:, :],
                ALU.add, ALU.add),
                waits=[tok, t_x, t_prev_st, t_prev_gp, rbig.free])
            rC[i].free = t_r
            rxr[i].free = t_r
            rt_ready.append(t_r)
        rbig.free = None
        if _QT_PARTS < 5:
            return ("s_dve", P.cnt["s_dve"])

        ln_done = layer_norm(rt_ready, r32)

        if _QT_PARTS >= 7:
            il = (4 * b + qt) % 2
            tok = None
            for dc in range(8):
                tok = mm(psD[0][0:4, 0:512], wr_sb[:, 4 * dc:4 * dc + 4],
                         r32[dc], dc == 0, dc == 7,
                         waits=wl(rD[0].free, ln_done[dc], t_wr, t_br)
                         if dc == 0 else [ln_done[dc]],
                         inc="s_pe" if dc == 7 else None)
            t_lg = vop(lambda v, il=il: v.tensor_scalar(
                lg_sb[il][:, :], psD[0][0:4, 0:512], br_c[:, :], None,
                ALU.add), waits=wl(tok, rlg[il].free))
            rD[0].free = t_lg
            rlg[il].free = dma_store(
                lg_scr[0:4, SEQ * b + 512 * qt: SEQ * b + 512 * qt + 512],
                lg_sb[il][:, :], waits=[t_lg])

        t_c = None
        for dc in range(8):
            t_c = sop(lambda s, dc=dc: s.activation(
                x1bf_out[:, 512 * dc:512 * dc + 512], r32[dc], AF.Copy,
                bias=0.0, scale=1.0),
                waits=wl(ln_done[dc], rx1o.free if dc == 0 else None))
        rx1o.free = dma_store(
            sx1bf[b, :, :, 512 * qt:512 * qt + 512].transpose([1, 0, 2]),
            x1bf_out[:, :].rearrange("p (dc t) -> p dc t", t=512),
            waits=[t_c])
        dma_store(sx32[b, :, :, 512 * qt:512 * qt + 512].transpose([1, 0, 2]),
                  big_scr[:, :].rearrange("p (dc t) -> p dc t", t=512),
                  waits=[("s_dve", P.cnt["s_dve"])])
        rbig.free = ("s_st", P.cnt["s_st"])
        return ("s_dve", P.cnt["s_dve"])

    cast_jobs = []
    for e in range(E):
        for h in range(2):
            cast_jobs.append((sw1bf[e, 512 * h:512 * h + 512, :],
                              w1_h[e, 512 * h:512 * h + 512, :]))
            cast_jobs.append((sw2bf[e, 2048 * h:2048 * h + 2048, :],
                              w2_h[e, 2048 * h:2048 * h + 2048, :]))

    def cast_piece(n=1):
        for _ in range(n):
            if cast_jobs:
                dst, srcp = cast_jobs.pop(0)
                dma("gpsimd", dst, srcp, sem="s_cast")

    for b in range(B_CORE):
        if b * 5 >= _PHASE:
            break
        es_x = ExitStack()
        cur["xTbf"] = sb(es_x, f"xTbf{b}", [128, 8 * SEQ], BF16)
        t_ing_st = ingest(b)
        if _PHASE < b * 5 + 2:
            es_x.close()
            break
        kv_proj(b)
        cast_piece(2)
        if b == 0:
            tap("xTbf", cur["xTbf"][:, :], [("s_act", P.cnt["s_act"])])
            tap("kT", kT[:, :], [("s_act", P.cnt["s_act"])])
            tap("v_all", v_all[:, :], [("s_act", P.cnt["s_act"])])
        es_x.close()
        if _PHASE < b * 5 + 3:
            break
        es_qo = ExitStack()
        cur["wqt"] = sb(es_qo, f"wqt{b}", [128, 8 * D], BF16)
        cur["wot"] = sb(es_qo, f"wot{b}", [128, 8 * D], BF16)
        cur["qx"] = sb(es_qo, f"qx{b}", [128, 8 * 512], BF16)
        t_pe_now = ("s_pe", P.cnt["s_pe"])
        t_st_now = ("s_st", P.cnt["s_st"])
        cur["t_wq"] = dma("gpsimd", cur["wqt"][:, :], swq,
                          waits=[t_pe_now, t_st_now, t_sq])
        cur["t_wo"] = dma("gpsimd", cur["wot"][:, :], swo,
                          waits=[t_so, t_st_now])
        rqx.free = None
        for qt in range(4):
            attention_qtile(b, qt, t_ing_st)
            if b == 0 and qt == 0:
                tap("qT", qT[:, :], [("s_act", P.cnt["s_act"])])
                tap("ctxT", ctxT[:, :], [("s_dve", P.cnt["s_dve"])])
            if _PHASE < b * 5 + 4 and qt == 0:
                break
        es_qo.close()

    cast_piece(len(cast_jobs))
    t_attn = {s: ("" + s, P.cnt[s]) for s in
              ("s_dve", "s_act", "s_pe", "s_st", "s_gld", "s_ld", "s_gp",
               "s_lda", "s_cast")}
    t_attn_all = list(t_attn.values())

    es_attn.close()

    P.new_block()
    for r in (rA + rC + rD):
        r.free = list(t_attn_all)
    rbig.free = list(t_attn_all)
    for r in (rrbf + rsq):
        r.free = None

    es_fw = ExitStack()
    w1sb = sb(es_fw, "w1sb", [128, 8 * HID], BF16)
    w2sb = sb(es_fw, "w2sb", [128, 32 * D], BF16)
    rw1 = Res()
    rw2 = Res()

    def load_w1(e, waits):
        tk = None
        w = wl(waits, ("s_cast", P.cnt["s_cast"]))
        for dc in range(8):
            tk = dma("sync", w1sb[:, HID * dc:HID * dc + HID],
                     sw1bf[e, 128 * dc:128 * dc + 128, :],
                     waits=w if dc == 0 else ())
        return tk

    def load_w2(e, waits):
        tk = None
        w = wl(waits, ("s_cast", P.cnt["s_cast"]))
        for r in range(8):
            dst = w2sb[:, :].rearrange("p (g d) -> p g d", d=D)[
                :, 4 * r:4 * r + 4, :]
            srcv = sw2bf[e, 512 * r:512 * r + 512, :].rearrange(
                "(g p) d -> p g d", p=128)
            tk = dma("scalar", dst, srcv, waits=w if r == 0 else ())
        return tk

    def gate_pass():
        lg_all = big_scr[0:4, 0:4096]
        m4q = sb(es_fw, "m4qf", [4, 512], F32)
        gt = [sb(es_fw, f"gtf{i}", [128, 4], F32) for i in range(4)]
        gcol = [sb(es_fw, f"gcf{i}", [128, 1], F32) for i in range(3)]
        rm4 = Res()
        t_lg = dma("scalar", lg_all, lg_scr[0:4, :],
                   waits=wl(t_attn["s_st"], t_attn["s_gp"], t_attn["s_pe"]))
        for g in range(8):
            for i4 in range(4):
                pb = psB32[(4 * g + i4) % 2]
                t_tr = mm(pb[0:128, 0:4],
                          lg_all[0:4, 512 * g + 128 * i4:
                                 512 * g + 128 * i4 + 128],
                          ident32[0:4, 0:4], True, True,
                          waits=wl(t_lg, rB[(4 * g + i4) % 2].free),
                          transpose=True, inc="s_pe")
                t_l = vop(lambda v, pb=pb: v.tensor_copy(
                    gt[0][:, :], pb[0:128, 0:4]), waits=[t_tr])
                t_m1 = vop(lambda v: v.reduce_max(gcol[0][:, :], gt[0][:, :],
                                                  AX.X, negate=True),
                           waits=[t_l])
                t_msk = vop(lambda v: v.tensor_scalar(gt[1][:, :], gt[0][:, :],
                                                      gcol[0][:, :], 0.0,
                                                      ALU.add, ALU.is_ge),
                            waits=[t_m1])
                t_t = vop(lambda v: v.scalar_tensor_tensor(
                    gt[2][:, :], gt[1][:, :], -1e30, gt[0][:, :],
                    ALU.mult, ALU.add), waits=[t_msk])
                t_m2 = vop(lambda v: v.reduce_max(gcol[1][:, :], gt[2][:, :],
                                                  AX.X, negate=True),
                           waits=[t_t])
                t_ind = vop(lambda v: v.tensor_scalar(gt[3][:, :], gt[0][:, :],
                                                      gcol[1][:, :], 0.0,
                                                      ALU.add, ALU.is_ge),
                            waits=[t_m2])
                t_e = sop(lambda s: s.activation(gt[2][:, :], gt[0][:, :],
                                                 AF.Exp, bias=gcol[0][:, :],
                                                 scale=1.0), waits=[t_m2])
                t_en = vop(lambda v: v.tensor_mul(gt[2][:, :], gt[2][:, :],
                                                  gt[3][:, :]),
                           waits=[t_e, t_ind])
                t_zz = vop(lambda v: v.reduce_sum(gcol[2][:, :], gt[2][:, :],
                                                  AX.X), waits=[t_en])
                t_w = vop(lambda v: v.reciprocal(gcol[2][:, :], gcol[2][:, :]),
                          waits=[t_zz])
                t_m = vop(lambda v: v.tensor_scalar(gt[2][:, :], gt[2][:, :],
                                                    gcol[2][:, :], None,
                                                    ALU.mult), waits=[t_w])
                t_tr2 = mm(pb[0:4, 0:128], gt[2][:, :], ident32[:, :], True,
                           True, waits=[t_m], transpose=True, inc="s_pe")
                t_cp = vop(lambda v, pb=pb, i4=i4: v.tensor_copy(
                    m4q[0:4, 128 * i4:128 * i4 + 128], pb[0:4, 0:128]),
                    waits=wl(t_tr2, rm4.free if i4 == 0 else None))
                rB[(4 * g + i4) % 2].free = t_cp
            rm4.free = dma_store(m_scr[0:4, 512 * g:512 * g + 512],
                                 m4q[0:4, :], waits=[t_cp])
        return [("s_st", P.cnt["s_st"]), ("s_dve", P.cnt["s_dve"]),
                ("s_pe", P.cnt["s_pe"])]

    if _PHASE >= 11:
        rw1.ready = load_w1(0, t_attn_all)
        rw2.ready = load_w2(0, t_attn_all)
        t_gates = gate_pass()
        rbig.free = wl(rbig.free, t_gates)

    def c_pass(e):
        t_g = sop(lambda s, e=e: s.activation(hTc[:, :], b1c[e][:, :], AF.Gelu,
                                              bias=0.0, scale=1.0),
                  waits=wl(t_b1[e], ("s_pe", P.cnt["s_pe"])))
        crow = strow[0]
        t_cpass = None
        for half in range(2):
            tok = None
            for hc in range(32):
                tok = mm(psD[1][0:1, 0:512], hTc[:, hc:hc + 1],
                         w2sb[:, D * hc + 512 * half: D * hc + 512 * half + 512],
                         hc == 0, hc == 31,
                         waits=wl(rD[1].free, t_g, rw2.ready) if hc == 0 else (),
                         inc="s_pe" if hc == 31 else None)
            t_cr = vop(lambda v: v.tensor_copy(crow[:, :], psD[1][0:1, 0:512]),
                       waits=[tok])
            rD[1].free = t_cr
            for i4 in range(4):
                dc = 4 * half + i4
                t_tr = mm(psD[1][0:128, 0:1], crow[0:1, 128 * i4:128 * i4 + 128],
                          ident32[0:1, 0:1], True, True,
                          waits=wl(t_cr, rD[1].free), transpose=True, inc="s_pe")
                t_cc = vop(lambda v, dc=dc: v.tensor_copy(
                    ccols[:, dc:dc + 1], psD[1][0:128, 0:1]), waits=[t_tr])
                rD[1].free = t_cc
                t_cpass = vop(lambda v, dc=dc: v.tensor_add(
                    csum_c[:, dc:dc + 1], csum_c[:, dc:dc + 1],
                    ccols[:, dc:dc + 1]), waits=wl(t_cc, t_csum_b2))
        return t_cpass

    def ffn_expert(e, TW, bufs):
        x1t, hT, mb_sb, mrow_t, contrib, x1_ch, out_sb = bufs
        rx1t, rmb, rmrow = ffn_res["rx1t"], ffn_res["rmb"], ffn_res["rmrow"]
        rctr, racc, rx1c = ffn_res["rctr"], ffn_res["racc"], ffn_res["rx1c"]
        rout = ffn_res["rout"]
        t_cpass = None
        last_e = (e == E - 1)
        n_tt = TOK // TW
        ntc = TW // 128
        pend_ln = []
        for tt in range(n_tt):
            b_of = (TW * tt) // SEQ
            c_of = (TW * tt) % SEQ
            t_x1 = dma("sync", x1t[:, :].rearrange("p (dc t) -> p dc t", t=TW),
                       sx1bf[b_of, :, :, c_of:c_of + TW].transpose([1, 0, 2]),
                       waits=wl(rx1t.free, t_attn["s_st"]))
            t_mr = dma("sync", mrow_t[:, :],
                       m_scr[e:e + 1, TW * tt:TW * tt + TW],
                       waits=wl(rmrow.free, t_attn["s_st"]))
            t_mb = mm(psD[0][:, 0:TW], ones_row_32[:, :], mrow_t[:, :],
                      True, True, waits=wl(rD[0].free, t_mr), inc="s_pe")
            rmrow.free = t_mb
            t_mbe = vop(lambda v: v.tensor_copy(mb_sb[:, :], psD[0][:, 0:TW]),
                        waits=wl(t_mb, rmb.free))
            rD[0].free = t_mbe
            last_tile = (tt == n_tt - 1)
            gl_last = None
            for hc in range(32):
                i = hc % 2
                tok = None
                for dc in range(8):
                    tok = mm(psA[i][:, 0:TW],
                             w1sb[:, HID * dc + 128 * hc: HID * dc + 128 * hc + 128],
                             x1t[:, TW * dc:TW * dc + TW],
                             dc == 0, dc == 7,
                             waits=wl(rA[i].free, t_x1, rw1.ready)
                             if dc == 0 else (),
                             inc="s_pe" if dc == 7 else None)
                gl_last = sop(lambda s, hc=hc, i=i, e=e: s.activation(
                    hT[:, TW * hc:TW * hc + TW], psA[i][:, 0:TW], AF.Gelu,
                    bias=b1c[e][:, hc:hc + 1], scale=1.0), waits=[tok])
                rA[i].free = gl_last
            if last_tile and e < E - 1:
                rw1.ready = load_w1(e + 1, [("s_pe", P.cnt["s_pe"])])
            if tt == 0:
                t_cpass = c_pass(e)
            if last_e and pend_ln:
                pend_ln.pop(0)()
            for dc in range(8):
                i = dc % 2
                if last_e:
                    t_a = dma("gpsimd", x1_ch[i][:, TW:2 * TW],
                              sacc[dc, :, TW * tt:TW * tt + TW],
                              waits=wl(racc[i].free,
                                       ("s_gp", P.cnt["s_gp"])
                                       if (dc == 0 and tt == 0) else None))
                    t_x = dma("scalar", x1_ch[i][:, 0:TW],
                              sx32[b_of, dc, :, c_of:c_of + TW],
                              waits=wl(rx1c[i].free))
                tok = None
                for hc in range(32):
                    tok = mm(psC[i][:, 0:TW],
                             w2sb[:, D * hc + 128 * dc: D * hc + 128 * dc + 128],
                             hT[:, TW * hc:TW * hc + TW],
                             hc == 0, hc == 31,
                             waits=wl(rC[i].free, gl_last, rw2.ready)
                             if hc == 0 else (),
                             inc="s_pe" if hc == 31 else None)
                if not last_e:
                    t_ct = vop(lambda v, dc=dc, i=i: v.scalar_tensor_tensor(
                        contrib[i][:, :], psC[i][:, 0:TW], ccols[:, dc:dc + 1],
                        mb_sb[:, :], ALU.subtract, ALU.mult),
                        waits=wl(tok, t_mbe, t_cpass, rctr[i].free))
                    rC[i].free = t_ct
                    t_st = dma_store(sacc[dc, :, TW * tt:TW * tt + TW],
                                     contrib[i][:, :], waits=[t_ct],
                                     eng="gpsimd", accum=(e > 0))
                    rctr[i].free = t_st
                else:
                    r2 = big_scr[:, TW * dc:TW * dc + TW]
                    t_1 = vop(lambda v, dc=dc, i=i, r2=r2: v.scalar_tensor_tensor(
                        r2, psC[i][:, 0:TW], ccols[:, dc:dc + 1], mb_sb[:, :],
                        ALU.subtract, ALU.mult),
                        waits=wl(tok, t_mbe, t_cpass,
                                 rbig.free if dc == 0 else None))
                    rC[i].free = t_1
                    t_2 = vop(lambda v, r2=r2, i=i: v.tensor_add(
                        r2, r2, x1_ch[i][:, TW:2 * TW]), waits=[t_1, t_a])
                    racc[i].free = t_2
                    t_3 = vop(lambda v, r2=r2, dc=dc, i=i: v.scalar_tensor_tensor(
                        r2, x1_ch[i][:, 0:TW], csum_c[:, dc:dc + 1], r2,
                        ALU.add, ALU.add), waits=[t_2, t_x])
                    rx1c[i].free = t_3
            rx1t.free = ("s_pe", P.cnt["s_pe"])
            rmb.free = ("s_dve", P.cnt["s_dve"])
            if last_e:
                rt_ready = [("s_dve", P.cnt["s_dve"])] * 8

                def ln_out_tail(tt=tt, rt_ready=rt_ready):
                    r2t = [big_scr[:, TW * dc:TW * dc + TW] for dc in range(8)]
                    ln_done = layer_norm(rt_ready, r2t, TW=TW)
                    for tc4 in range(ntc):
                        i = tc4 % 2
                        for g in range(2):
                            tok = None
                            for j in range(4):
                                dc = 4 * g + j
                                tok = mm(psB32[g][:, 128 * j:128 * j + 128],
                                         big_scr[:, TW * dc + 128 * tc4:
                                                 TW * dc + 128 * tc4 + 128],
                                         ident32[:, :], j == 0, j == 3,
                                         waits=wl(rB[g].free, ln_done[dc])
                                         if j == 0 else [ln_done[dc]],
                                         transpose=True,
                                         inc="s_pe" if j == 3 else None)
                            t_ev = vop(lambda v, g=g, i=i: v.tensor_copy(
                                out_sb[i][:, 512 * g:512 * g + 512],
                                psB32[g][:, 0:512]),
                                waits=wl(tok, rout[i].free if g == 0 else None))
                            rB[g].free = t_ev
                        row0 = TW * tt + 128 * tc4
                        t_o = dma_store(y[row0:row0 + 128, :], out_sb[i][:, :],
                                        waits=[("s_dve", P.cnt["s_dve"])])
                        rout[i].free = t_o
                    rbig.free = ("s_pe", P.cnt["s_pe"])
                pend_ln.append(ln_out_tail)
        if last_e:
            while pend_ln:
                pend_ln.pop(0)()
        if not last_e:
            rw2.ready = load_w2(e + 1, [("s_pe", P.cnt["s_pe"])])

    ffn_res = {k: Res() for k in ("rx1t", "rmb", "rmrow")}
    ffn_res["rctr"] = [Res(), Res()]
    ffn_res["racc"] = [Res(), Res()]
    ffn_res["rx1c"] = [Res(), Res()]
    ffn_res["rout"] = [Res(), Res()]

    n_exp = max(0, min(3, _PHASE - 10))
    es_f1 = ExitStack()
    TW1 = 512
    bufs1 = (sb(es_f1, "x1t_a", [128, 8 * TW1], BF16),
             sb(es_f1, "hT_a", [128, 32 * TW1], BF16),
             sb(es_f1, "mb_a", [128, TW1], F32),
             sb(es_f1, "mrow_a", [1, TW1], F32),
             [sb(es_f1, f"ctr_a{i}", [128, TW1], F32) for i in range(2)],
             None, None)
    for e in range(n_exp):
        ffn_expert(e, TW1, bufs1)
    es_f1.close()

    es_f2 = ExitStack()
    TW2 = 256
    t_f1 = [(s, P.cnt[s]) for s in ("s_pe", "s_dve", "s_act", "s_gp", "s_st",
                                    "s_gld", "s_ld", "s_lda", "s_cast")]
    for r in ffn_res.values():
        if isinstance(r, list):
            for rr in r:
                rr.free = list(t_f1)
        else:
            r.free = list(t_f1)
    for r in (rA + rB + rC + rD):
        r.free = list(t_f1)
    rbig.free = list(t_f1)
    ffn_res["rx1t"].free = list(t_f1)
    bufs2 = (sb(es_f2, "x1t_b", [128, 8 * TW2], BF16),
             sb(es_f2, "hT_b", [128, 32 * TW2], BF16),
             sb(es_f2, "mb_b", [128, TW2], F32),
             sb(es_f2, "mrow_b", [1, TW2], F32),
             None,
             [sb(es_f2, f"x1c_b{i}", [128, 2 * TW2], F32) for i in range(2)],
             [sb(es_f2, f"out_b{i}", [128, D], F32) for i in range(2)])
    if _PHASE >= 14:
        ffn_expert(E - 1, TW2, bufs2)
    es_f2.close()
    es_fw.close()

    P.emit("sync", lambda E_: None,
           waits=[("s_st", P.cnt["s_st"]), ("s_gp", P.cnt["s_gp"])])

    P.replay()

    es.close()
    return nc


def _gi_semantics(gi, nb):
    gi = [int(v) for v in np.asarray(gi).tolist()]
    wrapped = [v + nb if v < 0 else v for v in gi]
    g_gather = [min(max(v, 0), nb - 1) for v in wrapped]
    g_keep = sorted({v for v in wrapped if 0 <= v < nb})
    return g_gather, g_keep


def kernel(L0_attn_x, L0_attn_Wq, L0_attn_bq, L0_attn_Wk, L0_attn_bk,
           L0_attn_Wv, L0_attn_bv, L0_attn_Wo, L0_attn_bo,
           L0_attn_global_indices, L0_ffn_w1, L0_ffn_b1, L0_ffn_w2, L0_ffn_b2,
           L0_ffn_wr, L0_ffn_br):
    x = np.ascontiguousarray(np.asarray(L0_attn_x, dtype=np.float32))
    B = x.shape[0]
    g_gather, g_keep = _gi_semantics(L0_attn_global_indices, NB)
    nc = build_program(g_gather, g_keep)
    common = {
        "wq": np.ascontiguousarray(L0_attn_Wq, dtype=np.float32),
        "wk": np.ascontiguousarray(L0_attn_Wk, dtype=np.float32),
        "wv": np.ascontiguousarray(L0_attn_Wv, dtype=np.float32),
        "wo": np.ascontiguousarray(L0_attn_Wo, dtype=np.float32),
        "bq": np.ascontiguousarray(L0_attn_bq, dtype=np.float32),
        "bk": np.ascontiguousarray(L0_attn_bk, dtype=np.float32),
        "bv": np.ascontiguousarray(L0_attn_bv, dtype=np.float32),
        "bo": np.ascontiguousarray(L0_attn_bo, dtype=np.float32),
        "w1": np.ascontiguousarray(L0_ffn_w1, dtype=np.float32),
        "b1": np.ascontiguousarray(L0_ffn_b1, dtype=np.float32),
        "w2": np.ascontiguousarray(L0_ffn_w2, dtype=np.float32),
        "b2": np.ascontiguousarray(L0_ffn_b2, dtype=np.float32),
        "wr": np.ascontiguousarray(L0_ffn_wr, dtype=np.float32),
        "br": np.ascontiguousarray(L0_ffn_br, dtype=np.float32),
    }
    in_maps = []
    for c in range(N_CORES):
        shard = x[B_CORE * c:B_CORE * (c + 1)].reshape(TOK, D)
        in_maps.append(dict(common, x=np.ascontiguousarray(shard)))
    res = run_bass_kernel_spmd(nc, in_maps, core_ids=list(range(N_CORES)))
    out = np.concatenate([np.asarray(r["y"]).reshape(B_CORE, SEQ, D)
                          for r in res.results], axis=0)
    return out.astype(np.float32)
